# version 11
# baseline (speedup 1.0000x reference)
import math
import numpy as np
import concourse.bass as bass
import concourse.mybir as mybir
from concourse.bass_utils import run_bass_kernel_spmd

F32 = mybir.dt.float32
BF16 = mybir.dt.bfloat16
U32 = mybir.dt.uint32
AF = mybir.ActivationFunctionType
ALU = mybir.AluOpType
AXX = mybir.AxisListType.X
AX = mybir.AxisListType

P = 128
DM = 1024
HID = 4096
NE = 8
NST = 64
LSEQ = 8192
NB = 2
TOK = 2048
NCH = 16
PRETOK = 6144
PREG = 12
GSUB = 4
TH = 1024
NTH = 2
TC = 512
EPS = 1e-5
ALPHA = 2.0


def _host_consts():
    c = {}
    tl = np.arange(P, dtype=np.float64)
    f = np.arange(P, dtype=np.float64)
    ang = 2.0 * np.pi * np.outer(tl, f) / 256.0
    sgn = ((-1.0) ** f)[None, :]
    Fr = sgn * np.cos(ang)
    Fi = -sgn * np.sin(ang)
    Fi[:, 0] = (-1.0) ** tl
    c["Fr"], c["Fi"] = Fr, Fi
    KFr = np.cos(ang)
    KFi = -np.sin(ang)
    KFi[:, 0] = (-1.0) ** tl
    c["KFr"], c["KFi"] = KFr, KFi
    w = np.full(P, 2.0 / 256.0)
    w[0] = 1.0 / 256.0
    ICr = (w * ((-1.0) ** f))[:, None] * np.cos(ang.T)
    ICi = -(w * ((-1.0) ** f))[:, None] * np.sin(ang.T)
    ICi[0, :] = (1.0 / 256.0) * ((-1.0) ** tl)
    c["ICr"], c["ICi"] = ICr, ICi
    c["ident"] = np.eye(P)
    c["rev"] = np.eye(P)[::-1].copy()
    c["ones_row"] = np.ones((1, P))
    return {k: np.ascontiguousarray(v, dtype=np.float32) for k, v in c.items()}


class _Sem:
    def __init__(self, h):
        self.h = h
        self.v = 0

    def inc(self, n=1):
        self.v += n
        return self.v


_DTSZ = {"dt.float32": 4, "dt.int32": 4, "dt.uint32": 4, "dt.bfloat16": 2,
         "dt.float16": 2, "dt.uint8": 1, "dt.int8": 1}


def _ap_rect(x):
    if type(x).__name__ != "AP":
        return None
    try:
        sp = str(x.space)
        if ("SBUF" not in sp) and ("PSUM" not in sp):
            return None
        h = x.tensor
        hshape = list(h.shape)
        wb = int(hshape[-1]) * _DTSZ.get(str(h.dtype), 4)
        asz = _DTSZ.get(str(x.dtype), 4)
        wc = max(1, wb // asz)
        off = int(x.offset)
        p0 = off // wc
        f0 = off % wc
        shp = list(x.shape)
        if len(shp) == 2:
            pn, fn_ = shp
        elif len(shp) == 1:
            pn, fn_ = 1, shp[0]
        else:
            pn = shp[0]
            fn_ = 1
            for s_ in shp[1:]:
                fn_ *= s_
        return (x.name, p0, p0 + pn, f0 * asz, (f0 + fn_) * asz)
    except Exception:
        try:
            return (x.name, 0, 1 << 20, 0, 1 << 40)
        except Exception:
            return None


def _olap(a, b):
    return (a[0] == b[0] and a[1] < b[2] and b[1] < a[2]
            and a[3] < b[4] and b[3] < a[4])


class _RecDummy:
    def __getattr__(self, n):
        return lambda *a, **k: self


class _Rec:
    def __init__(self):
        self.calls = 0
        self.reads = []
        self.writes = []
        self.names = []
        self._dummy = _RecDummy()

    def __getattr__(self, name):
        def f(*a, **k):
            self.calls += 1
            self.names.append(name)
            vals = list(a) + [v for kk, v in k.items() if kk != "out"]
            first = k["out"] if "out" in k else None
            for v in vals:
                r = _ap_rect(v)
                if r is None:
                    continue
                if first is None:
                    first = v
                    self.writes.append(r)
                else:
                    self.reads.append(r)
            if "out" in k:
                r = _ap_rect(k["out"])
                if r is not None:
                    self.writes.append(r)
            return self._dummy
        return f


DEBUG = 0
STOP = ""


def build(nc):
    con = _host_consts()

    xo = nc.dram_tensor("xo", [TOK, DM], F32, kind="ExternalInput").ap()
    xp = nc.dram_tensor("xp", [PRETOK, DM], F32, kind="ExternalInput").ap()
    pmask = nc.dram_tensor("pmask", [NST, PREG], F32, kind="ExternalInput").ap()
    lr = nc.dram_tensor("lr", [NST, 1], F32, kind="ExternalInput").ap()
    li = nc.dram_tensor("li", [NST, 1], F32, kind="ExternalInput").ap()
    cr = nc.dram_tensor("cr", [DM, NST], F32, kind="ExternalInput").ap()
    ci = nc.dram_tensor("ci", [DM, NST], F32, kind="ExternalInput").ap()
    dvec = nc.dram_tensor("dvec", [1, DM], F32, kind="ExternalInput").ap()
    gamma = nc.dram_tensor("gamma", [1, DM], F32, kind="ExternalInput").ap()
    beta = nc.dram_tensor("beta", [1, DM], F32, kind="ExternalInput").ap()
    wrout = nc.dram_tensor("wrout", [DM, NE], F32, kind="ExternalInput").ap()
    wup = nc.dram_tensor("wup", [NE, DM, HID], F32, kind="ExternalInput").ap()
    wgate = nc.dram_tensor("wgate", [NE, DM, HID], F32, kind="ExternalInput").ap()
    wdown = nc.dram_tensor("wdown", [NE, HID, DM], F32, kind="ExternalInput").ap()
    out = nc.dram_tensor("out", [TOK, DM], F32, kind="ExternalOutput").ap()
    xres = nc.dram_tensor("xres", [TOK, DM], F32,
                          kind="ExternalOutput" if DEBUG else "Internal").ap()
    coefd = nc.dram_tensor("coefd", [NE, TOK], F32, kind="Internal").ap()
    dbg = (nc.dram_tensor("dbg", [TOK, DM], F32, kind="ExternalOutput").ap()
           if DEBUG else None)
    dbg_r = dbg.rearrange("(t p) d -> t p d", p=P) if DEBUG else None

    cst = {k: nc.inline_tensor(v, name="c_" + k).ap() for k, v in con.items()}

    def S(name, shape, dtype):
        return nc.alloc_sbuf_tensor(name, shape, dtype)

    _a_guards = []

    def SA(name, shape, dtype):
        g = nc.sbuf_tensor(name, shape, dtype)
        h = g.__enter__()
        _a_guards.append(g)
        return h

    xn4T = S("xn4T", [P, 8 * TOK], BF16)
    coef_rows = S("coef_rows", [NE, TOK], F32)
    identb = S("identb", [P, P], BF16)
    identf = S("identf", [P, P], F32)
    ones_colf = S("ones_colf", [1, P], F32)
    stat = S("stat", [P, 16], F32)
    cols = S("cols", [P, 16], F32)
    wr_sb = S("wr_sb", [P, NE * 8], BF16)
    xe_t = S("xe_t", [P, DM], F32)
    lgits = S("lgits", [P, NE], F32)
    e8 = S("e8", [P, NE], F32)
    i8u = S("i8u", [P, NE], U32)
    v8 = S("v8", [P, NE], F32)
    ctile = S("ctile", [P, NE], F32)
    ctileb = S("ctileb", [P, NE], BF16)
    rt8 = S("rt8", [P, NE], F32)
    junkc = S("junkc", [P, 4], F32)
    carry_re = SA("carry_re", [NST, DM], F32)
    carry_im = SA("carry_im", [NST, DM], F32)
    cs_re = SA("cs_re", [NST, DM], F32)
    cs_im = SA("cs_im", [NST, DM], F32)
    csreb = SA("csreb", [NST, DM], BF16)
    csimb = SA("csimb", [NST, DM], BF16)
    kfr = SA("kfr", [P, DM], F32)
    kfi = SA("kfi", [P, DM], F32)
    kfr2 = SA("kfr2", [P, DM], F32)
    gammab = SA("gammab", [P, DM], F32)
    betab = SA("betab", [P, DM], F32)
    frb = SA("frb", [P, P], BF16)
    fib = SA("fib", [P, P], BF16)
    icrb = SA("icrb", [P, P], BF16)
    icib = SA("icib", [P, P], BF16)
    kfrb_c = SA("kfrb_c", [P, P], BF16)
    kfib_c = SA("kfib_c", [P, P], BF16)
    revb = SA("revb", [P, P], BF16)
    asre = SA("asre", [P, NST * 5], BF16)
    asim = SA("asim", [P, NST * 5], BF16)
    ppre = SA("ppre", [NST, P], BF16)
    ppim = SA("ppim", [NST, P], BF16)
    pkre = SA("pkre", [NST, P], BF16)
    pkim = SA("pkim", [NST, P], BF16)
    kb = SA("kb", [P, DM], BF16)
    aq_cols = SA("aq_cols", [NST, 4], F32)
    pmask_sb = SA("pmask_sb", [NST, PREG], F32)
    ones_mask = SA("ones_mask", [NST, 1], F32)
    pw_re = SA("pw_re", [NST, P], F32)
    pw_im = SA("pw_im", [NST, P], F32)
    sc64 = SA("sc64", [NST, 16], F32)
    sci = SA("sci", [NST, 1], mybir.dt.int32)
    powtab = SA("powtab", [NST, 8], F32)
    powrow = SA("powrow", [8, NST], F32)
    x_t = SA("x_t", [P, 4 * DM], F32)
    xn1_t = SA("xn1_t", [P, 2 * DM], F32)
    xn2a_t = SA("xn2a_t", [P, 2 * DM], F32)
    gn_t = SA("gn_t", [P, 2 * DM], F32)
    sq_t = SA("sq_t", [P, 2 * DM], F32)
    xn2b_t = SA("xn2b_t", [P, 4 * DM], BF16)
    m_t = SA("m_t", [P, 4 * DM], F32)
    yr_t = SA("yr_t", [P, DM], BF16)
    yi_t = SA("yi_t", [P, DM], BF16)
    w_re = SA("w_re", [NST, DM], F32)
    w_im = SA("w_im", [NST, DM], F32)
    wreb = SA("wreb", [NST, DM], BF16)
    wimb = SA("wimb", [NST, DM], BF16)
    wtmp = SA("wtmp", [NST, 2 * DM], F32)
    ctmp = SA("ctmp", [NST, 2 * DM], F32)
    xnew_t = SA("xnew_t", [P, 2 * DM], F32)
    xn4_t = SA("xn4_t", [P, DM], BF16)

    psA = nc.alloc_psum_tensor("psA", [P, DM], F32)
    psB = nc.alloc_psum_tensor("psB", [P, DM], F32)
    psC = nc.alloc_psum_tensor("psC", [P, DM], F32)
    psD = nc.alloc_psum_tensor("psD", [P, DM], F32)

    sems = {}

    def mksem(name):
        s = _Sem(nc.alloc_semaphore(name))
        sems[name] = s
        return s

    s_x = mksem("s_x")
    s_st = mksem("s_st")
    s_rs = mksem("s_rs")
    s_rc = mksem("s_rc")
    s_ap = mksem("s_ap")
    s_ln = mksem("s_ln")
    s_fft = mksem("s_fft")
    s_pw = mksem("s_pw")
    s_yy = mksem("s_yy")
    s_wl = mksem("s_wl")
    s_wg = mksem("s_wg")
    s_ys = mksem("s_ys")
    s_cv = mksem("s_cv")
    s_xc = mksem("s_xc")
    s_cu = mksem("s_cu")
    s_l3 = mksem("s_l3")
    s_l4 = mksem("s_l4")
    s_tp = mksem("s_tp")
    s_t4 = mksem("s_t4")
    s_rr = mksem("s_rr")
    s_rv = mksem("s_rv")
    s_rt = mksem("s_rt")
    s_rc2 = mksem("s_rc2")
    s_cfd = mksem("s_cfd")
    s_wdma = mksem("s_wdma")
    s_cb = mksem("s_cb")
    s_xs = mksem("s_xs")
    s_ug = mksem("s_ug")
    s_sg = mksem("s_sg")
    s_hm = mksem("s_hm")
    s_dn = mksem("s_dn")
    s_ac = mksem("s_ac")
    s_am = mksem("s_am")
    s_c5 = mksem("s_c5")
    s_e5 = mksem("s_e5")
    s_xr = mksem("s_xr")
    s_res = mksem("s_res")
    s_out = mksem("s_out")

    prog = {k: [] for k in ("sync", "tensor", "vector", "scalar", "gpsimd")}

    def em(eng, fn):
        prog[eng].append(fn)

    def wait(eng, sem, v):
        em(eng, lambda e, s=sem, v=v: e.wait_ge(s.h, v))

    _vhist = []
    _haz_sites = {}
    _n_spacers = [0]

    def _v_hazard(reads):
        between = 0
        for writes, real in reversed(_vhist):
            if between >= 1:
                break
            if any(_olap(r, w) for r in reads for w in writes):
                return True
            if real:
                between += 1
        return False

    def _v_push(writes, real):
        _vhist.append((writes, real))
        if len(_vhist) > 6:
            _vhist.pop(0)

    def op(eng, fn, sem=None, n=1):
        if eng == "vector":
            rec = _Rec()
            try:
                fn(rec)
            except Exception as ex:
                print("RAW-checker: recorder failed:", ex)
                rec.reads = rec.writes = [(nm, 0, 1 << 20, 0, 1 << 40)
                                          for nm in ("x_t", "cols", "ctmp")]
            if _v_hazard(rec.reads):
                import traceback
                ln = traceback.extract_stack()[-2].lineno
                _haz_sites[ln] = _haz_sites.get(ln, 0) + 1
                _n_spacers[0] += 1
                em("vector", lambda e: e.tensor_copy(junkc[:, 0:1], junkc[:, 2:3]))
                _v_push([("junkc", 0, P, 0, 4)], True)
            _v_push(rec.writes,
                    bool(rec.names) and any(nm != "memset" for nm in rec.names))
        if sem is None:
            em(eng, lambda e, f=fn: f(e))
        else:
            sem.inc(n)
            em(eng, lambda e, f=fn, s=sem, n=n: f(e).then_inc(s.h, n))

    build.haz_sites = _haz_sites
    build.n_spacers = _n_spacers

    op("vector", lambda e: e.memset(junkc[:], 0.0))

    for dst, src in [
        (sc64[:, 0:1], lr), (sc64[:, 1:2], li), (pmask_sb[:], pmask),
        (gammab[0:1, 0:DM], gamma), (betab[0:1, 0:DM], beta),
        (gn_t[0:1, DM:2 * DM], dvec), (ones_colf[0:1, 0:P], cst["ones_row"]),
        (identf[:], cst["ident"]),
        (m_t[0:P, 4 * P:5 * P], cst["Fr"]), (m_t[0:P, 5 * P:6 * P], cst["Fi"]),
        (m_t[0:P, 6 * P:7 * P], cst["ICr"]), (m_t[0:P, 7 * P:8 * P], cst["ICi"]),
        (m_t[0:P, 0:P], cst["KFr"]), (m_t[0:P, P:2 * P], cst["KFi"]),
        (m_t[0:P, 2 * P:3 * P], cst["rev"]),
    ]:
        op("sync", lambda e, d=dst, s=src: e.dma_start(out=d, in_=s), s_x, 16)
    v_ld0 = s_x.v

    wait("scalar", s_x, v_ld0)
    op("scalar", lambda e: e.copy(frb[:], m_t[0:P, 4 * P:5 * P]))
    op("scalar", lambda e: e.copy(fib[:], m_t[0:P, 5 * P:6 * P]))
    op("scalar", lambda e: e.copy(icrb[:], m_t[0:P, 6 * P:7 * P]))
    op("scalar", lambda e: e.copy(icib[:], m_t[0:P, 7 * P:8 * P]))
    op("scalar", lambda e: e.copy(kfrb_c[:], m_t[0:P, 0:P]))
    op("scalar", lambda e: e.copy(kfib_c[:], m_t[0:P, P:2 * P]))
    op("scalar", lambda e: e.copy(revb[:], m_t[0:P, 2 * P:3 * P]))
    op("scalar", lambda e: e.copy(identb[:], identf[:]), s_ap)

    wait("tensor", s_x, v_ld0)
    for ps, row, off in [(psA, gammab, 0), (psB, betab, 0), (psC, gn_t, DM)]:
        for h in range(2):
            op("tensor", lambda e, ps=ps, row=row, h=h, off=off: e.matmul(
                ps[:, h * 512:(h + 1) * 512], ones_colf[0:1, :],
                row[0:1, off + h * 512:off + (h + 1) * 512], start=True, stop=True),
               s_rr if (ps is psC and h == 1) else None)
    wait("scalar", s_rr, s_rr.v)
    op("scalar", lambda e: e.copy(gammab[:], psA[:]))
    op("scalar", lambda e: e.copy(betab[:], psB[:]))
    op("scalar", lambda e: e.copy(gn_t[:, DM:2 * DM], psC[:]), s_ap)

    wait("scalar", s_x, v_ld0)
    op("scalar", lambda e: e.activation(sc64[:, 2:3], sc64[:, 0:1], AF.Exp))
    op("scalar", lambda e: e.activation(sc64[:, 3:4], sc64[:, 1:2], AF.Exp), s_ap)
    wait("vector", s_ap, s_ap.v)
    op("vector", lambda e: e.tensor_scalar(sc64[:, 14:15], sc64[:, 2:3], -1.0, None, ALU.mult), s_rv)
    wait("scalar", s_rv, s_rv.v)
    op("scalar", lambda e: e.activation(sc64[:, 6:7], sc64[:, 14:15], AF.Exp), s_ap)
    twopi = 2.0 * math.pi
    wait("vector", s_ap, s_ap.v)
    op("vector", lambda e: e.tensor_scalar(sc64[:, 7:8], sc64[:, 3:4], 1.0 / twopi,
                                           None, ALU.mult))
    op("vector", lambda e: e.tensor_copy(sci[:], sc64[:, 7:8]))
    op("vector", lambda e: e.tensor_copy(sc64[:, 10:11], sci[:]))
    op("vector", lambda e: e.scalar_tensor_tensor(sc64[:, 7:8], sc64[:, 10:11], -twopi,
                                                  sc64[:, 3:4], ALU.mult, ALU.add))
    op("vector", lambda e: e.tensor_single_scalar(sc64[:, 10:11], sc64[:, 7:8], math.pi,
                                                  ALU.is_gt))
    op("vector", lambda e: e.scalar_tensor_tensor(sc64[:, 7:8], sc64[:, 10:11], -twopi,
                                                  sc64[:, 7:8], ALU.mult, ALU.add))
    op("vector", lambda e: e.tensor_single_scalar(sc64[:, 10:11], sc64[:, 7:8], -math.pi,
                                                  ALU.is_lt))
    op("vector", lambda e: e.scalar_tensor_tensor(sc64[:, 7:8], sc64[:, 10:11], twopi,
                                                  sc64[:, 7:8], ALU.mult, ALU.add))
    op("vector", lambda e: e.tensor_scalar(sc64[:, 11:12], sc64[:, 7:8], -1.0, None,
                                           ALU.mult))
    op("vector", lambda e: e.tensor_tensor(sc64[:, 11:12], sc64[:, 11:12], sc64[:, 7:8],
                                           ALU.max))
    op("vector", lambda e: e.tensor_scalar(sc64[:, 11:12], sc64[:, 11:12], -1.0,
                                           math.pi / 2.0, ALU.mult, ALU.add), s_rv)
    wait("scalar", s_rv, s_rv.v)
    op("scalar", lambda e: e.activation(sc64[:, 9:10], sc64[:, 7:8], AF.Sin))
    op("scalar", lambda e: e.activation(sc64[:, 8:9], sc64[:, 11:12], AF.Sin), s_ap)
    wait("vector", s_ap, s_ap.v)
    op("vector", lambda e: e.tensor_mul(sc64[:, 4:5], sc64[:, 6:7], sc64[:, 8:9]))
    op("vector", lambda e: e.tensor_mul(sc64[:, 5:6], sc64[:, 6:7], sc64[:, 9:10]))

    op("vector", lambda e: e.memset(pw_re[:, 0:1], 1.0))
    op("vector", lambda e: e.memset(pw_im[:, 0:1], 0.0))
    op("vector", lambda e: e.tensor_copy(sc64[:, 12:13], sc64[:, 4:5]))
    op("vector", lambda e: e.tensor_copy(sc64[:, 13:14], sc64[:, 5:6]))

    def csq():
        op("vector", lambda e: e.tensor_mul(sc64[:, 14:15], sc64[:, 12:13], sc64[:, 12:13]))
        op("vector", lambda e: e.tensor_mul(sc64[:, 15:16], sc64[:, 13:14], sc64[:, 13:14]))
        op("vector", lambda e: e.tensor_mul(sc64[:, 13:14], sc64[:, 12:13], sc64[:, 13:14]))
        op("vector", lambda e: e.tensor_scalar(sc64[:, 13:14], sc64[:, 13:14], 2.0, None, ALU.mult))
        op("vector", lambda e: e.tensor_sub(sc64[:, 12:13], sc64[:, 14:15], sc64[:, 15:16]))

    b = 1
    while b < P:
        op("vector", lambda e, b=b: e.tensor_scalar(ctmp[0:NST, 0:b], pw_re[:, 0:b],
                                                    sc64[:, 12:13], None, ALU.mult))
        op("vector", lambda e, b=b: e.tensor_scalar(ctmp[0:NST, DM:DM + b], pw_im[:, 0:b],
                                                    sc64[:, 13:14], None, ALU.mult))
        op("vector", lambda e, b=b: e.tensor_sub(pw_re[:, b:2 * b], ctmp[0:NST, 0:b],
                                                 ctmp[0:NST, DM:DM + b]))
        op("vector", lambda e, b=b: e.tensor_scalar(ctmp[0:NST, 0:b], pw_re[:, 0:b],
                                                    sc64[:, 13:14], None, ALU.mult))
        op("vector", lambda e, b=b: e.tensor_scalar(ctmp[0:NST, DM:DM + b], pw_im[:, 0:b],
                                                    sc64[:, 12:13], None, ALU.mult))
        op("vector", lambda e, b=b: e.tensor_add(pw_im[:, b:2 * b], ctmp[0:NST, 0:b],
                                                 ctmp[0:NST, DM:DM + b]))
        csq()
        b *= 2
    op("vector", lambda e: e.tensor_copy(aq_cols[:, 0:1], sc64[:, 12:13]))
    op("vector", lambda e: e.tensor_copy(aq_cols[:, 1:2], sc64[:, 13:14]))
    csq()
    csq()
    op("vector", lambda e: e.tensor_copy(aq_cols[:, 2:3], sc64[:, 12:13]))
    op("vector", lambda e: e.tensor_copy(aq_cols[:, 3:4], sc64[:, 13:14]))
    op("vector", lambda e: e.memset(ones_mask[:], 1.0))

    op("vector", lambda e: e.memset(powtab[:, 0:1], 1.0))
    op("vector", lambda e: e.memset(powtab[:, 1:2], 0.0))
    op("vector", lambda e: e.tensor_copy(powtab[:, 2:3], aq_cols[:, 0:1]))
    op("vector", lambda e: e.tensor_copy(powtab[:, 3:4], aq_cols[:, 1:2]))
    op("vector", lambda e: e.tensor_mul(sc64[:, 14:15], aq_cols[:, 0:1], aq_cols[:, 0:1]))
    op("vector", lambda e: e.tensor_mul(sc64[:, 15:16], aq_cols[:, 1:2], aq_cols[:, 1:2]))
    op("vector", lambda e: e.tensor_sub(powtab[:, 4:5], sc64[:, 14:15], sc64[:, 15:16]))
    op("vector", lambda e: e.tensor_mul(sc64[:, 14:15], aq_cols[:, 0:1], aq_cols[:, 1:2]))
    op("vector", lambda e: e.tensor_scalar(powtab[:, 5:6], sc64[:, 14:15], 2.0, None, ALU.mult))
    op("vector", lambda e: e.tensor_mul(sc64[:, 14:15], powtab[:, 4:5], aq_cols[:, 0:1]))
    op("vector", lambda e: e.tensor_mul(sc64[:, 15:16], powtab[:, 5:6], aq_cols[:, 1:2]))
    op("vector", lambda e: e.tensor_sub(powtab[:, 6:7], sc64[:, 14:15], sc64[:, 15:16]))
    op("vector", lambda e: e.tensor_mul(sc64[:, 14:15], powtab[:, 4:5], aq_cols[:, 1:2]))
    op("vector", lambda e: e.tensor_mul(sc64[:, 15:16], powtab[:, 5:6], aq_cols[:, 0:1]))
    op("vector", lambda e: e.tensor_add(powtab[:, 7:8], sc64[:, 14:15], sc64[:, 15:16]))

    op("vector", lambda e: e.tensor_scalar(sc64[:, 14:15], sc64[:, 4:5], -1.0, None, ALU.add))
    op("vector", lambda e: e.tensor_mul(sc64[:, 15:16], sc64[:, 2:3], sc64[:, 2:3]))
    op("vector", lambda e: e.tensor_mul(sc64[:, 10:11], sc64[:, 3:4], sc64[:, 3:4]))
    op("vector", lambda e: e.tensor_add(sc64[:, 15:16], sc64[:, 15:16], sc64[:, 10:11]))
    op("vector", lambda e: e.reciprocal(sc64[:, 15:16], sc64[:, 15:16]))
    op("vector", lambda e: e.tensor_mul(sc64[:, 10:11], sc64[:, 14:15], sc64[:, 2:3]))
    op("vector", lambda e: e.tensor_mul(sc64[:, 11:12], sc64[:, 5:6], sc64[:, 3:4]))
    op("vector", lambda e: e.tensor_sub(sc64[:, 11:12], sc64[:, 11:12], sc64[:, 10:11]))
    op("vector", lambda e: e.tensor_mul(sc64[:, 6:7], sc64[:, 11:12], sc64[:, 15:16]))
    op("vector", lambda e: e.tensor_mul(sc64[:, 10:11], sc64[:, 5:6], sc64[:, 2:3]))
    op("vector", lambda e: e.tensor_mul(sc64[:, 11:12], sc64[:, 14:15], sc64[:, 3:4]))
    op("vector", lambda e: e.tensor_add(sc64[:, 10:11], sc64[:, 10:11], sc64[:, 11:12]))
    op("vector", lambda e: e.tensor_mul(sc64[:, 10:11], sc64[:, 10:11], sc64[:, 15:16]))
    op("vector", lambda e: e.tensor_scalar(sc64[:, 7:8], sc64[:, 10:11], -1.0, None, ALU.mult),
       s_rv)

    cr_r = cr.rearrange("(t p) n -> t p n", p=P)
    ci_r = ci.rearrange("(t p) n -> t p n", p=P)
    for half, src_r in [(0, cr_r), (1, ci_r)]:
        for t8 in range(8):
            buf = t8 % 2
            if half or t8 >= 2:
                wait("sync", s_tp, s_tp.v)
            op("sync", lambda e, src_r=src_r, t8=t8, buf=buf: e.dma_start(
                out=x_t[0:P, buf * DM:buf * DM + NST], in_=src_r[t8]), s_x, 16)
            wait("tensor", s_x, s_x.v)
            if half or t8:
                wait("tensor", s_rv, s_rv.v)
            op("tensor", lambda e, buf=buf: e.transpose(
                psD[0:NST, 0:P], x_t[0:P, buf * DM:buf * DM + NST], identf[:]), s_tp)
            wait("vector", s_tp, s_tp.v)
            op("vector", lambda e, half=half, t8=t8: e.tensor_copy(
                wtmp[0:NST, half * DM + t8 * P:half * DM + (t8 + 1) * P],
                psD[0:NST, 0:P]), s_rv)
    op("vector", lambda e: e.tensor_scalar(ctmp[0:NST, 0:DM], wtmp[0:NST, 0:DM],
                                           sc64[:, 6:7], None, ALU.mult))
    op("vector", lambda e: e.tensor_scalar(ctmp[0:NST, DM:2 * DM], wtmp[0:NST, DM:2 * DM],
                                           sc64[:, 7:8], None, ALU.mult))
    op("vector", lambda e: e.tensor_sub(cs_re[:], ctmp[0:NST, 0:DM], ctmp[0:NST, DM:2 * DM]))
    op("vector", lambda e: e.tensor_scalar(ctmp[0:NST, 0:DM], wtmp[0:NST, 0:DM],
                                           sc64[:, 7:8], None, ALU.mult))
    op("vector", lambda e: e.tensor_scalar(ctmp[0:NST, DM:2 * DM], wtmp[0:NST, DM:2 * DM],
                                           sc64[:, 6:7], None, ALU.mult))
    op("vector", lambda e: e.tensor_add(cs_im[:], ctmp[0:NST, 0:DM], ctmp[0:NST, DM:2 * DM]),
       s_rv)
    wait("scalar", s_rv, s_rv.v)
    op("scalar", lambda e: e.copy(csreb[:], cs_re[:]))
    op("scalar", lambda e: e.copy(csimb[:], cs_im[:]))

    wait("scalar", s_rv, s_rv.v)
    op("scalar", lambda e: e.copy(pkre[:], pw_re[:]), s_ap)
    op("vector", lambda e: e.tensor_scalar(pkim[:], pw_im[:], -1.0, None, ALU.mult), s_rv)
    wait("scalar", s_rv, s_rv.v)
    wait("tensor", s_rv, s_rv.v)

    for m in range(4):
        if m > 0:
            wait("vector", s_ap, s_ap.v)
        op("vector", lambda e, m=m: e.tensor_scalar(ctmp[0:NST, 0:P], pw_re[:],
                                                    powtab[:, 2 * m:2 * m + 1], None, ALU.mult))
        op("vector", lambda e, m=m: e.tensor_scalar(ctmp[0:NST, P:2 * P], pw_im[:],
                                                    powtab[:, 2 * m + 1:2 * m + 2], None, ALU.mult))
        op("vector", lambda e: e.tensor_sub(ctmp[0:NST, 2 * P:3 * P], ctmp[0:NST, 0:P],
                                            ctmp[0:NST, P:2 * P]))
        op("vector", lambda e, m=m: e.tensor_scalar(ctmp[0:NST, 0:P], pw_re[:],
                                                    powtab[:, 2 * m + 1:2 * m + 2], None, ALU.mult))
        op("vector", lambda e, m=m: e.tensor_scalar(ctmp[0:NST, P:2 * P], pw_im[:],
                                                    powtab[:, 2 * m:2 * m + 1], None, ALU.mult))
        op("vector", lambda e: e.tensor_add(ctmp[0:NST, 3 * P:4 * P], ctmp[0:NST, 0:P],
                                            ctmp[0:NST, P:2 * P]), s_rv)
        wait("scalar", s_rv, s_rv.v)
        op("scalar", lambda e: e.copy(xn2b_t[0:NST, 0:P], ctmp[0:NST, 2 * P:3 * P]))
        op("scalar", lambda e: e.copy(xn2b_t[0:NST, P:2 * P], ctmp[0:NST, 3 * P:4 * P]), s_ap)
        wait("tensor", s_ap, s_ap.v)
        if m > 0:
            wait("tensor", s_t4, s_t4.v)
        op("tensor", lambda e: e.transpose(psA.bitcast(BF16)[0:P, 0:NST],
                                           xn2b_t[0:NST, 0:P], identb[0:NST, 0:NST]))
        op("tensor", lambda e: e.transpose(psB.bitcast(BF16)[0:P, 0:NST],
                                           xn2b_t[0:NST, P:2 * P], identb[0:NST, 0:NST]),
           s_tp)
        wait("scalar", s_tp, s_tp.v)
        op("scalar", lambda e: e.copy(xn4_t[0:P, 0:NST], psA.bitcast(BF16)[0:P, 0:NST]))
        op("scalar", lambda e: e.copy(xn4_t[0:P, NST:2 * NST],
                                      psB.bitcast(BF16)[0:P, 0:NST]), s_ap)
        wait("tensor", s_ap, s_ap.v)
        op("tensor", lambda e: e.matmul(psA[0:P, 0:NST], revb[:], xn4_t[0:P, 0:NST],
                                        start=True, stop=True))
        op("tensor", lambda e: e.matmul(psB[0:P, 0:NST], revb[:], xn4_t[0:P, NST:2 * NST],
                                        start=True, stop=True), s_rr)
        wait("scalar", s_rr, s_rr.v)
        slots = [0, 4] if m == 0 else [4 - m]
        for sl in slots:
            op("scalar", lambda e, sl=sl: e.copy(asre[:, sl * NST:(sl + 1) * NST],
                                                 psA[0:P, 0:NST]))
            op("scalar", lambda e, sl=sl: e.copy(asim[:, sl * NST:(sl + 1) * NST],
                                                 psB[0:P, 0:NST]),
               s_t4 if sl == slots[-1] else None)
    wait("vector", s_t4, s_t4.v)

    if DEBUG == 7:
        wait("gpsimd", s_t4, s_t4.v)
        op("gpsimd", lambda e: e.dma_start(out=dbg_r[0][0:P, 0:320], in_=asre[:]), s_wdma, 16)
        op("gpsimd", lambda e: e.dma_start(out=dbg_r[1][0:P, 0:320], in_=asim[:]), s_wdma, 16)
        wait("gpsimd", s_rv, s_rv.v)
        op("gpsimd", lambda e: e.dma_start(out=dbg_r[2][0:NST, 0:P], in_=ctmp[0:NST, 2 * P:3 * P]), s_wdma, 16)
        op("gpsimd", lambda e: e.dma_start(out=dbg_r[2][64:P, 0:P], in_=ctmp[0:NST, 3 * P:4 * P]), s_wdma, 16)
        op("gpsimd", lambda e: e.dma_start(out=dbg_r[3][0:NST, 0:8], in_=powtab[:]), s_wdma, 16)
    wait("vector", s_ap, s_ap.v)
    op("vector", lambda e: e.tensor_scalar(ctmp[0:NST, 0:P], pw_re[:], sc64[:, 4:5], None, ALU.mult))
    op("vector", lambda e: e.tensor_scalar(ctmp[0:NST, P:2 * P], pw_im[:], sc64[:, 5:6], None, ALU.mult))
    op("vector", lambda e: e.tensor_sub(ctmp[0:NST, 2 * P:3 * P], ctmp[0:NST, 0:P],
                                        ctmp[0:NST, P:2 * P]))
    op("vector", lambda e: e.tensor_scalar(ctmp[0:NST, 0:P], pw_re[:], sc64[:, 5:6], None, ALU.mult))
    op("vector", lambda e: e.tensor_scalar(ctmp[0:NST, P:2 * P], pw_im[:], sc64[:, 4:5], None, ALU.mult))
    op("vector", lambda e: e.tensor_add(ctmp[0:NST, 3 * P:4 * P], ctmp[0:NST, 0:P],
                                        ctmp[0:NST, P:2 * P]), s_rv)
    wait("scalar", s_rv, s_rv.v)
    op("scalar", lambda e: e.copy(ppre[:], ctmp[0:NST, 2 * P:3 * P]), s_ap)
    op("vector", lambda e: e.tensor_scalar(ppim[:], ctmp[0:NST, 3 * P:4 * P], -1.0,
                                           None, ALU.mult), s_rv)

    wait("tensor", s_ap, s_ap.v)
    for h in range(2):
        op("tensor", lambda e, h=h: e.matmul(psA[:, h * 512:(h + 1) * 512], pkre[:],
                                             csreb[:, h * 512:(h + 1) * 512],
                                             start=True, stop=False))
        op("tensor", lambda e, h=h: e.matmul(psA[:, h * 512:(h + 1) * 512], pkim[:],
                                             csimb[:, h * 512:(h + 1) * 512],
                                             start=False, stop=True),
           s_rr if h == 1 else None)
    wait("scalar", s_rr, s_rr.v)
    op("scalar", lambda e: e.copy(kb[:], psA[:]), s_ap)
    wait("tensor", s_ap, s_ap.v)
    for h in range(2):
        op("tensor", lambda e, h=h: e.matmul(psA[:, h * 512:(h + 1) * 512], kfrb_c[:],
                                             kb[:, h * 512:(h + 1) * 512],
                                             start=True, stop=True))
        op("tensor", lambda e, h=h: e.matmul(psB[:, h * 512:(h + 1) * 512], kfib_c[:],
                                             kb[:, h * 512:(h + 1) * 512],
                                             start=True, stop=True),
           s_rr if h == 1 else None)
    wait("vector", s_rr, s_rr.v)
    op("vector", lambda e: e.tensor_add(kfr[:], psA[:], gn_t[:, DM:2 * DM]))
    op("vector", lambda e: e.tensor_copy(kfi[:], psB[:]))
    op("vector", lambda e: e.memset(kfi[0:1, 0:DM], 0.0))
    op("vector", lambda e: e.tensor_copy(kfr2[:], kfr[:]))
    op("vector", lambda e: e.tensor_add(kfr2[0:1, 0:DM], psB[0:1, 0:DM], gn_t[0:1, DM:2 * DM]))
    op("vector", lambda e: e.memset(carry_re[:], 0.0))
    op("vector", lambda e: e.memset(carry_im[:], 0.0), s_cu)
    if DEBUG == 5:
        wait("sync", s_cu, s_cu.v)
        op("sync", lambda e: e.dma_start(out=dbg_r[0][0:NST, 0:P], in_=pw_re[:]), s_res, 16)
        op("sync", lambda e: e.dma_start(out=dbg_r[0][64:128, 0:P], in_=pw_im[:]), s_res, 16)
        op("sync", lambda e: e.dma_start(out=dbg_r[1][0:NST, 0:DM], in_=cs_re[:]), s_res, 16)
        op("sync", lambda e: e.dma_start(out=dbg_r[2][0:NST, 0:DM], in_=cs_im[:]), s_res, 16)
        op("sync", lambda e: e.dma_start(out=dbg_r[3][0:NST, 0:16], in_=sc64[:]), s_res, 16)
        op("sync", lambda e: e.dma_start(out=dbg_r[3][64:128, 0:4], in_=aq_cols[:]), s_res, 16)
    if DEBUG == 4:
        wait("sync", s_cu, s_cu.v)
        op("sync", lambda e: e.dma_start(out=dbg_r[0], in_=kfr[:]), s_res, 16)
        op("sync", lambda e: e.dma_start(out=dbg_r[1], in_=kfi[:]), s_res, 16)
        op("sync", lambda e: e.dma_start(out=dbg_r[2], in_=kfr2[:]), s_res, 16)
        wait("gpsimd", s_cu, s_cu.v)
        op("gpsimd", lambda e: e.dma_start(out=dbg_r[3], in_=kb[:]), s_wdma, 16)

    xo_r = xo.rearrange("(t p) d -> t p d", p=P)
    xp_r = xp.rearrange("(t p) d -> t p d", p=P)
    xres_r = xres.rearrange("(t p) d -> t p d", p=P)
    out_r = out.rearrange("(t p) d -> t p d", p=P)


    def tree_steps(eng, buf, w0, sem=None):
        w = w0
        while w > 1:
            h = w // 2
            op(eng, lambda e, h=h, w=w, buf=buf: e.tensor_tensor(
                buf[:, 0:h], buf[:, 0:h], buf[:, h:w], ALU.add),
               sem if h == 1 else None)
            w = h

    _lncnt = [0]

    def ln_stats(src_fn, mu_col, var_col, tree_buf, sq_buf, gp_tree=False):
        _lncnt[0] += 1
        """Emit mean/var(+eps) of src [128 x 1024] into cols columns.
        ACT squares; DVE does two tensor_reduce sums + the stats tail.
        Ends with var+eps in var_col (inc s_st)."""
        op("scalar", lambda e: e.activation(sq_buf[:], src_fn(), AF.Square), s_rs)
        op("vector", lambda e: e.tensor_reduce(stat[:, 0:1], src_fn(),
                                               axis=AXX, op=ALU.add))
        wait("vector", s_rs, s_rs.v)
        op("vector", lambda e: e.tensor_reduce(cols[:, 14:15], sq_buf[:],
                                               axis=AXX, op=ALU.add))
        op("vector", lambda e: e.tensor_scalar(mu_col, stat[:, 0:1], 1.0 / DM,
                                               None, ALU.mult))
        op("vector", lambda e: e.tensor_scalar(cols[:, 14:15], cols[:, 14:15],
                                               1.0 / DM, EPS, ALU.mult, ALU.add))
        op("vector", lambda e: e.tensor_mul(cols[:, 15:16], mu_col, mu_col))
        op("vector", lambda e: e.tensor_sub(var_col, cols[:, 14:15], cols[:, 15:16]),
           s_st)

    def ln_pipeline(src_slice, dst_slice):
        wait("vector", s_x, s_x.v)
        wait("scalar", s_x, s_x.v)
        ln_stats(lambda: src_slice, cols[:, 0:1], cols[:, 1:2], None,
                 sq_t[:, 0:DM], gp_tree=False)
        wait("scalar", s_st, s_st.v)
        op("scalar", lambda e: e.activation(cols[:, 2:3], cols[:, 1:2], AF.Sqrt), s_rs)
        wait("vector", s_rs, s_rs.v)
        op("vector", lambda e: e.reciprocal(cols[:, 3:4], cols[:, 2:3]))
        op("vector", lambda e: e.tensor_scalar(cols[:, 4:5], cols[:, 0:1], cols[:, 3:4], -1.0,
                                               ALU.mult, ALU.mult), s_rc)
        wait("scalar", s_rc, s_rc.v)
        op("scalar", lambda e: e.activation(xn1_t[:, 0:DM], src_slice, AF.Identity,
                                            bias=cols[:, 4:5], scale=cols[:, 3:4]), s_ap)
        wait("vector", s_ap, s_ap.v)
        ln_stats(lambda: xn1_t[:, 0:DM], cols[:, 5:6], cols[:, 6:7], None,
                 sq_t[:, 0:DM], gp_tree=False)
        wait("scalar", s_st, s_st.v)
        op("scalar", lambda e: e.activation(cols[:, 7:8], cols[:, 6:7], AF.Sqrt), s_rs)
        wait("vector", s_rs, s_rs.v)
        op("vector", lambda e: e.reciprocal(cols[:, 8:9], cols[:, 7:8]))
        op("vector", lambda e: e.tensor_scalar(cols[:, 9:10], cols[:, 5:6], cols[:, 8:9], -1.0,
                                               ALU.mult, ALU.mult), s_rc)
        wait("scalar", s_rc, s_rc.v)
        wait("scalar", s_ln, s_ln.v)
        op("scalar", lambda e: e.activation(xn2a_t[:, 0:DM], xn1_t[:, 0:DM], AF.Identity,
                                            bias=cols[:, 9:10], scale=cols[:, 8:9]), s_ap)
        wait("gpsimd", s_ap, s_ap.v)
        op("gpsimd", lambda e: e.tensor_mul(gn_t[:, 0:DM], xn2a_t[:, 0:DM], gammab[:]), s_wg)
        wait("vector", s_wg, s_wg.v)
        wait("vector", s_tp, s_tp.v)
        if s_xc.v >= 1:
            wait("vector", s_xc, s_xc.v - 1)
        op("vector", lambda e, d=dst_slice: e.tensor_tensor(d, gn_t[:, 0:DM], betab[:],
                                                            ALU.add), s_ln)

    def carry_update(mask_col, re_col, im_col):
        wait("vector", s_xc, s_xc.v)
        op("vector", lambda e, ic=im_col: e.tensor_scalar(ctmp[0:NST, 0:DM], carry_im[:],
                                                          ic, None, ALU.mult))
        op("vector", lambda e, rc=re_col: e.scalar_tensor_tensor(
            ctmp[0:NST, DM:2 * DM], carry_re[:], rc, ctmp[0:NST, 0:DM],
            ALU.mult, ALU.subtract))
        op("vector", lambda e, ic=im_col: e.tensor_scalar(ctmp[0:NST, 0:DM], carry_re[:],
                                                          ic, None, ALU.mult))
        op("vector", lambda e, rc=re_col: e.scalar_tensor_tensor(
            carry_im[:], carry_im[:], rc, ctmp[0:NST, 0:DM], ALU.mult, ALU.add))
        op("vector", lambda e, m=mask_col: e.scalar_tensor_tensor(
            carry_re[:], psA[0:NST, 0:DM], m, ctmp[0:NST, DM:2 * DM], ALU.mult, ALU.add))
        op("vector", lambda e, m=mask_col: e.scalar_tensor_tensor(
            carry_im[:], psB[0:NST, 0:DM], m, carry_im[:], ALU.mult, ALU.add), s_cu)

    wait("tensor", s_cu, s_cu.v)

    ap_ln1_pair = {}
    xc_after_pair = {}

    def ln_pair(base, pidx):
        src2 = x_t[:, base:base + 2 * DM]
        src2v = x_t[:, base:base + 2 * DM].rearrange("p (b d) -> p b d", d=DM)
        sq2 = sq_t[:, 0:2 * DM]
        sq2v = sq_t[:, 0:2 * DM].rearrange("p (b d) -> p b d", d=DM)
        wait("scalar", s_x, s_x.v)
        wait("vector", s_x, s_x.v)

        def stats2(sfull, sview):
            op("scalar", lambda e: e.activation(sq2, sfull, AF.Square), s_rs)
            op("vector", lambda e: e.tensor_reduce(stat[:, 0:2], sview, axis=AXX,
                                                   op=ALU.add, negate=True))
            wait("vector", s_rs, s_rs.v)
            op("vector", lambda e: e.tensor_reduce(stat[:, 2:4], sq2v, axis=AXX,
                                                   op=ALU.add))
            op("vector", lambda e: e.tensor_scalar(stat[:, 4:6], stat[:, 0:2],
                                                   1.0 / DM, None, ALU.mult))
            op("vector", lambda e: e.tensor_scalar(stat[:, 2:4], stat[:, 2:4],
                                                   1.0 / DM, EPS, ALU.mult, ALU.add))
            op("vector", lambda e: e.tensor_mul(stat[:, 6:8], stat[:, 4:6],
                                                stat[:, 4:6]))
            op("vector", lambda e: e.tensor_sub(stat[:, 8:10], stat[:, 2:4],
                                                stat[:, 6:8]), s_st)
            wait("scalar", s_st, s_st.v)
            op("scalar", lambda e: e.activation(stat[:, 10:12], stat[:, 8:10],
                                                AF.Sqrt), s_rs)
            wait("vector", s_rs, s_rs.v)
            op("vector", lambda e: e.reciprocal(stat[:, 12:14], stat[:, 10:12]))
            op("vector", lambda e: e.tensor_mul(stat[:, 14:16], stat[:, 4:6],
                                                stat[:, 12:14]), s_rc)
            wait("scalar", s_rc, s_rc.v)

        stats2(src2, src2v)
        for j in range(2):
            op("scalar", lambda e, j=j: e.activation(
                xn1_t[:, j * DM:(j + 1) * DM],
                x_t[:, base + j * DM:base + (j + 1) * DM],
                AF.Identity, bias=stat[:, 14 + j:15 + j],
                scale=stat[:, 12 + j:13 + j]), s_ap if j == 1 else None)
        ap_ln1_pair[pidx] = s_ap.v
        xn12 = xn1_t[:, 0:2 * DM]
        xn12v = xn1_t[:, 0:2 * DM].rearrange("p (b d) -> p b d", d=DM)
        wait("vector", s_ap, s_ap.v)
        stats2(xn12, xn12v)
        wait("scalar", s_ln, s_ln.v)
        for j in range(2):
            op("scalar", lambda e, j=j: e.activation(
                xn2a_t[:, j * DM:(j + 1) * DM], xn1_t[:, j * DM:(j + 1) * DM],
                AF.Identity, bias=stat[:, 14 + j:15 + j],
                scale=stat[:, 12 + j:13 + j]), s_ap if j == 1 else None)
        wait("gpsimd", s_ap, s_ap.v)
        for j in range(2):
            op("gpsimd", lambda e, j=j: e.tensor_mul(
                gn_t[:, j * DM:(j + 1) * DM], xn2a_t[:, j * DM:(j + 1) * DM],
                gammab[:]), s_wg if j == 1 else None)
        wait("vector", s_wg, s_wg.v)
        wait("vector", s_tp, s_tp.v)
        if pidx >= 2:
            wait("vector", s_xc, xc_after_pair[pidx - 2])
        for j in range(2):
            op("vector", lambda e, j=j: e.tensor_tensor(
                xn2b_t[:, base + j * DM:base + (j + 1) * DM],
                gn_t[:, j * DM:(j + 1) * DM], betab[:], ALU.add), s_ln)

    for g in range(PREG):
        for half in range(2):
            pidx = g * 2 + half
            base = (pidx % 2) * 2 * DM
            if pidx == 0:
                wait("sync", s_tp, s_tp.v)
            if pidx >= 2:
                wait("sync", s_ap, ap_ln1_pair[pidx - 2])
            for j in range(2):
                op("sync", lambda e, ch=pidx * 2 + j, base=base, j=j: e.dma_start(
                    out=x_t[:, base + j * DM:base + (j + 1) * DM],
                    in_=xp_r[ch]), s_x, 16)
            ln_pair(base, pidx)
            wait("tensor", s_ln, s_ln.v)
            if half == 0:
                wait("tensor", s_cu, s_cu.v)
            for j in range(2):
                k = half * 2 + j
                for h in range(2):
                    op("tensor", lambda e, k=k, h=h, base=base, j=j: e.matmul(
                        psA[0:NST, h * 512:(h + 1) * 512],
                        asre[:, (1 + k) * NST:(2 + k) * NST],
                        xn2b_t[:, base + j * DM + h * 512:
                               base + j * DM + (h + 1) * 512],
                        start=(k == 0), stop=(k == GSUB - 1)))
                    op("tensor", lambda e, k=k, h=h, base=base, j=j: e.matmul(
                        psB[0:NST, h * 512:(h + 1) * 512],
                        asim[:, (1 + k) * NST:(2 + k) * NST],
                        xn2b_t[:, base + j * DM + h * 512:
                               base + j * DM + (h + 1) * 512],
                        start=(k == 0), stop=(k == GSUB - 1)),
                       s_xc if h == 1 else None)
            xc_after_pair[pidx] = s_xc.v
        carry_update(pmask_sb[:, g:g + 1], aq_cols[:, 2:3], aq_cols[:, 3:4])

    for i in range(NCH):
        buf = i % 2
        if True:
            if i >= 2:
                wait("sync", s_l3, s_l3.v)
            else:
                wait("sync", s_ap, s_ap.v)
        op("sync", lambda e, i=i, buf=buf: e.dma_start(
            out=x_t[:, buf * DM:(buf + 1) * DM], in_=xo_r[i]), s_x, 16)
        ln_pipeline(x_t[:, buf * DM:(buf + 1) * DM],
                    xn2b_t[:, buf * DM:(buf + 1) * DM])
        if DEBUG == 1:
            op("gpsimd", lambda e, i=i, buf=buf: e.dma_start(
                out=dbg_r[i], in_=xn2b_t[:, buf * DM:(buf + 1) * DM]), s_wdma, 16)
        if DEBUG == 12 and i == 1:
            wait("sync", s_st, s_st.v)
            op("sync", lambda e: e.dma_start(out=dbg_r[0], in_=sq_t[:, 0:DM]), s_res, 16)
            op("sync", lambda e: e.dma_start(out=dbg_r[1][0:P, 0:512], in_=tr_t[:]), s_res, 16)
        if DEBUG == 8 and i < 8:
            wait("sync", s_ap, s_ap.v)
            op("sync", lambda e, i=i: e.dma_start(out=dbg_r[2 * i], in_=xn1_t[:, 0:DM]), s_res, 16)
            op("sync", lambda e, i=i: e.dma_start(out=dbg_r[2 * i + 1][0:P, 0:16],
                                                  in_=cols[:]), s_res, 16)
        if DEBUG == 9 and i < 8:
            wait("sync", s_x, s_x.v)
            op("sync", lambda e, i=i, buf=buf: e.dma_start(
                out=dbg_r[i], in_=x_t[:, buf * DM:(buf + 1) * DM]), s_res, 16)
        wait("gpsimd", s_cu, s_cu.v)
        op("gpsimd", lambda e: e.tensor_mul(wtmp[0:NST, 0:DM], cs_re[:], carry_re[:]))
        op("gpsimd", lambda e: e.tensor_mul(wtmp[0:NST, DM:2 * DM], cs_im[:], carry_im[:]))
        op("gpsimd", lambda e: e.tensor_sub(w_re[:], wtmp[0:NST, 0:DM], wtmp[0:NST, DM:2 * DM]))
        op("gpsimd", lambda e: e.tensor_mul(wtmp[0:NST, 0:DM], cs_re[:], carry_im[:]))
        op("gpsimd", lambda e: e.tensor_mul(wtmp[0:NST, DM:2 * DM], cs_im[:], carry_re[:]))
        op("gpsimd", lambda e: e.tensor_add(w_im[:], wtmp[0:NST, 0:DM],
                                            wtmp[0:NST, DM:2 * DM]), s_wg)
        wait("scalar", s_wg, s_wg.v)
        op("scalar", lambda e: e.copy(wreb[:], w_re[:]))
        op("scalar", lambda e: e.copy(wimb[:], w_im[:]), s_wl)
        wait("tensor", s_ln, s_ln.v)
        wait("tensor", s_cu, s_cu.v)
        if i > 0:
            wait("tensor", s_pw, s_pw.v)
        for h in range(2):
            op("tensor", lambda e, h=h, buf=buf: e.matmul(
                psA[:, h * 512:(h + 1) * 512], frb[:],
                xn2b_t[:, buf * DM + h * 512:buf * DM + (h + 1) * 512],
                start=True, stop=True))
            op("tensor", lambda e, h=h, buf=buf: e.matmul(
                psB[:, h * 512:(h + 1) * 512], fib[:],
                xn2b_t[:, buf * DM + h * 512:buf * DM + (h + 1) * 512],
                start=True, stop=True),
               s_fft if h == 1 else None)
        wait("vector", s_fft, s_fft.v)
        op("vector", lambda e: e.tensor_mul(m_t[:, 0:DM], psA[:], kfr[:]))
        op("vector", lambda e: e.tensor_mul(m_t[:, DM:2 * DM], psB[:], kfi[:]))
        op("vector", lambda e: e.tensor_mul(m_t[:, 2 * DM:3 * DM], psA[:], kfi[:]))
        op("vector", lambda e: e.tensor_mul(m_t[:, 3 * DM:4 * DM], psB[:], kfr2[:]), s_pw)
        wait("gpsimd", s_pw, s_pw.v)
        op("gpsimd", lambda e: e.tensor_sub(m_t[:, 0:DM], m_t[:, 0:DM], m_t[:, DM:2 * DM]))
        op("gpsimd", lambda e: e.tensor_add(m_t[:, 2 * DM:3 * DM], m_t[:, 2 * DM:3 * DM],
                                            m_t[:, 3 * DM:4 * DM]), s_yy)
        wait("scalar", s_yy, s_yy.v)
        op("scalar", lambda e: e.copy(yr_t[:], m_t[:, 0:DM]))
        op("scalar", lambda e: e.copy(yi_t[:], m_t[:, 2 * DM:3 * DM]), s_ys)
        wait("tensor", s_ys, s_ys.v)
        wait("tensor", s_wl, s_wl.v)
        if i > 0:
            wait("tensor", s_l3, s_l3.v)
        for h in range(2):
            op("tensor", lambda e, h=h: e.matmul(psD[:, h * 512:(h + 1) * 512], icrb[:],
                                                 yr_t[:, h * 512:(h + 1) * 512],
                                                 start=True, stop=False))
            op("tensor", lambda e, h=h: e.matmul(psD[:, h * 512:(h + 1) * 512], icib[:],
                                                 yi_t[:, h * 512:(h + 1) * 512],
                                                 start=False, stop=False))
            op("tensor", lambda e, h=h: e.matmul(psD[:, h * 512:(h + 1) * 512], ppre[:],
                                                 wreb[:, h * 512:(h + 1) * 512],
                                                 start=False, stop=False))
            op("tensor", lambda e, h=h: e.matmul(psD[:, h * 512:(h + 1) * 512], ppim[:],
                                                 wimb[:, h * 512:(h + 1) * 512],
                                                 start=False, stop=True),
               s_cv if h == 1 else None)
        wait("tensor", s_pw, s_pw.v)
        for h in range(2):
            op("tensor", lambda e, h=h, buf=buf: e.matmul(
                psA[0:NST, h * 512:(h + 1) * 512], asre[:, 0:NST],
                xn2b_t[:, buf * DM + h * 512:buf * DM + (h + 1) * 512],
                start=True, stop=True))
            op("tensor", lambda e, h=h, buf=buf: e.matmul(
                psB[0:NST, h * 512:(h + 1) * 512], asim[:, 0:NST],
                xn2b_t[:, buf * DM + h * 512:buf * DM + (h + 1) * 512],
                start=True, stop=True),
               s_xc if h == 1 else None)
        carry_update(ones_mask[:, 0:1], aq_cols[:, 0:1], aq_cols[:, 1:2])
        if DEBUG == 2:
            wait("vector", s_cv, s_cv.v)
            op("vector", lambda e: e.tensor_copy(gn_t[:, 0:DM], psD[:]), s_rv)
            wait("sync", s_rv, s_rv.v)
            op("sync", lambda e, i=i: e.dma_start(out=dbg_r[i], in_=gn_t[:, 0:DM]), s_res, 16)
            wait("vector", s_res, s_res.v)
        wait("vector", s_cv, s_cv.v)
        op("vector", lambda e: e.tensor_copy(gn_t[:, 0:DM], psD[:]), s_rv)
        wait("scalar", s_rv, s_rv.v)
        ln_stats(lambda: gn_t[:, 0:DM], cols[:, 10:11], cols[:, 11:12], None,
                 sq_t[:, 0:DM], gp_tree=False)
        wait("scalar", s_st, s_st.v)
        op("scalar", lambda e: e.activation(cols[:, 12:13], cols[:, 11:12], AF.Sqrt), s_rs)
        wait("vector", s_rs, s_rs.v)
        if i >= 2:
            wait("vector", s_res, s_res.v - 16)
        op("vector", lambda e: e.reciprocal(cols[:, 13:14], cols[:, 12:13]))
        op("vector", lambda e: e.tensor_scalar(xe_t[:], gn_t[:, 0:DM], cols[:, 10:11],
                                               cols[:, 13:14], ALU.subtract, ALU.mult))
        op("vector", lambda e, buf=buf: e.tensor_add(
            xnew_t[:, buf * DM:(buf + 1) * DM], xe_t[:],
            x_t[:, buf * DM:(buf + 1) * DM]), s_l3)
        wait("sync", s_l3, s_l3.v)
        op("sync", lambda e, i=i, buf=buf: e.dma_start(
            out=xres_r[i], in_=xnew_t[:, buf * DM:(buf + 1) * DM]), s_res, 16)
        wait("scalar", s_l3, s_l3.v)
        ln_stats(lambda buf=buf: xnew_t[:, buf * DM:(buf + 1) * DM],
                 cols[:, 10:11], cols[:, 11:12], None, sq_t[:, 0:DM], gp_tree=False)
        wait("scalar", s_st, s_st.v)
        op("scalar", lambda e: e.activation(cols[:, 12:13], cols[:, 11:12], AF.Sqrt), s_rs)
        wait("vector", s_rs, s_rs.v)
        op("vector", lambda e: e.reciprocal(cols[:, 13:14], cols[:, 12:13]))
        op("vector", lambda e: e.tensor_scalar(cols[:, 6:7], cols[:, 10:11],
                                               cols[:, 13:14], -1.0, ALU.mult, ALU.mult),
           s_rc)
        wait("scalar", s_rc, s_rc.v)
        if i > 0:
            wait("scalar", s_tp, s_tp.v)
        op("scalar", lambda e, buf=buf: e.activation(
            xn4_t[:], xnew_t[:, buf * DM:(buf + 1) * DM], AF.Identity,
            bias=cols[:, 6:7], scale=cols[:, 13:14]), s_l4)
        if DEBUG == 10:
            wait("gpsimd", s_l4, s_l4.v)
            op("gpsimd", lambda e, i=i: e.dma_start(out=dbg_r[i], in_=xn4_t[:]), s_wdma, 16)
        wait("tensor", s_l4, s_l4.v)
        if i > 0:
            wait("tensor", s_t4, s_t4.v)
        for dc in range(8):
            op("tensor", lambda e, dc=dc: e.transpose(
                psC.bitcast(BF16)[:, dc * P:(dc + 1) * P],
                xn4_t[:, dc * P:(dc + 1) * P], identb[:]),
               s_tp if dc == 7 else None)
        wait("scalar", s_tp, s_tp.v)
        for dc in range(8):
            op("scalar", lambda e, dc=dc, i=i: e.copy(
                xn4T[:, dc * TOK + i * P:dc * TOK + (i + 1) * P],
                psC.bitcast(BF16)[:, dc * P:(dc + 1) * P]),
               s_t4 if dc == 7 else None)

    if STOP == "A":
        wait("sync", s_res, s_res.v)
        op("sync", lambda e: e.dma_start(out=out, in_=xres), s_out, 16)
        wait("sync", s_out, s_out.v)
        with nc.Block() as blk:
            blk.sync(lambda e: [f(e) for f in prog["sync"]])
            blk.tensor(lambda e: [f(e) for f in prog["tensor"]])
            blk.vector(lambda e: [f(e) for f in prog["vector"]])
            blk.scalar(lambda e: [f(e) for f in prog["scalar"]])
            blk.gpsimd(lambda e: [f(e) for f in prog["gpsimd"]])
        return nc

    wr_r = wrout.rearrange("(dc p) e -> p dc e", p=P)
    op("sync", lambda e: e.dma_start(
        out=m_t[0:P, 0:NE * 8].rearrange("p (dc e) -> p dc e", e=NE), in_=wr_r),
       s_x, 16)
    wait("scalar", s_x, s_x.v)
    op("scalar", lambda e: e.copy(wr_sb[:], m_t[0:P, 0:NE * 8]), s_ap)
    op("vector", lambda e: e.memset(ctile[:], 0.0))
    wait("tensor", s_ap, s_ap.v)
    wait("tensor", s_l3, s_l3.v)
    wait("tensor", s_t4, s_t4.v)
    for ti in range(NCH):
        if ti > 0:
            wait("tensor", s_rc2, s_rc2.v)
        for dc in range(8):
            op("tensor", lambda e, ti=ti, dc=dc: e.matmul(
                psD[0:P, 0:NE],
                xn4T[:, dc * TOK + ti * P:dc * TOK + (ti + 1) * P],
                wr_sb[:, dc * NE:(dc + 1) * NE],
                start=(dc == 0), stop=(dc == 7)),
               s_rr if dc == 7 else None)
        wait("vector", s_rr, s_rr.v)
        op("vector", lambda e: e.tensor_copy(lgits[:], psD[0:P, 0:NE]))
        op("vector", lambda e: e.tensor_tensor(rt8[:, 0:4], lgits[:, 0:4],
                                               lgits[:, 4:8], ALU.max))
        op("vector", lambda e: e.tensor_tensor(rt8[:, 0:2], rt8[:, 0:2],
                                               rt8[:, 2:4], ALU.max))
        op("vector", lambda e: e.tensor_tensor(rt8[:, 0:1], rt8[:, 0:1],
                                               rt8[:, 1:2], ALU.max))
        op("vector", lambda e: e.tensor_scalar(cols[:, 1:2], rt8[:, 0:1], -1.0, None,
                                               ALU.mult), s_rv)
        wait("scalar", s_rv, s_rv.v)
        op("scalar", lambda e: e.activation(e8[:], lgits[:], AF.Exp, bias=cols[:, 1:2]),
           s_ap)
        wait("vector", s_ap, s_ap.v)
        op("vector", lambda e: e.tensor_tensor(rt8[:, 4:8], e8[:, 0:4], e8[:, 4:8],
                                               ALU.add))
        op("vector", lambda e: e.tensor_tensor(rt8[:, 4:6], rt8[:, 4:6], rt8[:, 6:8],
                                               ALU.add))
        op("vector", lambda e: e.tensor_tensor(rt8[:, 4:5], rt8[:, 4:5], rt8[:, 5:6],
                                               ALU.add))
        op("vector", lambda e: e.reciprocal(cols[:, 3:4], rt8[:, 4:5]))
        op("vector", lambda e: e.tensor_scalar(cols[:, 4:5], cols[:, 3:4], ALPHA, None,
                                               ALU.mult))
        op("vector", lambda e: e.memset(cols[:, 7:8], 0.0))
        for ex in range(7):
            op("vector", lambda e, ex=ex: e.tensor_scalar(
                cols[:, 6:7], lgits[:, ex:ex + 1], rt8[:, 0:1], None, ALU.is_ge))
            op("vector", lambda e: e.tensor_scalar(cols[:, 8:9], cols[:, 7:8], -1.0, 1.0,
                                                   ALU.mult, ALU.add))
            op("vector", lambda e: e.tensor_mul(cols[:, 6:7], cols[:, 6:7], cols[:, 8:9]))
            op("vector", lambda e, ex=ex: e.tensor_mul(ctile[:, ex:ex + 1],
                                                       cols[:, 6:7], cols[:, 4:5]))
            op("vector", lambda e: e.tensor_tensor(cols[:, 7:8], cols[:, 7:8],
                                                   cols[:, 6:7], ALU.max))
        op("vector", lambda e: e.memset(ctile[:, 7:8], 1.0))
        op("vector", lambda e: e.tensor_copy(ctileb[:], ctile[:]), s_rv)
        wait("tensor", s_rv, s_rv.v)
        op("tensor", lambda e: e.transpose(psD.bitcast(BF16)[0:NE, 0:P], ctileb[:],
                                           identb[:]), s_rt)
        wait("scalar", s_rt, s_rt.v)
        op("scalar", lambda e, ti=ti: e.copy(coef_rows[:, ti * P:(ti + 1) * P],
                                             psD.bitcast(BF16)[0:NE, 0:P]), s_rc2)

    if DEBUG == 11:
        wait("gpsimd", s_t4, s_t4.v)
        for j in range(16):
            op("gpsimd", lambda e, j=j: e.dma_start(
                out=dbg_r[j], in_=xn4T[:, j * DM:(j + 1) * DM]), s_wdma, 16)
    if DEBUG == 3:
        wait("sync", s_rc2, s_rc2.v)
        op("sync", lambda e: e.dma_start(out=dbg_r[0][0:NE, 0:DM],
                                         in_=coef_rows[:, 0:DM]), s_res, 16)
        op("sync", lambda e: e.dma_start(out=dbg_r[1][0:NE, 0:DM],
                                         in_=coef_rows[:, DM:2 * DM]), s_res, 16)
    wait("sync", s_rc2, s_rc2.v)
    op("sync", lambda e: e.dma_start(out=coefd, in_=coef_rows[:]), s_xr, 16)

    if STOP == "R":
        wait("sync", s_res, s_res.v)
        wait("sync", s_rc2, s_rc2.v)
        op("sync", lambda e: e.dma_start(out=out, in_=xres), s_out, 16)
        wait("sync", s_out, s_out.v)
        with nc.Block() as blk:
            blk.sync(lambda e: [f(e) for f in prog["sync"]])
            blk.tensor(lambda e: [f(e) for f in prog["tensor"]])
            blk.vector(lambda e: [f(e) for f in prog["vector"]])
            blk.scalar(lambda e: [f(e) for f in prog["scalar"]])
            blk.gpsimd(lambda e: [f(e) for f in prog["gpsimd"]])
        return nc

    for _g in reversed(_a_guards):
        _g.__exit__(None, None, None)
    wu_buf = S("wu_buf", [P, 3 * 2048], BF16)
    wg_buf = S("wg_buf", [P, 3 * 2048], BF16)
    wd_buf = S("wd_buf", [P, 3 * 2048], BF16)
    xs = S("xs", [P, 8 * TH], BF16)
    h_half = S("h_half", [P, 16 * TH], BF16)
    accT = S("accT", [P, 8 * TH], F32)
    coefb = S("coefb", [P, TH], BF16)
    sg_t = S("sg_t", [P, 2 * TC], F32)
    sq5_t = S("sq5_t", [P, DM], F32)
    pc_t = S("pc_t", [P, DM], F32)
    out_t = S("out_t", [P, 2 * DM], F32)
    xr_t = S("xr_t", [P, 2 * DM], F32)

    wup_r = wup.rearrange("e (dc p) h -> e p dc h", p=P)
    wgate_r = wgate.rearrange("e (dc p) h -> e p dc h", p=P)
    wdown_r = wdown.rearrange("e (hc p) d -> e p hc d", p=P)
    wu_v = wu_buf.rearrange("p (b dc h) -> p b dc h", b=3, h=256)
    wg_v = wg_buf.rearrange("p (b dc h) -> p b dc h", b=3, h=256)
    wd_v = wd_buf.rearrange("p (b hc dl) -> p b hc dl", b=3, dl=P)

    wait("gpsimd", s_rc2, s_rc2.v)
    last_ug_of_buf = {0: 0, 1: 0, 2: 0}
    last_dn_of_buf = {0: 0, 1: 0, 2: 0}

    def ffn_expert(th, e, first):
        wait("gpsimd", s_xs, s_xs.v)
        if first:
            wait("gpsimd", s_xr, s_xr.v)
        op("gpsimd", lambda e_, e=e, th=th: e_.dma_start(
            out=coefb[:],
            in_=coefd[e:e + 1, th * TH:(th + 1) * TH].broadcast_to([P, TH])),
           s_cb, 16)
        wait("vector", s_cb, s_cb.v)
        if not first:
            wait("vector", s_ug, s_ug.v)
        for dc in range(8):
            op("vector", lambda e_, dc=dc, th=th: e_.tensor_mul(
                xs[:, dc * TH:(dc + 1) * TH],
                xn4T[:, dc * TOK + th * TH:dc * TOK + (th + 1) * TH], coefb[:]),
               s_xs if dc == 7 else None)
        for hh in range(2):
            for hg in range(8):
                gidx = hh * 8 + hg
                b = gidx % 3
                if last_ug_of_buf[b]:
                    wait("gpsimd", s_ug, last_ug_of_buf[b])
                h0 = (hh * 16 + hg * 2) * P
                op("gpsimd", lambda e_, e=e, h0=h0, b=b: e_.dma_start(
                    out=wu_v[:, b], in_=wup_r[e][:, :, h0:h0 + 256]), s_wdma, 16)
                op("gpsimd", lambda e_, e=e, h0=h0, b=b: e_.dma_start(
                    out=wg_v[:, b], in_=wgate_r[e][:, :, h0:h0 + 256]), s_wdma, 16)
                wait("tensor", s_wdma, s_wdma.v)
                if hg == 0:
                    wait("tensor", s_ac, s_ac.v)
                    if hh == 0:
                        wait("tensor", s_xs, s_xs.v)
                        wait("tensor", s_e5, s_e5.v)
                for hl in range(2):
                    hc = hh * 16 + hg * 2 + hl
                    hcl = hg * 2 + hl
                    pu, pg = (psA, psB) if hc % 2 == 0 else (psC, psD)
                    if s_hm.v >= 2:
                        wait("tensor", s_hm, s_hm.v - 1)
                    if s_sg.v >= 2:
                        wait("tensor", s_sg, s_sg.v - 1)
                    for tcol in range(2):
                        for dc in range(8):
                            op("tensor", lambda e_, b=b, hl=hl, dc=dc, tcol=tcol,
                               pu=pu: e_.matmul(
                                pu[:, tcol * 512:(tcol + 1) * 512],
                                wu_v[:, b, dc, hl * P:(hl + 1) * P],
                                xs[:, dc * TH + tcol * 512:dc * TH + (tcol + 1) * 512],
                                start=(dc == 0), stop=(dc == 7)))
                        for dc in range(8):
                            op("tensor", lambda e_, b=b, hl=hl, dc=dc, tcol=tcol,
                               pg=pg, th=th: e_.matmul(
                                pg[:, tcol * 512:(tcol + 1) * 512],
                                wg_v[:, b, dc, hl * P:(hl + 1) * P],
                                xn4T[:, dc * TOK + th * TH + tcol * 512:
                                     dc * TOK + th * TH + (tcol + 1) * 512],
                                start=(dc == 0), stop=(dc == 7)),
                               s_ug if (dc == 7 and tcol == 1) else None)
                    wait("scalar", s_ug, s_ug.v)
                    for tcol in range(2):
                        op("scalar", lambda e_, pg=pg, tcol=tcol: e_.activation(
                            sg_t[:, tcol * TC:(tcol + 1) * TC],
                            pg[:, tcol * 512:(tcol + 1) * 512], AF.Silu),
                           s_sg if tcol == 1 else None)
                    wait("vector", s_sg, s_sg.v)
                    for tcol in range(2):
                        op("vector", lambda e_, pu=pu, hcl=hcl, tcol=tcol: e_.tensor_mul(
                            h_half[:, hcl * TH + tcol * 512:hcl * TH + (tcol + 1) * 512],
                            pu[:, tcol * 512:(tcol + 1) * 512],
                            sg_t[:, tcol * TC:(tcol + 1) * TC]),
                           s_hm if tcol == 1 else None)
                last_ug_of_buf[b] = s_ug.v
            for dc in range(8):
                b = dc % 3
                if last_dn_of_buf[b]:
                    wait("gpsimd", s_dn, last_dn_of_buf[b])
                op("gpsimd", lambda e_, e=e, hh=hh, dc=dc, b=b: e_.dma_start(
                    out=wd_v[:, b],
                    in_=wdown_r[e][:, hh * 16:(hh + 1) * 16, dc * P:(dc + 1) * P]),
                   s_wdma, 16)
                wait("tensor", s_wdma, s_wdma.v)
                pd = psA if dc % 2 == 0 else psB
                if dc == 0:
                    wait("tensor", s_hm, s_hm.v)
                    wait("tensor", s_sg, s_sg.v)
                elif s_ac.v >= 2:
                    wait("tensor", s_ac, s_ac.v - 1)
                for tcol in range(2):
                    for hcl in range(16):
                        op("tensor", lambda e_, b=b, hcl=hcl, tcol=tcol, pd=pd: e_.matmul(
                            pd[:, tcol * 512:(tcol + 1) * 512],
                            wd_v[:, b, hcl, 0:P],
                            h_half[:, hcl * TH + tcol * 512:hcl * TH + (tcol + 1) * 512],
                            start=(hcl == 0), stop=(hcl == 15)),
                           s_dn if (hcl == 15 and tcol == 1) else None)
                last_dn_of_buf[b] = s_dn.v
                wait("vector", s_dn, s_dn.v)
                if first and hh == 0 and dc == 0:
                    wait("vector", s_am, s_am.v)
                for tcol in range(2):
                    op("vector", lambda e_, dc=dc, tcol=tcol, pd=pd: e_.tensor_tensor(
                        accT[:, dc * TH + tcol * 512:dc * TH + (tcol + 1) * 512],
                        accT[:, dc * TH + tcol * 512:dc * TH + (tcol + 1) * 512],
                        pd[:, tcol * 512:(tcol + 1) * 512], ALU.add),
                       s_ac if tcol == 1 else None)

    def phase_c(th):
        for tt in range(TH // P):
            ti = th * (TH // P) + tt
            buf = tt % 2
            wait("sync", s_l3, s_l3.v)
            op("sync", lambda e_, ti=ti, buf=buf: e_.dma_start(
                out=xr_t[:, buf * DM:(buf + 1) * DM], in_=xres_r[ti]), s_xr, 16)
            pt = psC if tt % 2 == 0 else psD
            if tt == 0:
                wait("tensor", s_ac, s_ac.v)
                wait("tensor", s_sg, s_sg.v)
                wait("tensor", s_hm, s_hm.v)
            elif s_e5.v >= 2:
                wait("tensor", s_e5, s_e5.v - 1)
            for dc in range(8):
                op("tensor", lambda e_, dc=dc, tt=tt, pt=pt: e_.transpose(
                    pt[:, dc * P:(dc + 1) * P],
                    accT[:, dc * TH + tt * P:dc * TH + (tt + 1) * P], identf[:]),
                   s_c5 if dc == 7 else None)
            wait("vector", s_c5, s_c5.v)
            op("vector", lambda e_, pt=pt: e_.tensor_copy(pc_t[:], pt[:]), s_rv)
            wait("scalar", s_rv, s_rv.v)
            ln_stats(lambda: pc_t[:], cols[:, 10:11], cols[:, 11:12], None, sq5_t,
                     gp_tree=False)
            wait("scalar", s_st, s_st.v)
            op("scalar", lambda e_: e_.activation(cols[:, 12:13], cols[:, 11:12],
                                                  AF.Sqrt), s_rs)
            wait("vector", s_rs, s_rs.v)
            wait("vector", s_xr, s_xr.v)
            if s_out.v >= 32:
                wait("vector", s_out, s_out.v - 16)
            op("vector", lambda e_: e_.reciprocal(cols[:, 13:14], cols[:, 12:13]))
            op("vector", lambda e_: e_.tensor_scalar(
                xe_t[:], pc_t[:], cols[:, 10:11], cols[:, 13:14],
                ALU.subtract, ALU.mult), s_e5)
            op("vector", lambda e_, buf=buf: e_.tensor_add(
                out_t[:, buf * DM:(buf + 1) * DM], xe_t[:],
                xr_t[:, buf * DM:(buf + 1) * DM]), s_l3)
            wait("sync", s_l3, s_l3.v)
            op("sync", lambda e_, ti=ti, buf=buf: e_.dma_start(
                out=out_r[ti], in_=out_t[:, buf * DM:(buf + 1) * DM]), s_out, 16)

    if STOP == "B":
        wait("vector", s_c5, s_c5.v)
        op("vector", lambda e_: e_.memset(accT[:], 0.0), s_am)
        ffn_expert(0, 0, first=True)
        phase_c(0)
        wait("sync", s_res, s_res.v)
        op("sync", lambda e: e.dma_start(out=out.rearrange("(h t) d -> h t d", h=2)[1],
                                         in_=xres.rearrange("(h t) d -> h t d", h=2)[1]),
           s_out, 16)
    else:
        for th in range(NTH):
            wait("vector", s_c5, s_c5.v)
            op("vector", lambda e_: e_.memset(accT[:], 0.0), s_am)
            for ei in range(NE):
                ffn_expert(th, ei, first=(th == 0 and ei == 0))
            phase_c(th)

    wait("sync", s_out, s_out.v)
    wait("sync", s_res, s_res.v)

    with nc.Block() as blk:
        blk.sync(lambda e: [f(e) for f in prog["sync"]])
        blk.tensor(lambda e: [f(e) for f in prog["tensor"]])
        blk.vector(lambda e: [f(e) for f in prog["vector"]])
        blk.scalar(lambda e: [f(e) for f in prog["scalar"]])
        blk.gpsimd(lambda e: [f(e) for f in prog["gpsimd"]])
    return nc


def kernel(L0_attn_x, L0_attn_gamma, L0_attn_beta, L0_attn_LR, L0_attn_LI,
           L0_attn_CR, L0_attn_CI, L0_attn_D, L0_ffn_w_router, L0_ffn_w_up,
           L0_ffn_w_gate, L0_ffn_w_down, L0_ffn_ws_up, L0_ffn_ws_gate,
           L0_ffn_ws_down):
    x = np.ascontiguousarray(np.asarray(L0_attn_x, dtype=np.float32)).reshape(
        NB * LSEQ, DM)
    wupc = np.ascontiguousarray(np.concatenate(
        [np.asarray(L0_ffn_w_up, np.float32),
         np.asarray(L0_ffn_ws_up, np.float32)[None]], axis=0))
    wgatec = np.ascontiguousarray(np.concatenate(
        [np.asarray(L0_ffn_w_gate, np.float32),
         np.asarray(L0_ffn_ws_gate, np.float32)[None]], axis=0))
    wdownc = np.ascontiguousarray(np.concatenate(
        [np.asarray(L0_ffn_w_down, np.float32),
         np.asarray(L0_ffn_ws_down, np.float32)[None]], axis=0))

    nc = bass.Bass("TRN2", target_bir_lowering=False, detect_race_conditions=False)
    build(nc)

    in_maps = []
    for c in range(8):
        q = c % 4
        bstart = (c // 4) * LSEQ
        xov = x[bstart + q * TOK: bstart + (q + 1) * TOK]
        xpv = np.zeros((PRETOK, DM), np.float32)
        if q > 0:
            xpv[PRETOK - q * TOK:] = x[bstart: bstart + q * TOK]
        pm = np.zeros((NST, PREG), np.float32)
        ng = q * TOK // 512
        if ng > 0:
            pm[:, PREG - ng:] = 1.0
        in_maps.append({
            "xo": np.ascontiguousarray(xov),
            "xp": xpv,
            "pmask": pm,
            "lr": np.asarray(L0_attn_LR, np.float32).reshape(NST, 1),
            "li": np.asarray(L0_attn_LI, np.float32).reshape(NST, 1),
            "cr": np.ascontiguousarray(np.asarray(L0_attn_CR, np.float32)),
            "ci": np.ascontiguousarray(np.asarray(L0_attn_CI, np.float32)),
            "dvec": np.asarray(L0_attn_D, np.float32).reshape(1, DM),
            "gamma": np.asarray(L0_attn_gamma, np.float32).reshape(1, DM),
            "beta": np.asarray(L0_attn_beta, np.float32).reshape(1, DM),
            "wrout": np.ascontiguousarray(np.asarray(L0_ffn_w_router, np.float32)),
            "wup": wupc, "wgate": wgatec, "wdown": wdownc,
        })

    res = run_bass_kernel_spmd(nc, in_maps, core_ids=list(range(8)))
    full = np.concatenate([res.results[c]["out"] for c in range(8)], axis=0)
    return full.reshape(NB, LSEQ, DM)


# version 12
# speedup vs baseline: 1.0442x; 1.0442x over previous
import math
import numpy as np
import concourse.bass as bass
import concourse.mybir as mybir
from concourse.bass_utils import run_bass_kernel_spmd

F32 = mybir.dt.float32
BF16 = mybir.dt.bfloat16
U32 = mybir.dt.uint32
AF = mybir.ActivationFunctionType
ALU = mybir.AluOpType
AXX = mybir.AxisListType.X
AX = mybir.AxisListType

P = 128
DM = 1024
HID = 4096
NE = 8
NST = 64
LSEQ = 8192
NB = 2
TOK = 2048
NCH = 16
PRETOK = 6144
PREG = 12
GSUB = 4
TH = 1024
NTH = 2
TC = 512
EPS = 1e-5
ALPHA = 2.0


def _host_consts():
    c = {}
    tl = np.arange(P, dtype=np.float64)
    f = np.arange(P, dtype=np.float64)
    ang = 2.0 * np.pi * np.outer(tl, f) / 256.0
    sgn = ((-1.0) ** f)[None, :]
    Fr = sgn * np.cos(ang)
    Fi = -sgn * np.sin(ang)
    Fi[:, 0] = (-1.0) ** tl
    c["Fr"], c["Fi"] = Fr, Fi
    KFr = np.cos(ang)
    KFi = -np.sin(ang)
    KFi[:, 0] = (-1.0) ** tl
    c["KFr"], c["KFi"] = KFr, KFi
    w = np.full(P, 2.0 / 256.0)
    w[0] = 1.0 / 256.0
    ICr = (w * ((-1.0) ** f))[:, None] * np.cos(ang.T)
    ICi = -(w * ((-1.0) ** f))[:, None] * np.sin(ang.T)
    ICi[0, :] = (1.0 / 256.0) * ((-1.0) ** tl)
    c["ICr"], c["ICi"] = ICr, ICi
    c["ident"] = np.eye(P)
    c["rev"] = np.eye(P)[::-1].copy()
    c["ones_row"] = np.ones((1, P))
    return {k: np.ascontiguousarray(v, dtype=np.float32) for k, v in c.items()}


class _Sem:
    def __init__(self, h):
        self.h = h
        self.v = 0

    def inc(self, n=1):
        self.v += n
        return self.v


_DTSZ = {"dt.float32": 4, "dt.int32": 4, "dt.uint32": 4, "dt.bfloat16": 2,
         "dt.float16": 2, "dt.uint8": 1, "dt.int8": 1}


def _ap_rect(x):
    if type(x).__name__ != "AP":
        return None
    try:
        sp = str(x.space)
        if ("SBUF" not in sp) and ("PSUM" not in sp):
            return None
        h = x.tensor
        hshape = list(h.shape)
        wb = int(hshape[-1]) * _DTSZ.get(str(h.dtype), 4)
        asz = _DTSZ.get(str(x.dtype), 4)
        wc = max(1, wb // asz)
        off = int(x.offset)
        p0 = off // wc
        f0 = off % wc
        shp = list(x.shape)
        if len(shp) == 2:
            pn, fn_ = shp
        elif len(shp) == 1:
            pn, fn_ = 1, shp[0]
        else:
            pn = shp[0]
            fn_ = 1
            for s_ in shp[1:]:
                fn_ *= s_
        return (x.name, p0, p0 + pn, f0 * asz, (f0 + fn_) * asz)
    except Exception:
        try:
            return (x.name, 0, 1 << 20, 0, 1 << 40)
        except Exception:
            return None


def _olap(a, b):
    return (a[0] == b[0] and a[1] < b[2] and b[1] < a[2]
            and a[3] < b[4] and b[3] < a[4])


class _RecDummy:
    def __getattr__(self, n):
        return lambda *a, **k: self


class _Rec:
    def __init__(self):
        self.calls = 0
        self.reads = []
        self.writes = []
        self.names = []
        self._dummy = _RecDummy()

    def __getattr__(self, name):
        def f(*a, **k):
            self.calls += 1
            self.names.append(name)
            vals = list(a) + [v for kk, v in k.items() if kk != "out"]
            first = k["out"] if "out" in k else None
            for v in vals:
                r = _ap_rect(v)
                if r is None:
                    continue
                if first is None:
                    first = v
                    self.writes.append(r)
                else:
                    self.reads.append(r)
            if "out" in k:
                r = _ap_rect(k["out"])
                if r is not None:
                    self.writes.append(r)
            return self._dummy
        return f


DEBUG = 0
STOP = ""


def build(nc):
    con = _host_consts()

    xo = nc.dram_tensor("xo", [TOK, DM], F32, kind="ExternalInput").ap()
    xp = nc.dram_tensor("xp", [PRETOK, DM], F32, kind="ExternalInput").ap()
    pmask = nc.dram_tensor("pmask", [NST, PREG], F32, kind="ExternalInput").ap()
    lr = nc.dram_tensor("lr", [NST, 1], F32, kind="ExternalInput").ap()
    li = nc.dram_tensor("li", [NST, 1], F32, kind="ExternalInput").ap()
    cr = nc.dram_tensor("cr", [DM, NST], F32, kind="ExternalInput").ap()
    ci = nc.dram_tensor("ci", [DM, NST], F32, kind="ExternalInput").ap()
    dvec = nc.dram_tensor("dvec", [1, DM], F32, kind="ExternalInput").ap()
    gamma = nc.dram_tensor("gamma", [1, DM], F32, kind="ExternalInput").ap()
    beta = nc.dram_tensor("beta", [1, DM], F32, kind="ExternalInput").ap()
    wrout = nc.dram_tensor("wrout", [DM, NE], F32, kind="ExternalInput").ap()
    wup = nc.dram_tensor("wup", [NE, DM, HID], F32, kind="ExternalInput").ap()
    wgate = nc.dram_tensor("wgate", [NE, DM, HID], F32, kind="ExternalInput").ap()
    wdown = nc.dram_tensor("wdown", [NE, HID, DM], F32, kind="ExternalInput").ap()
    out = nc.dram_tensor("out", [TOK, DM], F32, kind="ExternalOutput").ap()
    xres = nc.dram_tensor("xres", [TOK, DM], F32,
                          kind="ExternalOutput" if DEBUG else "Internal").ap()
    coefd = nc.dram_tensor("coefd", [NE, TOK], F32, kind="Internal").ap()
    dbg = (nc.dram_tensor("dbg", [TOK, DM], F32, kind="ExternalOutput").ap()
           if DEBUG else None)
    dbg_r = dbg.rearrange("(t p) d -> t p d", p=P) if DEBUG else None

    cst = {k: nc.inline_tensor(v, name="c_" + k).ap() for k, v in con.items()}

    def S(name, shape, dtype):
        return nc.alloc_sbuf_tensor(name, shape, dtype)

    _a_guards = []

    def SA(name, shape, dtype):
        g = nc.sbuf_tensor(name, shape, dtype)
        h = g.__enter__()
        _a_guards.append(g)
        return h

    xn4T = S("xn4T", [P, 8 * TOK], BF16)
    coef_rows = S("coef_rows", [NE, TOK], F32)
    identb = S("identb", [P, P], BF16)
    identf = S("identf", [P, P], F32)
    ones_colf = S("ones_colf", [1, P], F32)
    stat = S("stat", [P, 16], F32)
    cols = S("cols", [P, 16], F32)
    wr_sb = S("wr_sb", [P, NE * 8], BF16)
    xe_t = S("xe_t", [P, DM], F32)
    lgits = S("lgits", [P, NE], F32)
    e8 = S("e8", [P, NE], F32)
    i8u = S("i8u", [P, NE], U32)
    v8 = S("v8", [P, NE], F32)
    ctile = S("ctile", [P, NE], F32)
    ctileb = S("ctileb", [P, NE], BF16)
    rt8 = S("rt8", [P, NE], F32)
    junkc = S("junkc", [P, 4], F32)
    carry_re = SA("carry_re", [NST, DM], F32)
    carry_im = SA("carry_im", [NST, DM], F32)
    cs_re = SA("cs_re", [NST, DM], F32)
    cs_im = SA("cs_im", [NST, DM], F32)
    csreb = SA("csreb", [NST, DM], BF16)
    csimb = SA("csimb", [NST, DM], BF16)
    kfr = SA("kfr", [P, DM], F32)
    kfi = SA("kfi", [P, DM], F32)
    kfr2 = SA("kfr2", [P, DM], F32)
    gammab = SA("gammab", [P, DM], F32)
    betab = SA("betab", [P, DM], F32)
    frb = SA("frb", [P, P], BF16)
    fib = SA("fib", [P, P], BF16)
    icrb = SA("icrb", [P, P], BF16)
    icib = SA("icib", [P, P], BF16)
    kfrb_c = SA("kfrb_c", [P, P], BF16)
    kfib_c = SA("kfib_c", [P, P], BF16)
    revb = SA("revb", [P, P], BF16)
    asre = SA("asre", [P, NST * 5], BF16)
    asim = SA("asim", [P, NST * 5], BF16)
    ppre = SA("ppre", [NST, P], BF16)
    ppim = SA("ppim", [NST, P], BF16)
    pkre = SA("pkre", [NST, P], BF16)
    pkim = SA("pkim", [NST, P], BF16)
    kb = SA("kb", [P, DM], BF16)
    aq_cols = SA("aq_cols", [NST, 4], F32)
    pmask_sb = SA("pmask_sb", [NST, PREG], F32)
    ones_mask = SA("ones_mask", [NST, 1], F32)
    pw_re = SA("pw_re", [NST, P], F32)
    pw_im = SA("pw_im", [NST, P], F32)
    sc64 = SA("sc64", [NST, 16], F32)
    sci = SA("sci", [NST, 1], mybir.dt.int32)
    powtab = SA("powtab", [NST, 8], F32)
    powrow = SA("powrow", [8, NST], F32)
    x_t = SA("x_t", [P, 4 * DM], F32)
    xn1_t = SA("xn1_t", [P, 2 * DM], F32)
    xn2a_t = SA("xn2a_t", [P, 2 * DM], F32)
    gn_t = SA("gn_t", [P, 2 * DM], F32)
    sq_t = SA("sq_t", [P, 2 * DM], F32)
    xn2b_t = SA("xn2b_t", [P, 4 * DM], BF16)
    m_t = SA("m_t", [P, 4 * DM], F32)
    yr_t = SA("yr_t", [P, DM], BF16)
    yi_t = SA("yi_t", [P, DM], BF16)
    w_re = SA("w_re", [NST, DM], F32)
    w_im = SA("w_im", [NST, DM], F32)
    wreb = SA("wreb", [NST, DM], BF16)
    wimb = SA("wimb", [NST, DM], BF16)
    wtmp = SA("wtmp", [NST, 2 * DM], F32)
    ctmp = SA("ctmp", [NST, 2 * DM], F32)
    xnew_t = SA("xnew_t", [P, 2 * DM], F32)
    xn4_t = SA("xn4_t", [P, DM], BF16)

    psA = nc.alloc_psum_tensor("psA", [P, DM], F32)
    psB = nc.alloc_psum_tensor("psB", [P, DM], F32)
    psC = nc.alloc_psum_tensor("psC", [P, DM], F32)
    psD = nc.alloc_psum_tensor("psD", [P, DM], F32)

    sems = {}

    def mksem(name):
        s = _Sem(nc.alloc_semaphore(name))
        sems[name] = s
        return s

    s_x = mksem("s_x")
    s_st = mksem("s_st")
    s_rs = mksem("s_rs")
    s_rc = mksem("s_rc")
    s_ap = mksem("s_ap")
    s_ln = mksem("s_ln")
    s_fft = mksem("s_fft")
    s_pw = mksem("s_pw")
    s_yy = mksem("s_yy")
    s_wl = mksem("s_wl")
    s_wg = mksem("s_wg")
    s_ys = mksem("s_ys")
    s_cv = mksem("s_cv")
    s_xc = mksem("s_xc")
    s_cu = mksem("s_cu")
    s_l3 = mksem("s_l3")
    s_l4 = mksem("s_l4")
    s_tp = mksem("s_tp")
    s_t4 = mksem("s_t4")
    s_rr = mksem("s_rr")
    s_rv = mksem("s_rv")
    s_rt = mksem("s_rt")
    s_rc2 = mksem("s_rc2")
    s_cfd = mksem("s_cfd")
    s_wdma = mksem("s_wdma")
    s_cb = mksem("s_cb")
    s_xs = mksem("s_xs")
    s_ug = mksem("s_ug")
    s_sg = mksem("s_sg")
    s_hm = mksem("s_hm")
    s_dn = mksem("s_dn")
    s_ac = mksem("s_ac")
    s_am = mksem("s_am")
    s_c5 = mksem("s_c5")
    s_e5 = mksem("s_e5")
    s_xr = mksem("s_xr")
    s_res = mksem("s_res")
    s_out = mksem("s_out")

    prog = {k: [] for k in ("sync", "tensor", "vector", "scalar", "gpsimd")}

    def em(eng, fn):
        prog[eng].append(fn)

    def wait(eng, sem, v):
        em(eng, lambda e, s=sem, v=v: e.wait_ge(s.h, v))

    _vhist = []
    _haz_sites = {}
    _n_spacers = [0]

    def _v_hazard(reads):
        between = 0
        for writes, real in reversed(_vhist):
            if between >= 1:
                break
            if any(_olap(r, w) for r in reads for w in writes):
                return True
            if real:
                between += 1
        return False

    def _v_push(writes, real):
        _vhist.append((writes, real))
        if len(_vhist) > 6:
            _vhist.pop(0)

    def op(eng, fn, sem=None, n=1):
        if eng == "vector":
            rec = _Rec()
            try:
                fn(rec)
            except Exception as ex:
                print("RAW-checker: recorder failed:", ex)
                rec.reads = rec.writes = [(nm, 0, 1 << 20, 0, 1 << 40)
                                          for nm in ("x_t", "cols", "ctmp")]
            if _v_hazard(rec.reads):
                import traceback
                ln = traceback.extract_stack()[-2].lineno
                _haz_sites[ln] = _haz_sites.get(ln, 0) + 1
                _n_spacers[0] += 1
                em("vector", lambda e: e.tensor_copy(junkc[:, 0:1], junkc[:, 2:3]))
                _v_push([("junkc", 0, P, 0, 4)], True)
            _v_push(rec.writes,
                    bool(rec.names) and any(nm != "memset" for nm in rec.names))
        if sem is None:
            em(eng, lambda e, f=fn: f(e))
        else:
            sem.inc(n)
            em(eng, lambda e, f=fn, s=sem, n=n: f(e).then_inc(s.h, n))

    build.haz_sites = _haz_sites
    build.n_spacers = _n_spacers

    op("vector", lambda e: e.memset(junkc[:], 0.0))

    for dst, src in [
        (sc64[:, 0:1], lr), (sc64[:, 1:2], li), (pmask_sb[:], pmask),
        (gammab[0:1, 0:DM], gamma), (betab[0:1, 0:DM], beta),
        (gn_t[0:1, DM:2 * DM], dvec), (ones_colf[0:1, 0:P], cst["ones_row"]),
        (identf[:], cst["ident"]),
        (m_t[0:P, 4 * P:5 * P], cst["Fr"]), (m_t[0:P, 5 * P:6 * P], cst["Fi"]),
        (m_t[0:P, 6 * P:7 * P], cst["ICr"]), (m_t[0:P, 7 * P:8 * P], cst["ICi"]),
        (m_t[0:P, 0:P], cst["KFr"]), (m_t[0:P, P:2 * P], cst["KFi"]),
        (m_t[0:P, 2 * P:3 * P], cst["rev"]),
    ]:
        op("sync", lambda e, d=dst, s=src: e.dma_start(out=d, in_=s), s_x, 16)
    v_ld0 = s_x.v

    wait("scalar", s_x, v_ld0)
    op("scalar", lambda e: e.copy(frb[:], m_t[0:P, 4 * P:5 * P]))
    op("scalar", lambda e: e.copy(fib[:], m_t[0:P, 5 * P:6 * P]))
    op("scalar", lambda e: e.copy(icrb[:], m_t[0:P, 6 * P:7 * P]))
    op("scalar", lambda e: e.copy(icib[:], m_t[0:P, 7 * P:8 * P]))
    op("scalar", lambda e: e.copy(kfrb_c[:], m_t[0:P, 0:P]))
    op("scalar", lambda e: e.copy(kfib_c[:], m_t[0:P, P:2 * P]))
    op("scalar", lambda e: e.copy(revb[:], m_t[0:P, 2 * P:3 * P]))
    op("scalar", lambda e: e.copy(identb[:], identf[:]), s_ap)

    wait("tensor", s_x, v_ld0)
    for ps, row, off in [(psA, gammab, 0), (psB, betab, 0), (psC, gn_t, DM)]:
        for h in range(2):
            op("tensor", lambda e, ps=ps, row=row, h=h, off=off: e.matmul(
                ps[:, h * 512:(h + 1) * 512], ones_colf[0:1, :],
                row[0:1, off + h * 512:off + (h + 1) * 512], start=True, stop=True),
               s_rr if (ps is psC and h == 1) else None)
    wait("scalar", s_rr, s_rr.v)
    op("scalar", lambda e: e.copy(gammab[:], psA[:]))
    op("scalar", lambda e: e.copy(betab[:], psB[:]))
    op("scalar", lambda e: e.copy(gn_t[:, DM:2 * DM], psC[:]), s_ap)

    wait("scalar", s_x, v_ld0)
    op("scalar", lambda e: e.activation(sc64[:, 2:3], sc64[:, 0:1], AF.Exp))
    op("scalar", lambda e: e.activation(sc64[:, 3:4], sc64[:, 1:2], AF.Exp), s_ap)
    wait("vector", s_ap, s_ap.v)
    op("vector", lambda e: e.tensor_scalar(sc64[:, 14:15], sc64[:, 2:3], -1.0, None, ALU.mult), s_rv)
    wait("scalar", s_rv, s_rv.v)
    op("scalar", lambda e: e.activation(sc64[:, 6:7], sc64[:, 14:15], AF.Exp), s_ap)
    twopi = 2.0 * math.pi
    wait("vector", s_ap, s_ap.v)
    op("vector", lambda e: e.tensor_scalar(sc64[:, 7:8], sc64[:, 3:4], 1.0 / twopi,
                                           None, ALU.mult))
    op("vector", lambda e: e.tensor_copy(sci[:], sc64[:, 7:8]))
    op("vector", lambda e: e.tensor_copy(sc64[:, 10:11], sci[:]))
    op("vector", lambda e: e.scalar_tensor_tensor(sc64[:, 7:8], sc64[:, 10:11], -twopi,
                                                  sc64[:, 3:4], ALU.mult, ALU.add))
    op("vector", lambda e: e.tensor_single_scalar(sc64[:, 10:11], sc64[:, 7:8], math.pi,
                                                  ALU.is_gt))
    op("vector", lambda e: e.scalar_tensor_tensor(sc64[:, 7:8], sc64[:, 10:11], -twopi,
                                                  sc64[:, 7:8], ALU.mult, ALU.add))
    op("vector", lambda e: e.tensor_single_scalar(sc64[:, 10:11], sc64[:, 7:8], -math.pi,
                                                  ALU.is_lt))
    op("vector", lambda e: e.scalar_tensor_tensor(sc64[:, 7:8], sc64[:, 10:11], twopi,
                                                  sc64[:, 7:8], ALU.mult, ALU.add))
    op("vector", lambda e: e.tensor_scalar(sc64[:, 11:12], sc64[:, 7:8], -1.0, None,
                                           ALU.mult))
    op("vector", lambda e: e.tensor_tensor(sc64[:, 11:12], sc64[:, 11:12], sc64[:, 7:8],
                                           ALU.max))
    op("vector", lambda e: e.tensor_scalar(sc64[:, 11:12], sc64[:, 11:12], -1.0,
                                           math.pi / 2.0, ALU.mult, ALU.add), s_rv)
    wait("scalar", s_rv, s_rv.v)
    op("scalar", lambda e: e.activation(sc64[:, 9:10], sc64[:, 7:8], AF.Sin))
    op("scalar", lambda e: e.activation(sc64[:, 8:9], sc64[:, 11:12], AF.Sin), s_ap)
    wait("vector", s_ap, s_ap.v)
    op("vector", lambda e: e.tensor_mul(sc64[:, 4:5], sc64[:, 6:7], sc64[:, 8:9]))
    op("vector", lambda e: e.tensor_mul(sc64[:, 5:6], sc64[:, 6:7], sc64[:, 9:10]))

    op("vector", lambda e: e.memset(pw_re[:, 0:1], 1.0))
    op("vector", lambda e: e.memset(pw_im[:, 0:1], 0.0))
    op("vector", lambda e: e.tensor_copy(sc64[:, 12:13], sc64[:, 4:5]))
    op("vector", lambda e: e.tensor_copy(sc64[:, 13:14], sc64[:, 5:6]))

    def csq():
        op("vector", lambda e: e.tensor_mul(sc64[:, 14:15], sc64[:, 12:13], sc64[:, 12:13]))
        op("vector", lambda e: e.tensor_mul(sc64[:, 15:16], sc64[:, 13:14], sc64[:, 13:14]))
        op("vector", lambda e: e.tensor_mul(sc64[:, 13:14], sc64[:, 12:13], sc64[:, 13:14]))
        op("vector", lambda e: e.tensor_scalar(sc64[:, 13:14], sc64[:, 13:14], 2.0, None, ALU.mult))
        op("vector", lambda e: e.tensor_sub(sc64[:, 12:13], sc64[:, 14:15], sc64[:, 15:16]))

    b = 1
    while b < P:
        op("vector", lambda e, b=b: e.tensor_scalar(ctmp[0:NST, 0:b], pw_re[:, 0:b],
                                                    sc64[:, 12:13], None, ALU.mult))
        op("vector", lambda e, b=b: e.tensor_scalar(ctmp[0:NST, DM:DM + b], pw_im[:, 0:b],
                                                    sc64[:, 13:14], None, ALU.mult))
        op("vector", lambda e, b=b: e.tensor_sub(pw_re[:, b:2 * b], ctmp[0:NST, 0:b],
                                                 ctmp[0:NST, DM:DM + b]))
        op("vector", lambda e, b=b: e.tensor_scalar(ctmp[0:NST, 0:b], pw_re[:, 0:b],
                                                    sc64[:, 13:14], None, ALU.mult))
        op("vector", lambda e, b=b: e.tensor_scalar(ctmp[0:NST, DM:DM + b], pw_im[:, 0:b],
                                                    sc64[:, 12:13], None, ALU.mult))
        op("vector", lambda e, b=b: e.tensor_add(pw_im[:, b:2 * b], ctmp[0:NST, 0:b],
                                                 ctmp[0:NST, DM:DM + b]))
        csq()
        b *= 2
    op("vector", lambda e: e.tensor_copy(aq_cols[:, 0:1], sc64[:, 12:13]))
    op("vector", lambda e: e.tensor_copy(aq_cols[:, 1:2], sc64[:, 13:14]))
    csq()
    csq()
    op("vector", lambda e: e.tensor_copy(aq_cols[:, 2:3], sc64[:, 12:13]))
    op("vector", lambda e: e.tensor_copy(aq_cols[:, 3:4], sc64[:, 13:14]))
    op("vector", lambda e: e.memset(ones_mask[:], 1.0))

    op("vector", lambda e: e.memset(powtab[:, 0:1], 1.0))
    op("vector", lambda e: e.memset(powtab[:, 1:2], 0.0))
    op("vector", lambda e: e.tensor_copy(powtab[:, 2:3], aq_cols[:, 0:1]))
    op("vector", lambda e: e.tensor_copy(powtab[:, 3:4], aq_cols[:, 1:2]))
    op("vector", lambda e: e.tensor_mul(sc64[:, 14:15], aq_cols[:, 0:1], aq_cols[:, 0:1]))
    op("vector", lambda e: e.tensor_mul(sc64[:, 15:16], aq_cols[:, 1:2], aq_cols[:, 1:2]))
    op("vector", lambda e: e.tensor_sub(powtab[:, 4:5], sc64[:, 14:15], sc64[:, 15:16]))
    op("vector", lambda e: e.tensor_mul(sc64[:, 14:15], aq_cols[:, 0:1], aq_cols[:, 1:2]))
    op("vector", lambda e: e.tensor_scalar(powtab[:, 5:6], sc64[:, 14:15], 2.0, None, ALU.mult))
    op("vector", lambda e: e.tensor_mul(sc64[:, 14:15], powtab[:, 4:5], aq_cols[:, 0:1]))
    op("vector", lambda e: e.tensor_mul(sc64[:, 15:16], powtab[:, 5:6], aq_cols[:, 1:2]))
    op("vector", lambda e: e.tensor_sub(powtab[:, 6:7], sc64[:, 14:15], sc64[:, 15:16]))
    op("vector", lambda e: e.tensor_mul(sc64[:, 14:15], powtab[:, 4:5], aq_cols[:, 1:2]))
    op("vector", lambda e: e.tensor_mul(sc64[:, 15:16], powtab[:, 5:6], aq_cols[:, 0:1]))
    op("vector", lambda e: e.tensor_add(powtab[:, 7:8], sc64[:, 14:15], sc64[:, 15:16]))

    op("vector", lambda e: e.tensor_scalar(sc64[:, 14:15], sc64[:, 4:5], -1.0, None, ALU.add))
    op("vector", lambda e: e.tensor_mul(sc64[:, 15:16], sc64[:, 2:3], sc64[:, 2:3]))
    op("vector", lambda e: e.tensor_mul(sc64[:, 10:11], sc64[:, 3:4], sc64[:, 3:4]))
    op("vector", lambda e: e.tensor_add(sc64[:, 15:16], sc64[:, 15:16], sc64[:, 10:11]))
    op("vector", lambda e: e.reciprocal(sc64[:, 15:16], sc64[:, 15:16]))
    op("vector", lambda e: e.tensor_mul(sc64[:, 10:11], sc64[:, 14:15], sc64[:, 2:3]))
    op("vector", lambda e: e.tensor_mul(sc64[:, 11:12], sc64[:, 5:6], sc64[:, 3:4]))
    op("vector", lambda e: e.tensor_sub(sc64[:, 11:12], sc64[:, 11:12], sc64[:, 10:11]))
    op("vector", lambda e: e.tensor_mul(sc64[:, 6:7], sc64[:, 11:12], sc64[:, 15:16]))
    op("vector", lambda e: e.tensor_mul(sc64[:, 10:11], sc64[:, 5:6], sc64[:, 2:3]))
    op("vector", lambda e: e.tensor_mul(sc64[:, 11:12], sc64[:, 14:15], sc64[:, 3:4]))
    op("vector", lambda e: e.tensor_add(sc64[:, 10:11], sc64[:, 10:11], sc64[:, 11:12]))
    op("vector", lambda e: e.tensor_mul(sc64[:, 10:11], sc64[:, 10:11], sc64[:, 15:16]))
    op("vector", lambda e: e.tensor_scalar(sc64[:, 7:8], sc64[:, 10:11], -1.0, None, ALU.mult),
       s_rv)

    cr_r = cr.rearrange("(t p) n -> t p n", p=P)
    ci_r = ci.rearrange("(t p) n -> t p n", p=P)
    for half, src_r in [(0, cr_r), (1, ci_r)]:
        for t8 in range(8):
            buf = t8 % 2
            if half or t8 >= 2:
                wait("sync", s_tp, s_tp.v)
            op("sync", lambda e, src_r=src_r, t8=t8, buf=buf: e.dma_start(
                out=x_t[0:P, buf * DM:buf * DM + NST], in_=src_r[t8]), s_x, 16)
            wait("tensor", s_x, s_x.v)
            if half or t8:
                wait("tensor", s_rv, s_rv.v)
            op("tensor", lambda e, buf=buf: e.transpose(
                psD[0:NST, 0:P], x_t[0:P, buf * DM:buf * DM + NST], identf[:]), s_tp)
            wait("vector", s_tp, s_tp.v)
            op("vector", lambda e, half=half, t8=t8: e.tensor_copy(
                wtmp[0:NST, half * DM + t8 * P:half * DM + (t8 + 1) * P],
                psD[0:NST, 0:P]), s_rv)
    op("vector", lambda e: e.tensor_scalar(ctmp[0:NST, 0:DM], wtmp[0:NST, 0:DM],
                                           sc64[:, 6:7], None, ALU.mult))
    op("vector", lambda e: e.tensor_scalar(ctmp[0:NST, DM:2 * DM], wtmp[0:NST, DM:2 * DM],
                                           sc64[:, 7:8], None, ALU.mult))
    op("vector", lambda e: e.tensor_sub(cs_re[:], ctmp[0:NST, 0:DM], ctmp[0:NST, DM:2 * DM]))
    op("vector", lambda e: e.tensor_scalar(ctmp[0:NST, 0:DM], wtmp[0:NST, 0:DM],
                                           sc64[:, 7:8], None, ALU.mult))
    op("vector", lambda e: e.tensor_scalar(ctmp[0:NST, DM:2 * DM], wtmp[0:NST, DM:2 * DM],
                                           sc64[:, 6:7], None, ALU.mult))
    op("vector", lambda e: e.tensor_add(cs_im[:], ctmp[0:NST, 0:DM], ctmp[0:NST, DM:2 * DM]),
       s_rv)
    wait("scalar", s_rv, s_rv.v)
    op("scalar", lambda e: e.copy(csreb[:], cs_re[:]))
    op("scalar", lambda e: e.copy(csimb[:], cs_im[:]))

    wait("scalar", s_rv, s_rv.v)
    op("scalar", lambda e: e.copy(pkre[:], pw_re[:]), s_ap)
    op("vector", lambda e: e.tensor_scalar(pkim[:], pw_im[:], -1.0, None, ALU.mult), s_rv)
    wait("scalar", s_rv, s_rv.v)
    wait("tensor", s_rv, s_rv.v)

    for m in range(4):
        if m > 0:
            wait("vector", s_ap, s_ap.v)
        op("vector", lambda e, m=m: e.tensor_scalar(ctmp[0:NST, 0:P], pw_re[:],
                                                    powtab[:, 2 * m:2 * m + 1], None, ALU.mult))
        op("vector", lambda e, m=m: e.tensor_scalar(ctmp[0:NST, P:2 * P], pw_im[:],
                                                    powtab[:, 2 * m + 1:2 * m + 2], None, ALU.mult))
        op("vector", lambda e: e.tensor_sub(ctmp[0:NST, 2 * P:3 * P], ctmp[0:NST, 0:P],
                                            ctmp[0:NST, P:2 * P]))
        op("vector", lambda e, m=m: e.tensor_scalar(ctmp[0:NST, 0:P], pw_re[:],
                                                    powtab[:, 2 * m + 1:2 * m + 2], None, ALU.mult))
        op("vector", lambda e, m=m: e.tensor_scalar(ctmp[0:NST, P:2 * P], pw_im[:],
                                                    powtab[:, 2 * m:2 * m + 1], None, ALU.mult))
        op("vector", lambda e: e.tensor_add(ctmp[0:NST, 3 * P:4 * P], ctmp[0:NST, 0:P],
                                            ctmp[0:NST, P:2 * P]), s_rv)
        wait("scalar", s_rv, s_rv.v)
        op("scalar", lambda e: e.copy(xn2b_t[0:NST, 0:P], ctmp[0:NST, 2 * P:3 * P]))
        op("scalar", lambda e: e.copy(xn2b_t[0:NST, P:2 * P], ctmp[0:NST, 3 * P:4 * P]), s_ap)
        wait("tensor", s_ap, s_ap.v)
        if m > 0:
            wait("tensor", s_t4, s_t4.v)
        op("tensor", lambda e: e.transpose(psA.bitcast(BF16)[0:P, 0:NST],
                                           xn2b_t[0:NST, 0:P], identb[0:NST, 0:NST]))
        op("tensor", lambda e: e.transpose(psB.bitcast(BF16)[0:P, 0:NST],
                                           xn2b_t[0:NST, P:2 * P], identb[0:NST, 0:NST]),
           s_tp)
        wait("scalar", s_tp, s_tp.v)
        op("scalar", lambda e: e.copy(xn4_t[0:P, 0:NST], psA.bitcast(BF16)[0:P, 0:NST]))
        op("scalar", lambda e: e.copy(xn4_t[0:P, NST:2 * NST],
                                      psB.bitcast(BF16)[0:P, 0:NST]), s_ap)
        wait("tensor", s_ap, s_ap.v)
        op("tensor", lambda e: e.matmul(psA[0:P, 0:NST], revb[:], xn4_t[0:P, 0:NST],
                                        start=True, stop=True))
        op("tensor", lambda e: e.matmul(psB[0:P, 0:NST], revb[:], xn4_t[0:P, NST:2 * NST],
                                        start=True, stop=True), s_rr)
        wait("scalar", s_rr, s_rr.v)
        slots = [0, 4] if m == 0 else [4 - m]
        for sl in slots:
            op("scalar", lambda e, sl=sl: e.copy(asre[:, sl * NST:(sl + 1) * NST],
                                                 psA[0:P, 0:NST]))
            op("scalar", lambda e, sl=sl: e.copy(asim[:, sl * NST:(sl + 1) * NST],
                                                 psB[0:P, 0:NST]),
               s_t4 if sl == slots[-1] else None)
    wait("vector", s_t4, s_t4.v)

    if DEBUG == 7:
        wait("gpsimd", s_t4, s_t4.v)
        op("gpsimd", lambda e: e.dma_start(out=dbg_r[0][0:P, 0:320], in_=asre[:]), s_wdma, 16)
        op("gpsimd", lambda e: e.dma_start(out=dbg_r[1][0:P, 0:320], in_=asim[:]), s_wdma, 16)
        wait("gpsimd", s_rv, s_rv.v)
        op("gpsimd", lambda e: e.dma_start(out=dbg_r[2][0:NST, 0:P], in_=ctmp[0:NST, 2 * P:3 * P]), s_wdma, 16)
        op("gpsimd", lambda e: e.dma_start(out=dbg_r[2][64:P, 0:P], in_=ctmp[0:NST, 3 * P:4 * P]), s_wdma, 16)
        op("gpsimd", lambda e: e.dma_start(out=dbg_r[3][0:NST, 0:8], in_=powtab[:]), s_wdma, 16)
    wait("vector", s_ap, s_ap.v)
    op("vector", lambda e: e.tensor_scalar(ctmp[0:NST, 0:P], pw_re[:], sc64[:, 4:5], None, ALU.mult))
    op("vector", lambda e: e.tensor_scalar(ctmp[0:NST, P:2 * P], pw_im[:], sc64[:, 5:6], None, ALU.mult))
    op("vector", lambda e: e.tensor_sub(ctmp[0:NST, 2 * P:3 * P], ctmp[0:NST, 0:P],
                                        ctmp[0:NST, P:2 * P]))
    op("vector", lambda e: e.tensor_scalar(ctmp[0:NST, 0:P], pw_re[:], sc64[:, 5:6], None, ALU.mult))
    op("vector", lambda e: e.tensor_scalar(ctmp[0:NST, P:2 * P], pw_im[:], sc64[:, 4:5], None, ALU.mult))
    op("vector", lambda e: e.tensor_add(ctmp[0:NST, 3 * P:4 * P], ctmp[0:NST, 0:P],
                                        ctmp[0:NST, P:2 * P]), s_rv)
    wait("scalar", s_rv, s_rv.v)
    op("scalar", lambda e: e.copy(ppre[:], ctmp[0:NST, 2 * P:3 * P]), s_ap)
    op("vector", lambda e: e.tensor_scalar(ppim[:], ctmp[0:NST, 3 * P:4 * P], -1.0,
                                           None, ALU.mult), s_rv)

    wait("tensor", s_ap, s_ap.v)
    for h in range(2):
        op("tensor", lambda e, h=h: e.matmul(psA[:, h * 512:(h + 1) * 512], pkre[:],
                                             csreb[:, h * 512:(h + 1) * 512],
                                             start=True, stop=False))
        op("tensor", lambda e, h=h: e.matmul(psA[:, h * 512:(h + 1) * 512], pkim[:],
                                             csimb[:, h * 512:(h + 1) * 512],
                                             start=False, stop=True),
           s_rr if h == 1 else None)
    wait("scalar", s_rr, s_rr.v)
    op("scalar", lambda e: e.copy(kb[:], psA[:]), s_ap)
    wait("tensor", s_ap, s_ap.v)
    for h in range(2):
        op("tensor", lambda e, h=h: e.matmul(psA[:, h * 512:(h + 1) * 512], kfrb_c[:],
                                             kb[:, h * 512:(h + 1) * 512],
                                             start=True, stop=True))
        op("tensor", lambda e, h=h: e.matmul(psB[:, h * 512:(h + 1) * 512], kfib_c[:],
                                             kb[:, h * 512:(h + 1) * 512],
                                             start=True, stop=True),
           s_rr if h == 1 else None)
    wait("vector", s_rr, s_rr.v)
    op("vector", lambda e: e.tensor_add(kfr[:], psA[:], gn_t[:, DM:2 * DM]))
    op("vector", lambda e: e.tensor_copy(kfi[:], psB[:]))
    op("vector", lambda e: e.memset(kfi[0:1, 0:DM], 0.0))
    op("vector", lambda e: e.tensor_copy(kfr2[:], kfr[:]))
    op("vector", lambda e: e.tensor_add(kfr2[0:1, 0:DM], psB[0:1, 0:DM], gn_t[0:1, DM:2 * DM]))
    op("vector", lambda e: e.memset(carry_re[:], 0.0))
    op("vector", lambda e: e.memset(carry_im[:], 0.0), s_cu)
    if DEBUG == 5:
        wait("sync", s_cu, s_cu.v)
        op("sync", lambda e: e.dma_start(out=dbg_r[0][0:NST, 0:P], in_=pw_re[:]), s_res, 16)
        op("sync", lambda e: e.dma_start(out=dbg_r[0][64:128, 0:P], in_=pw_im[:]), s_res, 16)
        op("sync", lambda e: e.dma_start(out=dbg_r[1][0:NST, 0:DM], in_=cs_re[:]), s_res, 16)
        op("sync", lambda e: e.dma_start(out=dbg_r[2][0:NST, 0:DM], in_=cs_im[:]), s_res, 16)
        op("sync", lambda e: e.dma_start(out=dbg_r[3][0:NST, 0:16], in_=sc64[:]), s_res, 16)
        op("sync", lambda e: e.dma_start(out=dbg_r[3][64:128, 0:4], in_=aq_cols[:]), s_res, 16)
    if DEBUG == 4:
        wait("sync", s_cu, s_cu.v)
        op("sync", lambda e: e.dma_start(out=dbg_r[0], in_=kfr[:]), s_res, 16)
        op("sync", lambda e: e.dma_start(out=dbg_r[1], in_=kfi[:]), s_res, 16)
        op("sync", lambda e: e.dma_start(out=dbg_r[2], in_=kfr2[:]), s_res, 16)
        wait("gpsimd", s_cu, s_cu.v)
        op("gpsimd", lambda e: e.dma_start(out=dbg_r[3], in_=kb[:]), s_wdma, 16)

    xo_r = xo.rearrange("(t p) d -> t p d", p=P)
    xp_r = xp.rearrange("(t p) d -> t p d", p=P)
    xres_r = xres.rearrange("(t p) d -> t p d", p=P)
    out_r = out.rearrange("(t p) d -> t p d", p=P)


    def tree_steps(eng, buf, w0, sem=None):
        w = w0
        while w > 1:
            h = w // 2
            op(eng, lambda e, h=h, w=w, buf=buf: e.tensor_tensor(
                buf[:, 0:h], buf[:, 0:h], buf[:, h:w], ALU.add),
               sem if h == 1 else None)
            w = h

    _lncnt = [0]

    def ln_stats(src_fn, mu_col, var_col, tree_buf, sq_buf, gp_tree=False):
        _lncnt[0] += 1
        """Emit mean/var(+eps) of src [128 x 1024] into cols columns.
        ACT squares; DVE does two tensor_reduce sums + the stats tail.
        Ends with var+eps in var_col (inc s_st)."""
        op("scalar", lambda e: e.activation(sq_buf[:], src_fn(), AF.Square), s_rs)
        op("vector", lambda e: e.tensor_reduce(stat[:, 0:1], src_fn(),
                                               axis=AXX, op=ALU.add))
        wait("vector", s_rs, s_rs.v)
        op("vector", lambda e: e.tensor_reduce(cols[:, 14:15], sq_buf[:],
                                               axis=AXX, op=ALU.add))
        op("vector", lambda e: e.tensor_scalar(mu_col, stat[:, 0:1], 1.0 / DM,
                                               None, ALU.mult))
        op("vector", lambda e: e.tensor_scalar(cols[:, 14:15], cols[:, 14:15],
                                               1.0 / DM, EPS, ALU.mult, ALU.add))
        op("vector", lambda e: e.tensor_mul(cols[:, 15:16], mu_col, mu_col))
        op("vector", lambda e: e.tensor_sub(var_col, cols[:, 14:15], cols[:, 15:16]),
           s_st)

    def ln_pipeline(src_slice, dst_slice):
        wait("vector", s_x, s_x.v)
        wait("scalar", s_x, s_x.v)
        ln_stats(lambda: src_slice, cols[:, 0:1], cols[:, 1:2], None,
                 sq_t[:, 0:DM], gp_tree=False)
        wait("scalar", s_st, s_st.v)
        op("scalar", lambda e: e.activation(cols[:, 2:3], cols[:, 1:2], AF.Sqrt), s_rs)
        wait("vector", s_rs, s_rs.v)
        op("vector", lambda e: e.reciprocal(cols[:, 3:4], cols[:, 2:3]))
        op("vector", lambda e: e.tensor_scalar(cols[:, 4:5], cols[:, 0:1], cols[:, 3:4], -1.0,
                                               ALU.mult, ALU.mult), s_rc)
        wait("scalar", s_rc, s_rc.v)
        op("scalar", lambda e: e.activation(xn1_t[:, 0:DM], src_slice, AF.Identity,
                                            bias=cols[:, 4:5], scale=cols[:, 3:4]), s_ap)
        wait("vector", s_ap, s_ap.v)
        ln_stats(lambda: xn1_t[:, 0:DM], cols[:, 5:6], cols[:, 6:7], None,
                 sq_t[:, 0:DM], gp_tree=False)
        wait("scalar", s_st, s_st.v)
        op("scalar", lambda e: e.activation(cols[:, 7:8], cols[:, 6:7], AF.Sqrt), s_rs)
        wait("vector", s_rs, s_rs.v)
        op("vector", lambda e: e.reciprocal(cols[:, 8:9], cols[:, 7:8]))
        op("vector", lambda e: e.tensor_scalar(cols[:, 9:10], cols[:, 5:6], cols[:, 8:9], -1.0,
                                               ALU.mult, ALU.mult), s_rc)
        wait("scalar", s_rc, s_rc.v)
        wait("scalar", s_ln, s_ln.v)
        op("scalar", lambda e: e.activation(xn2a_t[:, 0:DM], xn1_t[:, 0:DM], AF.Identity,
                                            bias=cols[:, 9:10], scale=cols[:, 8:9]), s_ap)
        wait("gpsimd", s_ap, s_ap.v)
        op("gpsimd", lambda e: e.tensor_mul(gn_t[:, 0:DM], xn2a_t[:, 0:DM], gammab[:]), s_wg)
        wait("vector", s_wg, s_wg.v)
        wait("vector", s_tp, s_tp.v)
        if s_xc.v >= 1:
            wait("vector", s_xc, s_xc.v - 1)
        op("vector", lambda e, d=dst_slice: e.tensor_tensor(d, gn_t[:, 0:DM], betab[:],
                                                            ALU.add), s_ln)

    def carry_update(mask_col, re_col, im_col):
        wait("vector", s_xc, s_xc.v)
        op("vector", lambda e, ic=im_col: e.tensor_scalar(ctmp[0:NST, 0:DM], carry_im[:],
                                                          ic, None, ALU.mult))
        op("vector", lambda e, rc=re_col: e.scalar_tensor_tensor(
            ctmp[0:NST, DM:2 * DM], carry_re[:], rc, ctmp[0:NST, 0:DM],
            ALU.mult, ALU.subtract))
        op("vector", lambda e, ic=im_col: e.tensor_scalar(ctmp[0:NST, 0:DM], carry_re[:],
                                                          ic, None, ALU.mult))
        op("vector", lambda e, rc=re_col: e.scalar_tensor_tensor(
            carry_im[:], carry_im[:], rc, ctmp[0:NST, 0:DM], ALU.mult, ALU.add))
        op("vector", lambda e, m=mask_col: e.scalar_tensor_tensor(
            carry_re[:], psA[0:NST, 0:DM], m, ctmp[0:NST, DM:2 * DM], ALU.mult, ALU.add))
        op("vector", lambda e, m=mask_col: e.scalar_tensor_tensor(
            carry_im[:], psB[0:NST, 0:DM], m, carry_im[:], ALU.mult, ALU.add), s_cu)

    wait("tensor", s_cu, s_cu.v)

    ap_ln1_pair = {}
    xc_after_pair = {}

    def ln_pair(base, pidx):
        src2 = x_t[:, base:base + 2 * DM]
        src2v = x_t[:, base:base + 2 * DM].rearrange("p (b d) -> p b d", d=DM)
        sq2 = sq_t[:, 0:2 * DM]
        sq2v = sq_t[:, 0:2 * DM].rearrange("p (b d) -> p b d", d=DM)
        wait("scalar", s_x, s_x.v)
        wait("vector", s_x, s_x.v)

        def stats2(sfull, sview):
            op("scalar", lambda e: e.activation(sq2, sfull, AF.Square), s_rs)
            op("vector", lambda e: e.tensor_reduce(stat[:, 0:2], sview, axis=AXX,
                                                   op=ALU.add, negate=True))
            wait("vector", s_rs, s_rs.v)
            op("vector", lambda e: e.tensor_reduce(stat[:, 2:4], sq2v, axis=AXX,
                                                   op=ALU.add))
            op("vector", lambda e: e.tensor_scalar(stat[:, 4:6], stat[:, 0:2],
                                                   1.0 / DM, None, ALU.mult))
            op("vector", lambda e: e.tensor_scalar(stat[:, 2:4], stat[:, 2:4],
                                                   1.0 / DM, EPS, ALU.mult, ALU.add))
            op("vector", lambda e: e.tensor_mul(stat[:, 6:8], stat[:, 4:6],
                                                stat[:, 4:6]))
            op("vector", lambda e: e.tensor_sub(stat[:, 8:10], stat[:, 2:4],
                                                stat[:, 6:8]), s_st)
            wait("scalar", s_st, s_st.v)
            op("scalar", lambda e: e.activation(stat[:, 10:12], stat[:, 8:10],
                                                AF.Sqrt), s_rs)
            wait("vector", s_rs, s_rs.v)
            op("vector", lambda e: e.reciprocal(stat[:, 12:14], stat[:, 10:12]))
            op("vector", lambda e: e.tensor_mul(stat[:, 14:16], stat[:, 4:6],
                                                stat[:, 12:14]), s_rc)
            wait("scalar", s_rc, s_rc.v)

        stats2(src2, src2v)
        for j in range(2):
            op("scalar", lambda e, j=j: e.activation(
                xn1_t[:, j * DM:(j + 1) * DM],
                x_t[:, base + j * DM:base + (j + 1) * DM],
                AF.Identity, bias=stat[:, 14 + j:15 + j],
                scale=stat[:, 12 + j:13 + j]), s_ap if j == 1 else None)
        ap_ln1_pair[pidx] = s_ap.v
        xn12 = xn1_t[:, 0:2 * DM]
        xn12v = xn1_t[:, 0:2 * DM].rearrange("p (b d) -> p b d", d=DM)
        wait("vector", s_ap, s_ap.v)
        stats2(xn12, xn12v)
        wait("scalar", s_ln, s_ln.v)
        for j in range(2):
            op("scalar", lambda e, j=j: e.activation(
                xn2a_t[:, j * DM:(j + 1) * DM], xn1_t[:, j * DM:(j + 1) * DM],
                AF.Identity, bias=stat[:, 14 + j:15 + j],
                scale=stat[:, 12 + j:13 + j]), s_ap if j == 1 else None)
        wait("gpsimd", s_ap, s_ap.v)
        for j in range(2):
            op("gpsimd", lambda e, j=j: e.tensor_mul(
                gn_t[:, j * DM:(j + 1) * DM], xn2a_t[:, j * DM:(j + 1) * DM],
                gammab[:]), s_wg if j == 1 else None)
        wait("vector", s_wg, s_wg.v)
        wait("vector", s_tp, s_tp.v)
        if pidx >= 2:
            wait("vector", s_xc, xc_after_pair[pidx - 2])
        for j in range(2):
            op("vector", lambda e, j=j: e.tensor_tensor(
                xn2b_t[:, base + j * DM:base + (j + 1) * DM],
                gn_t[:, j * DM:(j + 1) * DM], betab[:], ALU.add), s_ln)

    for g in range(PREG):
        for half in range(2):
            pidx = g * 2 + half
            base = (pidx % 2) * 2 * DM
            if pidx == 0:
                wait("sync", s_tp, s_tp.v)
            if pidx >= 2:
                wait("sync", s_ap, ap_ln1_pair[pidx - 2])
            for j in range(2):
                op("sync", lambda e, ch=pidx * 2 + j, base=base, j=j: e.dma_start(
                    out=x_t[:, base + j * DM:base + (j + 1) * DM],
                    in_=xp_r[ch]), s_x, 16)
            ln_pair(base, pidx)
            wait("tensor", s_ln, s_ln.v)
            if half == 0:
                wait("tensor", s_cu, s_cu.v)
            for j in range(2):
                k = half * 2 + j
                for h in range(2):
                    op("tensor", lambda e, k=k, h=h, base=base, j=j: e.matmul(
                        psA[0:NST, h * 512:(h + 1) * 512],
                        asre[:, (1 + k) * NST:(2 + k) * NST],
                        xn2b_t[:, base + j * DM + h * 512:
                               base + j * DM + (h + 1) * 512],
                        start=(k == 0), stop=(k == GSUB - 1)))
                    op("tensor", lambda e, k=k, h=h, base=base, j=j: e.matmul(
                        psB[0:NST, h * 512:(h + 1) * 512],
                        asim[:, (1 + k) * NST:(2 + k) * NST],
                        xn2b_t[:, base + j * DM + h * 512:
                               base + j * DM + (h + 1) * 512],
                        start=(k == 0), stop=(k == GSUB - 1)),
                       s_xc if h == 1 else None)
            xc_after_pair[pidx] = s_xc.v
        carry_update(pmask_sb[:, g:g + 1], aq_cols[:, 2:3], aq_cols[:, 3:4])

    for i in range(NCH):
        buf = i % 2
        if True:
            if i >= 2:
                wait("sync", s_l3, s_l3.v)
            else:
                wait("sync", s_ap, s_ap.v)
        op("sync", lambda e, i=i, buf=buf: e.dma_start(
            out=x_t[:, buf * DM:(buf + 1) * DM], in_=xo_r[i]), s_x, 16)
        ln_pipeline(x_t[:, buf * DM:(buf + 1) * DM],
                    xn2b_t[:, buf * DM:(buf + 1) * DM])
        if DEBUG == 1:
            op("gpsimd", lambda e, i=i, buf=buf: e.dma_start(
                out=dbg_r[i], in_=xn2b_t[:, buf * DM:(buf + 1) * DM]), s_wdma, 16)
        if DEBUG == 12 and i == 1:
            wait("sync", s_st, s_st.v)
            op("sync", lambda e: e.dma_start(out=dbg_r[0], in_=sq_t[:, 0:DM]), s_res, 16)
            op("sync", lambda e: e.dma_start(out=dbg_r[1][0:P, 0:512], in_=tr_t[:]), s_res, 16)
        if DEBUG == 8 and i < 8:
            wait("sync", s_ap, s_ap.v)
            op("sync", lambda e, i=i: e.dma_start(out=dbg_r[2 * i], in_=xn1_t[:, 0:DM]), s_res, 16)
            op("sync", lambda e, i=i: e.dma_start(out=dbg_r[2 * i + 1][0:P, 0:16],
                                                  in_=cols[:]), s_res, 16)
        if DEBUG == 9 and i < 8:
            wait("sync", s_x, s_x.v)
            op("sync", lambda e, i=i, buf=buf: e.dma_start(
                out=dbg_r[i], in_=x_t[:, buf * DM:(buf + 1) * DM]), s_res, 16)
        wait("gpsimd", s_cu, s_cu.v)
        op("gpsimd", lambda e: e.tensor_mul(wtmp[0:NST, 0:DM], cs_re[:], carry_re[:]))
        op("gpsimd", lambda e: e.tensor_mul(wtmp[0:NST, DM:2 * DM], cs_im[:], carry_im[:]))
        op("gpsimd", lambda e: e.tensor_sub(w_re[:], wtmp[0:NST, 0:DM], wtmp[0:NST, DM:2 * DM]))
        op("gpsimd", lambda e: e.tensor_mul(wtmp[0:NST, 0:DM], cs_re[:], carry_im[:]))
        op("gpsimd", lambda e: e.tensor_mul(wtmp[0:NST, DM:2 * DM], cs_im[:], carry_re[:]))
        op("gpsimd", lambda e: e.tensor_add(w_im[:], wtmp[0:NST, 0:DM],
                                            wtmp[0:NST, DM:2 * DM]), s_wg)
        wait("scalar", s_wg, s_wg.v)
        op("scalar", lambda e: e.copy(wreb[:], w_re[:]))
        op("scalar", lambda e: e.copy(wimb[:], w_im[:]), s_wl)
        wait("tensor", s_ln, s_ln.v)
        wait("tensor", s_cu, s_cu.v)
        if i > 0:
            wait("tensor", s_pw, s_pw.v)
        for h in range(2):
            op("tensor", lambda e, h=h, buf=buf: e.matmul(
                psA[:, h * 512:(h + 1) * 512], frb[:],
                xn2b_t[:, buf * DM + h * 512:buf * DM + (h + 1) * 512],
                start=True, stop=True))
            op("tensor", lambda e, h=h, buf=buf: e.matmul(
                psB[:, h * 512:(h + 1) * 512], fib[:],
                xn2b_t[:, buf * DM + h * 512:buf * DM + (h + 1) * 512],
                start=True, stop=True),
               s_fft if h == 1 else None)
        wait("vector", s_fft, s_fft.v)
        op("vector", lambda e: e.tensor_mul(m_t[:, 0:DM], psA[:], kfr[:]))
        op("vector", lambda e: e.tensor_mul(m_t[:, DM:2 * DM], psB[:], kfi[:]))
        op("vector", lambda e: e.tensor_mul(m_t[:, 2 * DM:3 * DM], psA[:], kfi[:]))
        op("vector", lambda e: e.tensor_mul(m_t[:, 3 * DM:4 * DM], psB[:], kfr2[:]), s_pw)
        wait("gpsimd", s_pw, s_pw.v)
        op("gpsimd", lambda e: e.tensor_sub(m_t[:, 0:DM], m_t[:, 0:DM], m_t[:, DM:2 * DM]))
        op("gpsimd", lambda e: e.tensor_add(m_t[:, 2 * DM:3 * DM], m_t[:, 2 * DM:3 * DM],
                                            m_t[:, 3 * DM:4 * DM]), s_yy)
        wait("scalar", s_yy, s_yy.v)
        op("scalar", lambda e: e.copy(yr_t[:], m_t[:, 0:DM]))
        op("scalar", lambda e: e.copy(yi_t[:], m_t[:, 2 * DM:3 * DM]), s_ys)
        wait("tensor", s_ys, s_ys.v)
        wait("tensor", s_wl, s_wl.v)
        if i > 0:
            wait("tensor", s_l3, s_l3.v)
        for h in range(2):
            op("tensor", lambda e, h=h: e.matmul(psD[:, h * 512:(h + 1) * 512], icrb[:],
                                                 yr_t[:, h * 512:(h + 1) * 512],
                                                 start=True, stop=False))
            op("tensor", lambda e, h=h: e.matmul(psD[:, h * 512:(h + 1) * 512], icib[:],
                                                 yi_t[:, h * 512:(h + 1) * 512],
                                                 start=False, stop=False))
            op("tensor", lambda e, h=h: e.matmul(psD[:, h * 512:(h + 1) * 512], ppre[:],
                                                 wreb[:, h * 512:(h + 1) * 512],
                                                 start=False, stop=False))
            op("tensor", lambda e, h=h: e.matmul(psD[:, h * 512:(h + 1) * 512], ppim[:],
                                                 wimb[:, h * 512:(h + 1) * 512],
                                                 start=False, stop=True),
               s_cv if h == 1 else None)
        wait("tensor", s_pw, s_pw.v)
        for h in range(2):
            op("tensor", lambda e, h=h, buf=buf: e.matmul(
                psA[0:NST, h * 512:(h + 1) * 512], asre[:, 0:NST],
                xn2b_t[:, buf * DM + h * 512:buf * DM + (h + 1) * 512],
                start=True, stop=True))
            op("tensor", lambda e, h=h, buf=buf: e.matmul(
                psB[0:NST, h * 512:(h + 1) * 512], asim[:, 0:NST],
                xn2b_t[:, buf * DM + h * 512:buf * DM + (h + 1) * 512],
                start=True, stop=True),
               s_xc if h == 1 else None)
        carry_update(ones_mask[:, 0:1], aq_cols[:, 0:1], aq_cols[:, 1:2])
        if DEBUG == 2:
            wait("vector", s_cv, s_cv.v)
            op("vector", lambda e: e.tensor_copy(gn_t[:, 0:DM], psD[:]), s_rv)
            wait("sync", s_rv, s_rv.v)
            op("sync", lambda e, i=i: e.dma_start(out=dbg_r[i], in_=gn_t[:, 0:DM]), s_res, 16)
            wait("vector", s_res, s_res.v)
        wait("vector", s_cv, s_cv.v)
        op("vector", lambda e: e.tensor_copy(gn_t[:, 0:DM], psD[:]), s_rv)
        wait("scalar", s_rv, s_rv.v)
        ln_stats(lambda: gn_t[:, 0:DM], cols[:, 10:11], cols[:, 11:12], None,
                 sq_t[:, 0:DM], gp_tree=False)
        wait("scalar", s_st, s_st.v)
        op("scalar", lambda e: e.activation(cols[:, 12:13], cols[:, 11:12], AF.Sqrt), s_rs)
        wait("vector", s_rs, s_rs.v)
        if i >= 2:
            wait("vector", s_res, s_res.v - 16)
        op("vector", lambda e: e.reciprocal(cols[:, 13:14], cols[:, 12:13]))
        op("vector", lambda e: e.tensor_scalar(xe_t[:], gn_t[:, 0:DM], cols[:, 10:11],
                                               cols[:, 13:14], ALU.subtract, ALU.mult))
        op("vector", lambda e, buf=buf: e.tensor_add(
            xnew_t[:, buf * DM:(buf + 1) * DM], xe_t[:],
            x_t[:, buf * DM:(buf + 1) * DM]), s_l3)
        wait("sync", s_l3, s_l3.v)
        op("sync", lambda e, i=i, buf=buf: e.dma_start(
            out=xres_r[i], in_=xnew_t[:, buf * DM:(buf + 1) * DM]), s_res, 16)
        wait("scalar", s_l3, s_l3.v)
        ln_stats(lambda buf=buf: xnew_t[:, buf * DM:(buf + 1) * DM],
                 cols[:, 10:11], cols[:, 11:12], None, sq_t[:, 0:DM], gp_tree=False)
        wait("scalar", s_st, s_st.v)
        op("scalar", lambda e: e.activation(cols[:, 12:13], cols[:, 11:12], AF.Sqrt), s_rs)
        wait("vector", s_rs, s_rs.v)
        op("vector", lambda e: e.reciprocal(cols[:, 13:14], cols[:, 12:13]))
        op("vector", lambda e: e.tensor_scalar(cols[:, 6:7], cols[:, 10:11],
                                               cols[:, 13:14], -1.0, ALU.mult, ALU.mult),
           s_rc)
        wait("scalar", s_rc, s_rc.v)
        if i > 0:
            wait("scalar", s_tp, s_tp.v)
        op("scalar", lambda e, buf=buf: e.activation(
            xn4_t[:], xnew_t[:, buf * DM:(buf + 1) * DM], AF.Identity,
            bias=cols[:, 6:7], scale=cols[:, 13:14]), s_l4)
        if DEBUG == 10:
            wait("gpsimd", s_l4, s_l4.v)
            op("gpsimd", lambda e, i=i: e.dma_start(out=dbg_r[i], in_=xn4_t[:]), s_wdma, 16)
        wait("tensor", s_l4, s_l4.v)
        if i > 0:
            wait("tensor", s_t4, s_t4.v)
        for dc in range(8):
            op("tensor", lambda e, dc=dc: e.transpose(
                psC.bitcast(BF16)[:, dc * P:(dc + 1) * P],
                xn4_t[:, dc * P:(dc + 1) * P], identb[:]),
               s_tp if dc == 7 else None)
        wait("scalar", s_tp, s_tp.v)
        for dc in range(8):
            op("scalar", lambda e, dc=dc, i=i: e.copy(
                xn4T[:, dc * TOK + i * P:dc * TOK + (i + 1) * P],
                psC.bitcast(BF16)[:, dc * P:(dc + 1) * P]),
               s_t4 if dc == 7 else None)

    if STOP == "A":
        wait("sync", s_res, s_res.v)
        op("sync", lambda e: e.dma_start(out=out, in_=xres), s_out, 16)
        wait("sync", s_out, s_out.v)
        with nc.Block() as blk:
            blk.sync(lambda e: [f(e) for f in prog["sync"]])
            blk.tensor(lambda e: [f(e) for f in prog["tensor"]])
            blk.vector(lambda e: [f(e) for f in prog["vector"]])
            blk.scalar(lambda e: [f(e) for f in prog["scalar"]])
            blk.gpsimd(lambda e: [f(e) for f in prog["gpsimd"]])
        return nc

    wr_r = wrout.rearrange("(dc p) e -> p dc e", p=P)
    op("sync", lambda e: e.dma_start(
        out=m_t[0:P, 0:NE * 8].rearrange("p (dc e) -> p dc e", e=NE), in_=wr_r),
       s_x, 16)
    wait("scalar", s_x, s_x.v)
    op("scalar", lambda e: e.copy(wr_sb[:], m_t[0:P, 0:NE * 8]), s_ap)
    op("vector", lambda e: e.memset(ctile[:], 0.0))
    wait("tensor", s_ap, s_ap.v)
    wait("tensor", s_l3, s_l3.v)
    wait("tensor", s_t4, s_t4.v)
    for ti in range(NCH):
        if ti > 0:
            wait("tensor", s_rc2, s_rc2.v)
        for dc in range(8):
            op("tensor", lambda e, ti=ti, dc=dc: e.matmul(
                psD[0:P, 0:NE],
                xn4T[:, dc * TOK + ti * P:dc * TOK + (ti + 1) * P],
                wr_sb[:, dc * NE:(dc + 1) * NE],
                start=(dc == 0), stop=(dc == 7)),
               s_rr if dc == 7 else None)
        wait("vector", s_rr, s_rr.v)
        op("vector", lambda e: e.tensor_copy(lgits[:], psD[0:P, 0:NE]))
        op("vector", lambda e: e.tensor_tensor(rt8[:, 0:4], lgits[:, 0:4],
                                               lgits[:, 4:8], ALU.max))
        op("vector", lambda e: e.tensor_tensor(rt8[:, 0:2], rt8[:, 0:2],
                                               rt8[:, 2:4], ALU.max))
        op("vector", lambda e: e.tensor_tensor(rt8[:, 0:1], rt8[:, 0:1],
                                               rt8[:, 1:2], ALU.max))
        op("vector", lambda e: e.tensor_scalar(cols[:, 1:2], rt8[:, 0:1], -1.0, None,
                                               ALU.mult), s_rv)
        wait("scalar", s_rv, s_rv.v)
        op("scalar", lambda e: e.activation(e8[:], lgits[:], AF.Exp, bias=cols[:, 1:2]),
           s_ap)
        wait("vector", s_ap, s_ap.v)
        op("vector", lambda e: e.tensor_tensor(rt8[:, 4:8], e8[:, 0:4], e8[:, 4:8],
                                               ALU.add))
        op("vector", lambda e: e.tensor_tensor(rt8[:, 4:6], rt8[:, 4:6], rt8[:, 6:8],
                                               ALU.add))
        op("vector", lambda e: e.tensor_tensor(rt8[:, 4:5], rt8[:, 4:5], rt8[:, 5:6],
                                               ALU.add))
        op("vector", lambda e: e.reciprocal(cols[:, 3:4], rt8[:, 4:5]))
        op("vector", lambda e: e.tensor_scalar(cols[:, 4:5], cols[:, 3:4], ALPHA, None,
                                               ALU.mult))
        op("vector", lambda e: e.memset(cols[:, 7:8], 0.0))
        for ex in range(7):
            op("vector", lambda e, ex=ex: e.tensor_scalar(
                cols[:, 6:7], lgits[:, ex:ex + 1], rt8[:, 0:1], None, ALU.is_ge))
            op("vector", lambda e: e.tensor_scalar(cols[:, 8:9], cols[:, 7:8], -1.0, 1.0,
                                                   ALU.mult, ALU.add))
            op("vector", lambda e: e.tensor_mul(cols[:, 6:7], cols[:, 6:7], cols[:, 8:9]))
            op("vector", lambda e, ex=ex: e.tensor_mul(ctile[:, ex:ex + 1],
                                                       cols[:, 6:7], cols[:, 4:5]))
            op("vector", lambda e: e.tensor_tensor(cols[:, 7:8], cols[:, 7:8],
                                                   cols[:, 6:7], ALU.max))
        op("vector", lambda e: e.memset(ctile[:, 7:8], 1.0))
        op("vector", lambda e: e.tensor_copy(ctileb[:], ctile[:]), s_rv)
        wait("tensor", s_rv, s_rv.v)
        op("tensor", lambda e: e.transpose(psD.bitcast(BF16)[0:NE, 0:P], ctileb[:],
                                           identb[:]), s_rt)
        wait("scalar", s_rt, s_rt.v)
        op("scalar", lambda e, ti=ti: e.copy(coef_rows[:, ti * P:(ti + 1) * P],
                                             psD.bitcast(BF16)[0:NE, 0:P]), s_rc2)

    if DEBUG == 11:
        wait("gpsimd", s_t4, s_t4.v)
        for j in range(16):
            op("gpsimd", lambda e, j=j: e.dma_start(
                out=dbg_r[j], in_=xn4T[:, j * DM:(j + 1) * DM]), s_wdma, 16)
    if DEBUG == 3:
        wait("sync", s_rc2, s_rc2.v)
        op("sync", lambda e: e.dma_start(out=dbg_r[0][0:NE, 0:DM],
                                         in_=coef_rows[:, 0:DM]), s_res, 16)
        op("sync", lambda e: e.dma_start(out=dbg_r[1][0:NE, 0:DM],
                                         in_=coef_rows[:, DM:2 * DM]), s_res, 16)
    wait("sync", s_rc2, s_rc2.v)
    op("sync", lambda e: e.dma_start(out=coefd, in_=coef_rows[:]), s_xr, 16)

    if STOP == "R":
        wait("sync", s_res, s_res.v)
        wait("sync", s_rc2, s_rc2.v)
        op("sync", lambda e: e.dma_start(out=out, in_=xres), s_out, 16)
        wait("sync", s_out, s_out.v)
        with nc.Block() as blk:
            blk.sync(lambda e: [f(e) for f in prog["sync"]])
            blk.tensor(lambda e: [f(e) for f in prog["tensor"]])
            blk.vector(lambda e: [f(e) for f in prog["vector"]])
            blk.scalar(lambda e: [f(e) for f in prog["scalar"]])
            blk.gpsimd(lambda e: [f(e) for f in prog["gpsimd"]])
        return nc

    for _g in reversed(_a_guards):
        _g.__exit__(None, None, None)
    wu_buf = S("wu_buf", [P, 3 * 2048], BF16)
    wg_buf = S("wg_buf", [P, 3 * 2048], BF16)
    wd_buf = S("wd_buf", [P, 3 * 2048], BF16)
    xs = S("xs", [P, 8 * TH], BF16)
    h_half = S("h_half", [P, 16 * TH], BF16)
    accT = S("accT", [P, 8 * TH], F32)
    coefb = S("coefb", [P, TH], BF16)
    sg_t = S("sg_t", [P, 2 * TC], F32)
    sq5_t = S("sq5_t", [P, DM], F32)
    pc_t = S("pc_t", [P, DM], F32)
    out_t = S("out_t", [P, 2 * DM], F32)
    xr_t = S("xr_t", [P, 2 * DM], F32)

    wup_r = wup.rearrange("e (dc p) h -> e p dc h", p=P)
    wgate_r = wgate.rearrange("e (dc p) h -> e p dc h", p=P)
    wdown_r = wdown.rearrange("e (hc p) d -> e p hc d", p=P)
    wu_v = wu_buf.rearrange("p (b dc h) -> p b dc h", b=3, h=256)
    wg_v = wg_buf.rearrange("p (b dc h) -> p b dc h", b=3, h=256)
    wd_v = wd_buf.rearrange("p (b hc dl) -> p b hc dl", b=3, dl=P)

    wait("gpsimd", s_rc2, s_rc2.v)
    last_ug_of_buf = {0: 0, 1: 0, 2: 0}
    last_dn_of_buf = {0: 0, 1: 0, 2: 0}

    def ffn_expert(th, e, first):
        wait("gpsimd", s_xs, s_xs.v)
        if first:
            wait("gpsimd", s_xr, s_xr.v)
        op("gpsimd", lambda e_, e=e, th=th: e_.dma_start(
            out=coefb[:],
            in_=coefd[e:e + 1, th * TH:(th + 1) * TH].broadcast_to([P, TH])),
           s_cb, 16)
        wait("vector", s_cb, s_cb.v)
        if not first:
            wait("vector", s_ug, s_ug.v)
        for dc in range(8):
            op("vector", lambda e_, dc=dc, th=th: e_.tensor_mul(
                xs[:, dc * TH:(dc + 1) * TH],
                xn4T[:, dc * TOK + th * TH:dc * TOK + (th + 1) * TH], coefb[:]),
               s_xs if dc == 7 else None)
        for hh in range(2):
            for hg in range(8):
                gidx = hh * 8 + hg
                b = gidx % 3
                if last_ug_of_buf[b]:
                    wait("gpsimd", s_ug, last_ug_of_buf[b])
                h0 = (hh * 16 + hg * 2) * P
                op("gpsimd", lambda e_, e=e, h0=h0, b=b: e_.dma_start(
                    out=wu_v[:, b], in_=wup_r[e][:, :, h0:h0 + 256]), s_wdma, 16)
                op("gpsimd", lambda e_, e=e, h0=h0, b=b: e_.dma_start(
                    out=wg_v[:, b], in_=wgate_r[e][:, :, h0:h0 + 256]), s_wdma, 16)
                wait("tensor", s_wdma, s_wdma.v)
                if hg == 0:
                    wait("tensor", s_ac, s_ac.v)
                    if hh == 0:
                        wait("tensor", s_xs, s_xs.v)
                        wait("tensor", s_e5, s_e5.v)
                for hl in range(2):
                    hc = hh * 16 + hg * 2 + hl
                    hcl = hg * 2 + hl
                    pu, pg = (psA, psB) if hc % 2 == 0 else (psC, psD)
                    if s_hm.v >= 2:
                        wait("tensor", s_hm, s_hm.v - 1)
                    if s_sg.v >= 2:
                        wait("tensor", s_sg, s_sg.v - 1)
                    for dc in range(8):
                        for tcol in range(2):
                            op("tensor", lambda e_, b=b, hl=hl, dc=dc, tcol=tcol,
                               pu=pu: e_.matmul(
                                pu[:, tcol * 512:(tcol + 1) * 512],
                                wu_v[:, b, dc, hl * P:(hl + 1) * P],
                                xs[:, dc * TH + tcol * 512:dc * TH + (tcol + 1) * 512],
                                start=(dc == 0), stop=(dc == 7)))
                    for dc in range(8):
                        for tcol in range(2):
                            op("tensor", lambda e_, b=b, hl=hl, dc=dc, tcol=tcol,
                               pg=pg, th=th: e_.matmul(
                                pg[:, tcol * 512:(tcol + 1) * 512],
                                wg_v[:, b, dc, hl * P:(hl + 1) * P],
                                xn4T[:, dc * TOK + th * TH + tcol * 512:
                                     dc * TOK + th * TH + (tcol + 1) * 512],
                                start=(dc == 0), stop=(dc == 7)),
                               s_ug if (dc == 7 and tcol == 1) else None)
                    wait("scalar", s_ug, s_ug.v)
                    for tcol in range(2):
                        op("scalar", lambda e_, pg=pg, tcol=tcol: e_.activation(
                            sg_t[:, tcol * TC:(tcol + 1) * TC],
                            pg[:, tcol * 512:(tcol + 1) * 512], AF.Silu),
                           s_sg if tcol == 1 else None)
                    wait("vector", s_sg, s_sg.v)
                    for tcol in range(2):
                        op("vector", lambda e_, pu=pu, hcl=hcl, tcol=tcol: e_.tensor_mul(
                            h_half[:, hcl * TH + tcol * 512:hcl * TH + (tcol + 1) * 512],
                            pu[:, tcol * 512:(tcol + 1) * 512],
                            sg_t[:, tcol * TC:(tcol + 1) * TC]),
                           s_hm if tcol == 1 else None)
                last_ug_of_buf[b] = s_ug.v
            for dc in range(8):
                b = dc % 3
                if last_dn_of_buf[b]:
                    wait("gpsimd", s_dn, last_dn_of_buf[b])
                op("gpsimd", lambda e_, e=e, hh=hh, dc=dc, b=b: e_.dma_start(
                    out=wd_v[:, b],
                    in_=wdown_r[e][:, hh * 16:(hh + 1) * 16, dc * P:(dc + 1) * P]),
                   s_wdma, 16)
                wait("tensor", s_wdma, s_wdma.v)
                pd = psA if dc % 2 == 0 else psB
                if dc == 0:
                    wait("tensor", s_hm, s_hm.v)
                    wait("tensor", s_sg, s_sg.v)
                elif s_ac.v >= 2:
                    wait("tensor", s_ac, s_ac.v - 1)
                for hcl in range(16):
                    for tcol in range(2):
                        op("tensor", lambda e_, b=b, hcl=hcl, tcol=tcol, pd=pd: e_.matmul(
                            pd[:, tcol * 512:(tcol + 1) * 512],
                            wd_v[:, b, hcl, 0:P],
                            h_half[:, hcl * TH + tcol * 512:hcl * TH + (tcol + 1) * 512],
                            start=(hcl == 0), stop=(hcl == 15)),
                           s_dn if (hcl == 15 and tcol == 1) else None)
                last_dn_of_buf[b] = s_dn.v
                wait("vector", s_dn, s_dn.v)
                if first and hh == 0 and dc == 0:
                    wait("vector", s_am, s_am.v)
                for tcol in range(2):
                    op("vector", lambda e_, dc=dc, tcol=tcol, pd=pd: e_.tensor_tensor(
                        accT[:, dc * TH + tcol * 512:dc * TH + (tcol + 1) * 512],
                        accT[:, dc * TH + tcol * 512:dc * TH + (tcol + 1) * 512],
                        pd[:, tcol * 512:(tcol + 1) * 512], ALU.add),
                       s_ac if tcol == 1 else None)

    def phase_c(th):
        for tt in range(TH // P):
            ti = th * (TH // P) + tt
            buf = tt % 2
            wait("sync", s_l3, s_l3.v)
            op("sync", lambda e_, ti=ti, buf=buf: e_.dma_start(
                out=xr_t[:, buf * DM:(buf + 1) * DM], in_=xres_r[ti]), s_xr, 16)
            pt = psC if tt % 2 == 0 else psD
            if tt == 0:
                wait("tensor", s_ac, s_ac.v)
                wait("tensor", s_sg, s_sg.v)
                wait("tensor", s_hm, s_hm.v)
            elif s_e5.v >= 2:
                wait("tensor", s_e5, s_e5.v - 1)
            for dc in range(8):
                op("tensor", lambda e_, dc=dc, tt=tt, pt=pt: e_.transpose(
                    pt[:, dc * P:(dc + 1) * P],
                    accT[:, dc * TH + tt * P:dc * TH + (tt + 1) * P], identf[:]),
                   s_c5 if dc == 7 else None)
            wait("vector", s_c5, s_c5.v)
            op("vector", lambda e_, pt=pt: e_.tensor_copy(pc_t[:], pt[:]), s_rv)
            wait("scalar", s_rv, s_rv.v)
            ln_stats(lambda: pc_t[:], cols[:, 10:11], cols[:, 11:12], None, sq5_t,
                     gp_tree=False)
            wait("scalar", s_st, s_st.v)
            op("scalar", lambda e_: e_.activation(cols[:, 12:13], cols[:, 11:12],
                                                  AF.Sqrt), s_rs)
            wait("vector", s_rs, s_rs.v)
            wait("vector", s_xr, s_xr.v)
            if s_out.v >= 32:
                wait("vector", s_out, s_out.v - 16)
            op("vector", lambda e_: e_.reciprocal(cols[:, 13:14], cols[:, 12:13]))
            op("vector", lambda e_: e_.tensor_scalar(
                xe_t[:], pc_t[:], cols[:, 10:11], cols[:, 13:14],
                ALU.subtract, ALU.mult), s_e5)
            op("vector", lambda e_, buf=buf: e_.tensor_add(
                out_t[:, buf * DM:(buf + 1) * DM], xe_t[:],
                xr_t[:, buf * DM:(buf + 1) * DM]), s_l3)
            wait("sync", s_l3, s_l3.v)
            op("sync", lambda e_, ti=ti, buf=buf: e_.dma_start(
                out=out_r[ti], in_=out_t[:, buf * DM:(buf + 1) * DM]), s_out, 16)

    if STOP == "B":
        wait("vector", s_c5, s_c5.v)
        op("vector", lambda e_: e_.memset(accT[:], 0.0), s_am)
        ffn_expert(0, 0, first=True)
        phase_c(0)
        wait("sync", s_res, s_res.v)
        op("sync", lambda e: e.dma_start(out=out.rearrange("(h t) d -> h t d", h=2)[1],
                                         in_=xres.rearrange("(h t) d -> h t d", h=2)[1]),
           s_out, 16)
    else:
        for th in range(NTH):
            wait("vector", s_c5, s_c5.v)
            op("vector", lambda e_: e_.memset(accT[:], 0.0), s_am)
            for ei in range(NE):
                ffn_expert(th, ei, first=(th == 0 and ei == 0))
            phase_c(th)

    wait("sync", s_out, s_out.v)
    wait("sync", s_res, s_res.v)

    with nc.Block() as blk:
        blk.sync(lambda e: [f(e) for f in prog["sync"]])
        blk.tensor(lambda e: [f(e) for f in prog["tensor"]])
        blk.vector(lambda e: [f(e) for f in prog["vector"]])
        blk.scalar(lambda e: [f(e) for f in prog["scalar"]])
        blk.gpsimd(lambda e: [f(e) for f in prog["gpsimd"]])
    return nc


def kernel(L0_attn_x, L0_attn_gamma, L0_attn_beta, L0_attn_LR, L0_attn_LI,
           L0_attn_CR, L0_attn_CI, L0_attn_D, L0_ffn_w_router, L0_ffn_w_up,
           L0_ffn_w_gate, L0_ffn_w_down, L0_ffn_ws_up, L0_ffn_ws_gate,
           L0_ffn_ws_down):
    x = np.ascontiguousarray(np.asarray(L0_attn_x, dtype=np.float32)).reshape(
        NB * LSEQ, DM)
    wupc = np.ascontiguousarray(np.concatenate(
        [np.asarray(L0_ffn_w_up, np.float32),
         np.asarray(L0_ffn_ws_up, np.float32)[None]], axis=0))
    wgatec = np.ascontiguousarray(np.concatenate(
        [np.asarray(L0_ffn_w_gate, np.float32),
         np.asarray(L0_ffn_ws_gate, np.float32)[None]], axis=0))
    wdownc = np.ascontiguousarray(np.concatenate(
        [np.asarray(L0_ffn_w_down, np.float32),
         np.asarray(L0_ffn_ws_down, np.float32)[None]], axis=0))

    nc = bass.Bass("TRN2", target_bir_lowering=False, detect_race_conditions=False)
    build(nc)

    in_maps = []
    for c in range(8):
        q = c % 4
        bstart = (c // 4) * LSEQ
        xov = x[bstart + q * TOK: bstart + (q + 1) * TOK]
        xpv = np.zeros((PRETOK, DM), np.float32)
        if q > 0:
            xpv[PRETOK - q * TOK:] = x[bstart: bstart + q * TOK]
        pm = np.zeros((NST, PREG), np.float32)
        ng = q * TOK // 512
        if ng > 0:
            pm[:, PREG - ng:] = 1.0
        in_maps.append({
            "xo": np.ascontiguousarray(xov),
            "xp": xpv,
            "pmask": pm,
            "lr": np.asarray(L0_attn_LR, np.float32).reshape(NST, 1),
            "li": np.asarray(L0_attn_LI, np.float32).reshape(NST, 1),
            "cr": np.ascontiguousarray(np.asarray(L0_attn_CR, np.float32)),
            "ci": np.ascontiguousarray(np.asarray(L0_attn_CI, np.float32)),
            "dvec": np.asarray(L0_attn_D, np.float32).reshape(1, DM),
            "gamma": np.asarray(L0_attn_gamma, np.float32).reshape(1, DM),
            "beta": np.asarray(L0_attn_beta, np.float32).reshape(1, DM),
            "wrout": np.ascontiguousarray(np.asarray(L0_ffn_w_router, np.float32)),
            "wup": wupc, "wgate": wgatec, "wdown": wdownc,
        })

    res = run_bass_kernel_spmd(nc, in_maps, core_ids=list(range(8)))
    full = np.concatenate([res.results[c]["out"] for c in range(8)], axis=0)
    return full.reshape(NB, LSEQ, DM)


# version 13
# speedup vs baseline: 1.1358x; 1.0877x over previous
import math
import numpy as np
import concourse.bass as bass
import concourse.mybir as mybir
from concourse.bass_utils import run_bass_kernel_spmd

F32 = mybir.dt.float32
BF16 = mybir.dt.bfloat16
U32 = mybir.dt.uint32
AF = mybir.ActivationFunctionType
ALU = mybir.AluOpType
AXX = mybir.AxisListType.X
AX = mybir.AxisListType

P = 128
DM = 1024
HID = 4096
NE = 8
NST = 64
LSEQ = 8192
NB = 2
TOK = 2048
NCH = 16
PRETOK = 6144
PREG = 12
GSUB = 4
TH = 1024
NTH = 2
TC = 512
EPS = 1e-5
ALPHA = 2.0


def _host_consts():
    c = {}
    tl = np.arange(P, dtype=np.float64)
    f = np.arange(P, dtype=np.float64)
    ang = 2.0 * np.pi * np.outer(tl, f) / 256.0
    sgn = ((-1.0) ** f)[None, :]
    Fr = sgn * np.cos(ang)
    Fi = -sgn * np.sin(ang)
    Fi[:, 0] = (-1.0) ** tl
    c["Fr"], c["Fi"] = Fr, Fi
    KFr = np.cos(ang)
    KFi = -np.sin(ang)
    KFi[:, 0] = (-1.0) ** tl
    c["KFr"], c["KFi"] = KFr, KFi
    w = np.full(P, 2.0 / 256.0)
    w[0] = 1.0 / 256.0
    ICr = (w * ((-1.0) ** f))[:, None] * np.cos(ang.T)
    ICi = -(w * ((-1.0) ** f))[:, None] * np.sin(ang.T)
    ICi[0, :] = (1.0 / 256.0) * ((-1.0) ** tl)
    c["ICr"], c["ICi"] = ICr, ICi
    c["ident"] = np.eye(P)
    c["rev"] = np.eye(P)[::-1].copy()
    c["ones_row"] = np.ones((1, P))
    return {k: np.ascontiguousarray(v, dtype=np.float32) for k, v in c.items()}


class _Sem:
    def __init__(self, h):
        self.h = h
        self.v = 0

    def inc(self, n=1):
        self.v += n
        return self.v


_DTSZ = {"dt.float32": 4, "dt.int32": 4, "dt.uint32": 4, "dt.bfloat16": 2,
         "dt.float16": 2, "dt.uint8": 1, "dt.int8": 1}


def _ap_rect(x):
    if type(x).__name__ != "AP":
        return None
    try:
        sp = str(x.space)
        if ("SBUF" not in sp) and ("PSUM" not in sp):
            return None
        h = x.tensor
        hshape = list(h.shape)
        wb = int(hshape[-1]) * _DTSZ.get(str(h.dtype), 4)
        asz = _DTSZ.get(str(x.dtype), 4)
        wc = max(1, wb // asz)
        off = int(x.offset)
        p0 = off // wc
        f0 = off % wc
        shp = list(x.shape)
        if len(shp) == 2:
            pn, fn_ = shp
        elif len(shp) == 1:
            pn, fn_ = 1, shp[0]
        else:
            pn = shp[0]
            fn_ = 1
            for s_ in shp[1:]:
                fn_ *= s_
        return (x.name, p0, p0 + pn, f0 * asz, (f0 + fn_) * asz)
    except Exception:
        try:
            return (x.name, 0, 1 << 20, 0, 1 << 40)
        except Exception:
            return None


def _olap(a, b):
    return (a[0] == b[0] and a[1] < b[2] and b[1] < a[2]
            and a[3] < b[4] and b[3] < a[4])


class _RecDummy:
    def __getattr__(self, n):
        return lambda *a, **k: self


class _Rec:
    def __init__(self):
        self.calls = 0
        self.reads = []
        self.writes = []
        self.names = []
        self._dummy = _RecDummy()

    def __getattr__(self, name):
        def f(*a, **k):
            self.calls += 1
            self.names.append(name)
            vals = list(a) + [v for kk, v in k.items() if kk != "out"]
            first = k["out"] if "out" in k else None
            for v in vals:
                r = _ap_rect(v)
                if r is None:
                    continue
                if first is None:
                    first = v
                    self.writes.append(r)
                else:
                    self.reads.append(r)
            if "out" in k:
                r = _ap_rect(k["out"])
                if r is not None:
                    self.writes.append(r)
            return self._dummy
        return f


DEBUG = 0
STOP = ""


def build(nc):
    con = _host_consts()

    xo = nc.dram_tensor("xo", [TOK, DM], F32, kind="ExternalInput").ap()
    xp = nc.dram_tensor("xp", [PRETOK, DM], F32, kind="ExternalInput").ap()
    pmask = nc.dram_tensor("pmask", [NST, PREG], F32, kind="ExternalInput").ap()
    lr = nc.dram_tensor("lr", [NST, 1], F32, kind="ExternalInput").ap()
    li = nc.dram_tensor("li", [NST, 1], F32, kind="ExternalInput").ap()
    cr = nc.dram_tensor("cr", [DM, NST], F32, kind="ExternalInput").ap()
    ci = nc.dram_tensor("ci", [DM, NST], F32, kind="ExternalInput").ap()
    dvec = nc.dram_tensor("dvec", [1, DM], F32, kind="ExternalInput").ap()
    gamma = nc.dram_tensor("gamma", [1, DM], F32, kind="ExternalInput").ap()
    beta = nc.dram_tensor("beta", [1, DM], F32, kind="ExternalInput").ap()
    wrout = nc.dram_tensor("wrout", [DM, NE], F32, kind="ExternalInput").ap()
    wup = nc.dram_tensor("wup", [NE, DM, HID], F32, kind="ExternalInput").ap()
    wgate = nc.dram_tensor("wgate", [NE, DM, HID], F32, kind="ExternalInput").ap()
    wdown = nc.dram_tensor("wdown", [NE, HID, DM], F32, kind="ExternalInput").ap()
    out = nc.dram_tensor("out", [TOK, DM], F32, kind="ExternalOutput").ap()
    xres = nc.dram_tensor("xres", [TOK, DM], F32,
                          kind="ExternalOutput" if DEBUG else "Internal").ap()
    coefd = nc.dram_tensor("coefd", [NE, TOK], F32, kind="Internal").ap()
    dbg = (nc.dram_tensor("dbg", [TOK, DM], F32, kind="ExternalOutput").ap()
           if DEBUG else None)
    dbg_r = dbg.rearrange("(t p) d -> t p d", p=P) if DEBUG else None

    cst = {k: nc.inline_tensor(v, name="c_" + k).ap() for k, v in con.items()}

    def S(name, shape, dtype):
        return nc.alloc_sbuf_tensor(name, shape, dtype)

    _a_guards = []

    def SA(name, shape, dtype):
        g = nc.sbuf_tensor(name, shape, dtype)
        h = g.__enter__()
        _a_guards.append(g)
        return h

    xn4T = S("xn4T", [P, 8 * TOK], BF16)
    coef_rows = S("coef_rows", [NE, TOK], F32)
    identb = S("identb", [P, P], BF16)
    identf = S("identf", [P, P], F32)
    ones_colf = S("ones_colf", [1, P], F32)
    stat = S("stat", [P, 16], F32)
    cols = S("cols", [P, 16], F32)
    wr_sb = S("wr_sb", [P, NE * 8], BF16)
    xe_t = S("xe_t", [P, DM], F32)
    lgits = S("lgits", [P, NE], F32)
    e8 = S("e8", [P, NE], F32)
    i8u = S("i8u", [P, NE], U32)
    v8 = S("v8", [P, NE], F32)
    ctile = S("ctile", [P, NE], F32)
    ctileb = S("ctileb", [P, NE], BF16)
    rt8 = S("rt8", [P, NE], F32)
    junkc = S("junkc", [P, 4], F32)
    carry_re = SA("carry_re", [NST, DM], F32)
    carry_im = SA("carry_im", [NST, DM], F32)
    cs_re = SA("cs_re", [NST, DM], F32)
    cs_im = SA("cs_im", [NST, DM], F32)
    csreb = SA("csreb", [NST, DM], BF16)
    csimb = SA("csimb", [NST, DM], BF16)
    kfr = SA("kfr", [P, DM], F32)
    kfi = SA("kfi", [P, DM], F32)
    kfr2 = SA("kfr2", [P, DM], F32)
    gammab = SA("gammab", [P, DM], F32)
    betab = SA("betab", [P, DM], F32)
    frb = SA("frb", [P, P], BF16)
    fib = SA("fib", [P, P], BF16)
    icrb = SA("icrb", [P, P], BF16)
    icib = SA("icib", [P, P], BF16)
    kfrb_c = SA("kfrb_c", [P, P], BF16)
    kfib_c = SA("kfib_c", [P, P], BF16)
    revb = SA("revb", [P, P], BF16)
    asre = SA("asre", [P, NST * 5], BF16)
    asim = SA("asim", [P, NST * 5], BF16)
    ppre = SA("ppre", [NST, P], BF16)
    ppim = SA("ppim", [NST, P], BF16)
    pkre = SA("pkre", [NST, P], BF16)
    pkim = SA("pkim", [NST, P], BF16)
    kb = SA("kb", [P, DM], BF16)
    aq_cols = SA("aq_cols", [NST, 4], F32)
    pmask_sb = SA("pmask_sb", [NST, PREG], F32)
    ones_mask = SA("ones_mask", [NST, 1], F32)
    pw_re = SA("pw_re", [NST, P], F32)
    pw_im = SA("pw_im", [NST, P], F32)
    sc64 = SA("sc64", [NST, 16], F32)
    sci = SA("sci", [NST, 1], mybir.dt.int32)
    powtab = SA("powtab", [NST, 8], F32)
    powrow = SA("powrow", [8, NST], F32)
    x_t = SA("x_t", [P, 4 * DM], F32)
    xn1_t = SA("xn1_t", [P, 2 * DM], F32)
    xn2a_t = SA("xn2a_t", [P, 2 * DM], F32)
    gn_t = SA("gn_t", [P, 2 * DM], F32)
    sq_t = SA("sq_t", [P, 2 * DM], F32)
    xn2b_t = SA("xn2b_t", [P, 4 * DM], BF16)
    m_t = SA("m_t", [P, 4 * DM], F32)
    yr_t = SA("yr_t", [P, DM], BF16)
    yi_t = SA("yi_t", [P, DM], BF16)
    w_re = SA("w_re", [NST, DM], F32)
    w_im = SA("w_im", [NST, DM], F32)
    wreb = SA("wreb", [NST, DM], BF16)
    wimb = SA("wimb", [NST, DM], BF16)
    wtmp = SA("wtmp", [NST, 2 * DM], F32)
    ctmp = SA("ctmp", [NST, 2 * DM], F32)
    xnew_t = SA("xnew_t", [P, 2 * DM], F32)
    xn4_t = SA("xn4_t", [P, DM], BF16)

    psA = nc.alloc_psum_tensor("psA", [P, DM], F32)
    psB = nc.alloc_psum_tensor("psB", [P, DM], F32)
    psC = nc.alloc_psum_tensor("psC", [P, DM], F32)
    psD = nc.alloc_psum_tensor("psD", [P, DM], F32)

    sems = {}

    def mksem(name):
        s = _Sem(nc.alloc_semaphore(name))
        sems[name] = s
        return s

    s_x = mksem("s_x")
    s_st = mksem("s_st")
    s_rs = mksem("s_rs")
    s_rc = mksem("s_rc")
    s_ap = mksem("s_ap")
    s_ln = mksem("s_ln")
    s_fft = mksem("s_fft")
    s_pw = mksem("s_pw")
    s_yy = mksem("s_yy")
    s_wl = mksem("s_wl")
    s_wg = mksem("s_wg")
    s_ys = mksem("s_ys")
    s_cv = mksem("s_cv")
    s_xc = mksem("s_xc")
    s_cu = mksem("s_cu")
    s_l3 = mksem("s_l3")
    s_l4 = mksem("s_l4")
    s_tp = mksem("s_tp")
    s_t4 = mksem("s_t4")
    s_rr = mksem("s_rr")
    s_rv = mksem("s_rv")
    s_rt = mksem("s_rt")
    s_rc2 = mksem("s_rc2")
    s_cfd = mksem("s_cfd")
    s_wdma = mksem("s_wdma")
    s_cb = mksem("s_cb")
    s_xs = mksem("s_xs")
    s_ug = mksem("s_ug")
    s_sg = mksem("s_sg")
    s_hm = mksem("s_hm")
    s_dn = mksem("s_dn")
    s_ac = mksem("s_ac")
    s_am = mksem("s_am")
    s_c5 = mksem("s_c5")
    s_e5 = mksem("s_e5")
    s_xr = mksem("s_xr")
    s_res = mksem("s_res")
    s_out = mksem("s_out")

    prog = {k: [] for k in ("sync", "tensor", "vector", "scalar", "gpsimd")}

    def em(eng, fn):
        prog[eng].append(fn)

    def wait(eng, sem, v):
        em(eng, lambda e, s=sem, v=v: e.wait_ge(s.h, v))

    _vhist = []
    _haz_sites = {}
    _n_spacers = [0]

    def _v_hazard(reads):
        between = 0
        for writes, real in reversed(_vhist):
            if between >= 1:
                break
            if any(_olap(r, w) for r in reads for w in writes):
                return True
            if real:
                between += 1
        return False

    def _v_push(writes, real):
        _vhist.append((writes, real))
        if len(_vhist) > 6:
            _vhist.pop(0)

    def op(eng, fn, sem=None, n=1):
        if eng == "vector":
            rec = _Rec()
            try:
                fn(rec)
            except Exception as ex:
                print("RAW-checker: recorder failed:", ex)
                rec.reads = rec.writes = [(nm, 0, 1 << 20, 0, 1 << 40)
                                          for nm in ("x_t", "cols", "ctmp")]
            if _v_hazard(rec.reads):
                import traceback
                ln = traceback.extract_stack()[-2].lineno
                _haz_sites[ln] = _haz_sites.get(ln, 0) + 1
                _n_spacers[0] += 1
                em("vector", lambda e: e.tensor_copy(junkc[:, 0:1], junkc[:, 2:3]))
                _v_push([("junkc", 0, P, 0, 4)], True)
            _v_push(rec.writes,
                    bool(rec.names) and any(nm != "memset" for nm in rec.names))
        if sem is None:
            em(eng, lambda e, f=fn: f(e))
        else:
            sem.inc(n)
            em(eng, lambda e, f=fn, s=sem, n=n: f(e).then_inc(s.h, n))

    build.haz_sites = _haz_sites
    build.n_spacers = _n_spacers

    op("vector", lambda e: e.memset(junkc[:], 0.0))

    for dst, src in [
        (sc64[:, 0:1], lr), (sc64[:, 1:2], li), (pmask_sb[:], pmask),
        (gammab[0:1, 0:DM], gamma), (betab[0:1, 0:DM], beta),
        (gn_t[0:1, DM:2 * DM], dvec), (ones_colf[0:1, 0:P], cst["ones_row"]),
        (identf[:], cst["ident"]),
        (m_t[0:P, 4 * P:5 * P], cst["Fr"]), (m_t[0:P, 5 * P:6 * P], cst["Fi"]),
        (m_t[0:P, 6 * P:7 * P], cst["ICr"]), (m_t[0:P, 7 * P:8 * P], cst["ICi"]),
        (m_t[0:P, 0:P], cst["KFr"]), (m_t[0:P, P:2 * P], cst["KFi"]),
        (m_t[0:P, 2 * P:3 * P], cst["rev"]),
    ]:
        op("sync", lambda e, d=dst, s=src: e.dma_start(out=d, in_=s), s_x, 16)
    v_ld0 = s_x.v

    wait("scalar", s_x, v_ld0)
    op("scalar", lambda e: e.copy(frb[:], m_t[0:P, 4 * P:5 * P]))
    op("scalar", lambda e: e.copy(fib[:], m_t[0:P, 5 * P:6 * P]))
    op("scalar", lambda e: e.copy(icrb[:], m_t[0:P, 6 * P:7 * P]))
    op("scalar", lambda e: e.copy(icib[:], m_t[0:P, 7 * P:8 * P]))
    op("scalar", lambda e: e.copy(kfrb_c[:], m_t[0:P, 0:P]))
    op("scalar", lambda e: e.copy(kfib_c[:], m_t[0:P, P:2 * P]))
    op("scalar", lambda e: e.copy(revb[:], m_t[0:P, 2 * P:3 * P]))
    op("scalar", lambda e: e.copy(identb[:], identf[:]), s_ap)

    wait("tensor", s_x, v_ld0)
    for ps, row, off in [(psA, gammab, 0), (psB, betab, 0), (psC, gn_t, DM)]:
        for h in range(2):
            op("tensor", lambda e, ps=ps, row=row, h=h, off=off: e.matmul(
                ps[:, h * 512:(h + 1) * 512], ones_colf[0:1, :],
                row[0:1, off + h * 512:off + (h + 1) * 512], start=True, stop=True),
               s_rr if (ps is psC and h == 1) else None)
    wait("scalar", s_rr, s_rr.v)
    op("scalar", lambda e: e.copy(gammab[:], psA[:]))
    op("scalar", lambda e: e.copy(betab[:], psB[:]))
    op("scalar", lambda e: e.copy(gn_t[:, DM:2 * DM], psC[:]), s_ap)

    wait("scalar", s_x, v_ld0)
    op("scalar", lambda e: e.activation(sc64[:, 2:3], sc64[:, 0:1], AF.Exp))
    op("scalar", lambda e: e.activation(sc64[:, 3:4], sc64[:, 1:2], AF.Exp), s_ap)
    wait("vector", s_ap, s_ap.v)
    op("vector", lambda e: e.tensor_scalar(sc64[:, 14:15], sc64[:, 2:3], -1.0, None, ALU.mult), s_rv)
    wait("scalar", s_rv, s_rv.v)
    op("scalar", lambda e: e.activation(sc64[:, 6:7], sc64[:, 14:15], AF.Exp), s_ap)
    twopi = 2.0 * math.pi
    wait("vector", s_ap, s_ap.v)
    op("vector", lambda e: e.tensor_scalar(sc64[:, 7:8], sc64[:, 3:4], 1.0 / twopi,
                                           None, ALU.mult))
    op("vector", lambda e: e.tensor_copy(sci[:], sc64[:, 7:8]))
    op("vector", lambda e: e.tensor_copy(sc64[:, 10:11], sci[:]))
    op("vector", lambda e: e.scalar_tensor_tensor(sc64[:, 7:8], sc64[:, 10:11], -twopi,
                                                  sc64[:, 3:4], ALU.mult, ALU.add))
    op("vector", lambda e: e.tensor_single_scalar(sc64[:, 10:11], sc64[:, 7:8], math.pi,
                                                  ALU.is_gt))
    op("vector", lambda e: e.scalar_tensor_tensor(sc64[:, 7:8], sc64[:, 10:11], -twopi,
                                                  sc64[:, 7:8], ALU.mult, ALU.add))
    op("vector", lambda e: e.tensor_single_scalar(sc64[:, 10:11], sc64[:, 7:8], -math.pi,
                                                  ALU.is_lt))
    op("vector", lambda e: e.scalar_tensor_tensor(sc64[:, 7:8], sc64[:, 10:11], twopi,
                                                  sc64[:, 7:8], ALU.mult, ALU.add))
    op("vector", lambda e: e.tensor_scalar(sc64[:, 11:12], sc64[:, 7:8], -1.0, None,
                                           ALU.mult))
    op("vector", lambda e: e.tensor_tensor(sc64[:, 11:12], sc64[:, 11:12], sc64[:, 7:8],
                                           ALU.max))
    op("vector", lambda e: e.tensor_scalar(sc64[:, 11:12], sc64[:, 11:12], -1.0,
                                           math.pi / 2.0, ALU.mult, ALU.add), s_rv)
    wait("scalar", s_rv, s_rv.v)
    op("scalar", lambda e: e.activation(sc64[:, 9:10], sc64[:, 7:8], AF.Sin))
    op("scalar", lambda e: e.activation(sc64[:, 8:9], sc64[:, 11:12], AF.Sin), s_ap)
    wait("vector", s_ap, s_ap.v)
    op("vector", lambda e: e.tensor_mul(sc64[:, 4:5], sc64[:, 6:7], sc64[:, 8:9]))
    op("vector", lambda e: e.tensor_mul(sc64[:, 5:6], sc64[:, 6:7], sc64[:, 9:10]))

    op("vector", lambda e: e.memset(pw_re[:, 0:1], 1.0))
    op("vector", lambda e: e.memset(pw_im[:, 0:1], 0.0))
    op("vector", lambda e: e.tensor_copy(sc64[:, 12:13], sc64[:, 4:5]))
    op("vector", lambda e: e.tensor_copy(sc64[:, 13:14], sc64[:, 5:6]))

    def csq():
        op("vector", lambda e: e.tensor_mul(sc64[:, 14:15], sc64[:, 12:13], sc64[:, 12:13]))
        op("vector", lambda e: e.tensor_mul(sc64[:, 15:16], sc64[:, 13:14], sc64[:, 13:14]))
        op("vector", lambda e: e.tensor_mul(sc64[:, 13:14], sc64[:, 12:13], sc64[:, 13:14]))
        op("vector", lambda e: e.tensor_scalar(sc64[:, 13:14], sc64[:, 13:14], 2.0, None, ALU.mult))
        op("vector", lambda e: e.tensor_sub(sc64[:, 12:13], sc64[:, 14:15], sc64[:, 15:16]))

    b = 1
    while b < P:
        op("vector", lambda e, b=b: e.tensor_scalar(ctmp[0:NST, 0:b], pw_re[:, 0:b],
                                                    sc64[:, 12:13], None, ALU.mult))
        op("vector", lambda e, b=b: e.tensor_scalar(ctmp[0:NST, DM:DM + b], pw_im[:, 0:b],
                                                    sc64[:, 13:14], None, ALU.mult))
        op("vector", lambda e, b=b: e.tensor_sub(pw_re[:, b:2 * b], ctmp[0:NST, 0:b],
                                                 ctmp[0:NST, DM:DM + b]))
        op("vector", lambda e, b=b: e.tensor_scalar(ctmp[0:NST, 0:b], pw_re[:, 0:b],
                                                    sc64[:, 13:14], None, ALU.mult))
        op("vector", lambda e, b=b: e.tensor_scalar(ctmp[0:NST, DM:DM + b], pw_im[:, 0:b],
                                                    sc64[:, 12:13], None, ALU.mult))
        op("vector", lambda e, b=b: e.tensor_add(pw_im[:, b:2 * b], ctmp[0:NST, 0:b],
                                                 ctmp[0:NST, DM:DM + b]))
        csq()
        b *= 2
    op("vector", lambda e: e.tensor_copy(aq_cols[:, 0:1], sc64[:, 12:13]))
    op("vector", lambda e: e.tensor_copy(aq_cols[:, 1:2], sc64[:, 13:14]))
    csq()
    csq()
    op("vector", lambda e: e.tensor_copy(aq_cols[:, 2:3], sc64[:, 12:13]))
    op("vector", lambda e: e.tensor_copy(aq_cols[:, 3:4], sc64[:, 13:14]))
    op("vector", lambda e: e.memset(ones_mask[:], 1.0))

    op("vector", lambda e: e.memset(powtab[:, 0:1], 1.0))
    op("vector", lambda e: e.memset(powtab[:, 1:2], 0.0))
    op("vector", lambda e: e.tensor_copy(powtab[:, 2:3], aq_cols[:, 0:1]))
    op("vector", lambda e: e.tensor_copy(powtab[:, 3:4], aq_cols[:, 1:2]))
    op("vector", lambda e: e.tensor_mul(sc64[:, 14:15], aq_cols[:, 0:1], aq_cols[:, 0:1]))
    op("vector", lambda e: e.tensor_mul(sc64[:, 15:16], aq_cols[:, 1:2], aq_cols[:, 1:2]))
    op("vector", lambda e: e.tensor_sub(powtab[:, 4:5], sc64[:, 14:15], sc64[:, 15:16]))
    op("vector", lambda e: e.tensor_mul(sc64[:, 14:15], aq_cols[:, 0:1], aq_cols[:, 1:2]))
    op("vector", lambda e: e.tensor_scalar(powtab[:, 5:6], sc64[:, 14:15], 2.0, None, ALU.mult))
    op("vector", lambda e: e.tensor_mul(sc64[:, 14:15], powtab[:, 4:5], aq_cols[:, 0:1]))
    op("vector", lambda e: e.tensor_mul(sc64[:, 15:16], powtab[:, 5:6], aq_cols[:, 1:2]))
    op("vector", lambda e: e.tensor_sub(powtab[:, 6:7], sc64[:, 14:15], sc64[:, 15:16]))
    op("vector", lambda e: e.tensor_mul(sc64[:, 14:15], powtab[:, 4:5], aq_cols[:, 1:2]))
    op("vector", lambda e: e.tensor_mul(sc64[:, 15:16], powtab[:, 5:6], aq_cols[:, 0:1]))
    op("vector", lambda e: e.tensor_add(powtab[:, 7:8], sc64[:, 14:15], sc64[:, 15:16]))

    op("vector", lambda e: e.tensor_scalar(sc64[:, 14:15], sc64[:, 4:5], -1.0, None, ALU.add))
    op("vector", lambda e: e.tensor_mul(sc64[:, 15:16], sc64[:, 2:3], sc64[:, 2:3]))
    op("vector", lambda e: e.tensor_mul(sc64[:, 10:11], sc64[:, 3:4], sc64[:, 3:4]))
    op("vector", lambda e: e.tensor_add(sc64[:, 15:16], sc64[:, 15:16], sc64[:, 10:11]))
    op("vector", lambda e: e.reciprocal(sc64[:, 15:16], sc64[:, 15:16]))
    op("vector", lambda e: e.tensor_mul(sc64[:, 10:11], sc64[:, 14:15], sc64[:, 2:3]))
    op("vector", lambda e: e.tensor_mul(sc64[:, 11:12], sc64[:, 5:6], sc64[:, 3:4]))
    op("vector", lambda e: e.tensor_sub(sc64[:, 11:12], sc64[:, 11:12], sc64[:, 10:11]))
    op("vector", lambda e: e.tensor_mul(sc64[:, 6:7], sc64[:, 11:12], sc64[:, 15:16]))
    op("vector", lambda e: e.tensor_mul(sc64[:, 10:11], sc64[:, 5:6], sc64[:, 2:3]))
    op("vector", lambda e: e.tensor_mul(sc64[:, 11:12], sc64[:, 14:15], sc64[:, 3:4]))
    op("vector", lambda e: e.tensor_add(sc64[:, 10:11], sc64[:, 10:11], sc64[:, 11:12]))
    op("vector", lambda e: e.tensor_mul(sc64[:, 10:11], sc64[:, 10:11], sc64[:, 15:16]))
    op("vector", lambda e: e.tensor_scalar(sc64[:, 7:8], sc64[:, 10:11], -1.0, None, ALU.mult),
       s_rv)

    cr_r = cr.rearrange("(t p) n -> t p n", p=P)
    ci_r = ci.rearrange("(t p) n -> t p n", p=P)
    for half, src_r in [(0, cr_r), (1, ci_r)]:
        for t8 in range(8):
            buf = t8 % 2
            if half or t8 >= 2:
                wait("sync", s_tp, s_tp.v)
            op("sync", lambda e, src_r=src_r, t8=t8, buf=buf: e.dma_start(
                out=x_t[0:P, buf * DM:buf * DM + NST], in_=src_r[t8]), s_x, 16)
            wait("tensor", s_x, s_x.v)
            if half or t8:
                wait("tensor", s_rv, s_rv.v)
            op("tensor", lambda e, buf=buf: e.transpose(
                psD[0:NST, 0:P], x_t[0:P, buf * DM:buf * DM + NST], identf[:]), s_tp)
            wait("vector", s_tp, s_tp.v)
            op("vector", lambda e, half=half, t8=t8: e.tensor_copy(
                wtmp[0:NST, half * DM + t8 * P:half * DM + (t8 + 1) * P],
                psD[0:NST, 0:P]), s_rv)
    op("vector", lambda e: e.tensor_scalar(ctmp[0:NST, 0:DM], wtmp[0:NST, 0:DM],
                                           sc64[:, 6:7], None, ALU.mult))
    op("vector", lambda e: e.tensor_scalar(ctmp[0:NST, DM:2 * DM], wtmp[0:NST, DM:2 * DM],
                                           sc64[:, 7:8], None, ALU.mult))
    op("vector", lambda e: e.tensor_sub(cs_re[:], ctmp[0:NST, 0:DM], ctmp[0:NST, DM:2 * DM]))
    op("vector", lambda e: e.tensor_scalar(ctmp[0:NST, 0:DM], wtmp[0:NST, 0:DM],
                                           sc64[:, 7:8], None, ALU.mult))
    op("vector", lambda e: e.tensor_scalar(ctmp[0:NST, DM:2 * DM], wtmp[0:NST, DM:2 * DM],
                                           sc64[:, 6:7], None, ALU.mult))
    op("vector", lambda e: e.tensor_add(cs_im[:], ctmp[0:NST, 0:DM], ctmp[0:NST, DM:2 * DM]),
       s_rv)
    wait("scalar", s_rv, s_rv.v)
    op("scalar", lambda e: e.copy(csreb[:], cs_re[:]))
    op("scalar", lambda e: e.copy(csimb[:], cs_im[:]))

    wait("scalar", s_rv, s_rv.v)
    op("scalar", lambda e: e.copy(pkre[:], pw_re[:]), s_ap)
    op("vector", lambda e: e.tensor_scalar(pkim[:], pw_im[:], -1.0, None, ALU.mult), s_rv)
    wait("scalar", s_rv, s_rv.v)
    wait("tensor", s_rv, s_rv.v)

    for m in range(4):
        if m > 0:
            wait("vector", s_ap, s_ap.v)
        op("vector", lambda e, m=m: e.tensor_scalar(ctmp[0:NST, 0:P], pw_re[:],
                                                    powtab[:, 2 * m:2 * m + 1], None, ALU.mult))
        op("vector", lambda e, m=m: e.tensor_scalar(ctmp[0:NST, P:2 * P], pw_im[:],
                                                    powtab[:, 2 * m + 1:2 * m + 2], None, ALU.mult))
        op("vector", lambda e: e.tensor_sub(ctmp[0:NST, 2 * P:3 * P], ctmp[0:NST, 0:P],
                                            ctmp[0:NST, P:2 * P]))
        op("vector", lambda e, m=m: e.tensor_scalar(ctmp[0:NST, 0:P], pw_re[:],
                                                    powtab[:, 2 * m + 1:2 * m + 2], None, ALU.mult))
        op("vector", lambda e, m=m: e.tensor_scalar(ctmp[0:NST, P:2 * P], pw_im[:],
                                                    powtab[:, 2 * m:2 * m + 1], None, ALU.mult))
        op("vector", lambda e: e.tensor_add(ctmp[0:NST, 3 * P:4 * P], ctmp[0:NST, 0:P],
                                            ctmp[0:NST, P:2 * P]), s_rv)
        wait("scalar", s_rv, s_rv.v)
        op("scalar", lambda e: e.copy(xn2b_t[0:NST, 0:P], ctmp[0:NST, 2 * P:3 * P]))
        op("scalar", lambda e: e.copy(xn2b_t[0:NST, P:2 * P], ctmp[0:NST, 3 * P:4 * P]), s_ap)
        wait("tensor", s_ap, s_ap.v)
        if m > 0:
            wait("tensor", s_t4, s_t4.v)
        op("tensor", lambda e: e.transpose(psA.bitcast(BF16)[0:P, 0:NST],
                                           xn2b_t[0:NST, 0:P], identb[0:NST, 0:NST]))
        op("tensor", lambda e: e.transpose(psB.bitcast(BF16)[0:P, 0:NST],
                                           xn2b_t[0:NST, P:2 * P], identb[0:NST, 0:NST]),
           s_tp)
        wait("scalar", s_tp, s_tp.v)
        op("scalar", lambda e: e.copy(xn4_t[0:P, 0:NST], psA.bitcast(BF16)[0:P, 0:NST]))
        op("scalar", lambda e: e.copy(xn4_t[0:P, NST:2 * NST],
                                      psB.bitcast(BF16)[0:P, 0:NST]), s_ap)
        wait("tensor", s_ap, s_ap.v)
        op("tensor", lambda e: e.matmul(psA[0:P, 0:NST], revb[:], xn4_t[0:P, 0:NST],
                                        start=True, stop=True))
        op("tensor", lambda e: e.matmul(psB[0:P, 0:NST], revb[:], xn4_t[0:P, NST:2 * NST],
                                        start=True, stop=True), s_rr)
        wait("scalar", s_rr, s_rr.v)
        slots = [0, 4] if m == 0 else [4 - m]
        for sl in slots:
            op("scalar", lambda e, sl=sl: e.copy(asre[:, sl * NST:(sl + 1) * NST],
                                                 psA[0:P, 0:NST]))
            op("scalar", lambda e, sl=sl: e.copy(asim[:, sl * NST:(sl + 1) * NST],
                                                 psB[0:P, 0:NST]),
               s_t4 if sl == slots[-1] else None)
    wait("vector", s_t4, s_t4.v)

    if DEBUG == 7:
        wait("gpsimd", s_t4, s_t4.v)
        op("gpsimd", lambda e: e.dma_start(out=dbg_r[0][0:P, 0:320], in_=asre[:]), s_wdma, 16)
        op("gpsimd", lambda e: e.dma_start(out=dbg_r[1][0:P, 0:320], in_=asim[:]), s_wdma, 16)
        wait("gpsimd", s_rv, s_rv.v)
        op("gpsimd", lambda e: e.dma_start(out=dbg_r[2][0:NST, 0:P], in_=ctmp[0:NST, 2 * P:3 * P]), s_wdma, 16)
        op("gpsimd", lambda e: e.dma_start(out=dbg_r[2][64:P, 0:P], in_=ctmp[0:NST, 3 * P:4 * P]), s_wdma, 16)
        op("gpsimd", lambda e: e.dma_start(out=dbg_r[3][0:NST, 0:8], in_=powtab[:]), s_wdma, 16)
    wait("vector", s_ap, s_ap.v)
    op("vector", lambda e: e.tensor_scalar(ctmp[0:NST, 0:P], pw_re[:], sc64[:, 4:5], None, ALU.mult))
    op("vector", lambda e: e.tensor_scalar(ctmp[0:NST, P:2 * P], pw_im[:], sc64[:, 5:6], None, ALU.mult))
    op("vector", lambda e: e.tensor_sub(ctmp[0:NST, 2 * P:3 * P], ctmp[0:NST, 0:P],
                                        ctmp[0:NST, P:2 * P]))
    op("vector", lambda e: e.tensor_scalar(ctmp[0:NST, 0:P], pw_re[:], sc64[:, 5:6], None, ALU.mult))
    op("vector", lambda e: e.tensor_scalar(ctmp[0:NST, P:2 * P], pw_im[:], sc64[:, 4:5], None, ALU.mult))
    op("vector", lambda e: e.tensor_add(ctmp[0:NST, 3 * P:4 * P], ctmp[0:NST, 0:P],
                                        ctmp[0:NST, P:2 * P]), s_rv)
    wait("scalar", s_rv, s_rv.v)
    op("scalar", lambda e: e.copy(ppre[:], ctmp[0:NST, 2 * P:3 * P]), s_ap)
    op("vector", lambda e: e.tensor_scalar(ppim[:], ctmp[0:NST, 3 * P:4 * P], -1.0,
                                           None, ALU.mult), s_rv)

    wait("tensor", s_ap, s_ap.v)
    for h in range(2):
        op("tensor", lambda e, h=h: e.matmul(psA[:, h * 512:(h + 1) * 512], pkre[:],
                                             csreb[:, h * 512:(h + 1) * 512],
                                             start=True, stop=False))
        op("tensor", lambda e, h=h: e.matmul(psA[:, h * 512:(h + 1) * 512], pkim[:],
                                             csimb[:, h * 512:(h + 1) * 512],
                                             start=False, stop=True),
           s_rr if h == 1 else None)
    wait("scalar", s_rr, s_rr.v)
    op("scalar", lambda e: e.copy(kb[:], psA[:]), s_ap)
    wait("tensor", s_ap, s_ap.v)
    for h in range(2):
        op("tensor", lambda e, h=h: e.matmul(psA[:, h * 512:(h + 1) * 512], kfrb_c[:],
                                             kb[:, h * 512:(h + 1) * 512],
                                             start=True, stop=True))
        op("tensor", lambda e, h=h: e.matmul(psB[:, h * 512:(h + 1) * 512], kfib_c[:],
                                             kb[:, h * 512:(h + 1) * 512],
                                             start=True, stop=True),
           s_rr if h == 1 else None)
    wait("vector", s_rr, s_rr.v)
    op("vector", lambda e: e.tensor_add(kfr[:], psA[:], gn_t[:, DM:2 * DM]))
    op("vector", lambda e: e.tensor_copy(kfi[:], psB[:]))
    op("vector", lambda e: e.memset(kfi[0:1, 0:DM], 0.0))
    op("vector", lambda e: e.tensor_copy(kfr2[:], kfr[:]))
    op("vector", lambda e: e.tensor_add(kfr2[0:1, 0:DM], psB[0:1, 0:DM], gn_t[0:1, DM:2 * DM]))
    op("vector", lambda e: e.memset(carry_re[:], 0.0))
    op("vector", lambda e: e.memset(carry_im[:], 0.0), s_cu)
    if DEBUG == 5:
        wait("sync", s_cu, s_cu.v)
        op("sync", lambda e: e.dma_start(out=dbg_r[0][0:NST, 0:P], in_=pw_re[:]), s_res, 16)
        op("sync", lambda e: e.dma_start(out=dbg_r[0][64:128, 0:P], in_=pw_im[:]), s_res, 16)
        op("sync", lambda e: e.dma_start(out=dbg_r[1][0:NST, 0:DM], in_=cs_re[:]), s_res, 16)
        op("sync", lambda e: e.dma_start(out=dbg_r[2][0:NST, 0:DM], in_=cs_im[:]), s_res, 16)
        op("sync", lambda e: e.dma_start(out=dbg_r[3][0:NST, 0:16], in_=sc64[:]), s_res, 16)
        op("sync", lambda e: e.dma_start(out=dbg_r[3][64:128, 0:4], in_=aq_cols[:]), s_res, 16)
    if DEBUG == 4:
        wait("sync", s_cu, s_cu.v)
        op("sync", lambda e: e.dma_start(out=dbg_r[0], in_=kfr[:]), s_res, 16)
        op("sync", lambda e: e.dma_start(out=dbg_r[1], in_=kfi[:]), s_res, 16)
        op("sync", lambda e: e.dma_start(out=dbg_r[2], in_=kfr2[:]), s_res, 16)
        wait("gpsimd", s_cu, s_cu.v)
        op("gpsimd", lambda e: e.dma_start(out=dbg_r[3], in_=kb[:]), s_wdma, 16)

    xo_r = xo.rearrange("(t p) d -> t p d", p=P)
    xp_r = xp.rearrange("(t p) d -> t p d", p=P)
    xres_r = xres.rearrange("(t p) d -> t p d", p=P)
    out_r = out.rearrange("(t p) d -> t p d", p=P)


    def tree_steps(eng, buf, w0, sem=None):
        w = w0
        while w > 1:
            h = w // 2
            op(eng, lambda e, h=h, w=w, buf=buf: e.tensor_tensor(
                buf[:, 0:h], buf[:, 0:h], buf[:, h:w], ALU.add),
               sem if h == 1 else None)
            w = h

    _lncnt = [0]

    def ln_stats(src_fn, mu_col, var_col, tree_buf, sq_buf, gp_tree=False):
        _lncnt[0] += 1
        """Emit mean/var(+eps) of src [128 x 1024] into cols columns.
        ACT squares; DVE does two tensor_reduce sums + the stats tail.
        Ends with var+eps in var_col (inc s_st)."""
        op("scalar", lambda e: e.activation(sq_buf[:], src_fn(), AF.Square), s_rs)
        op("vector", lambda e: e.tensor_reduce(stat[:, 0:1], src_fn(),
                                               axis=AXX, op=ALU.add))
        wait("vector", s_rs, s_rs.v)
        op("vector", lambda e: e.tensor_reduce(cols[:, 14:15], sq_buf[:],
                                               axis=AXX, op=ALU.add))
        op("vector", lambda e: e.tensor_scalar(mu_col, stat[:, 0:1], 1.0 / DM,
                                               None, ALU.mult))
        op("vector", lambda e: e.tensor_scalar(cols[:, 14:15], cols[:, 14:15],
                                               1.0 / DM, EPS, ALU.mult, ALU.add))
        op("vector", lambda e: e.tensor_mul(cols[:, 15:16], mu_col, mu_col))
        op("vector", lambda e: e.tensor_sub(var_col, cols[:, 14:15], cols[:, 15:16]),
           s_st)

    def ln_pipeline(src_slice, dst_slice):
        wait("vector", s_x, s_x.v)
        wait("scalar", s_x, s_x.v)
        ln_stats(lambda: src_slice, cols[:, 0:1], cols[:, 1:2], None,
                 sq_t[:, 0:DM], gp_tree=False)
        wait("scalar", s_st, s_st.v)
        op("scalar", lambda e: e.activation(cols[:, 2:3], cols[:, 1:2], AF.Sqrt), s_rs)
        wait("vector", s_rs, s_rs.v)
        op("vector", lambda e: e.reciprocal(cols[:, 3:4], cols[:, 2:3]))
        op("vector", lambda e: e.tensor_scalar(cols[:, 4:5], cols[:, 0:1], cols[:, 3:4], -1.0,
                                               ALU.mult, ALU.mult), s_rc)
        wait("scalar", s_rc, s_rc.v)
        op("scalar", lambda e: e.activation(xn1_t[:, 0:DM], src_slice, AF.Identity,
                                            bias=cols[:, 4:5], scale=cols[:, 3:4]), s_ap)
        wait("vector", s_ap, s_ap.v)
        ln_stats(lambda: xn1_t[:, 0:DM], cols[:, 5:6], cols[:, 6:7], None,
                 sq_t[:, 0:DM], gp_tree=False)
        wait("scalar", s_st, s_st.v)
        op("scalar", lambda e: e.activation(cols[:, 7:8], cols[:, 6:7], AF.Sqrt), s_rs)
        wait("vector", s_rs, s_rs.v)
        op("vector", lambda e: e.reciprocal(cols[:, 8:9], cols[:, 7:8]))
        op("vector", lambda e: e.tensor_scalar(cols[:, 9:10], cols[:, 5:6], cols[:, 8:9], -1.0,
                                               ALU.mult, ALU.mult), s_rc)
        wait("scalar", s_rc, s_rc.v)
        wait("scalar", s_ln, s_ln.v)
        op("scalar", lambda e: e.activation(xn2a_t[:, 0:DM], xn1_t[:, 0:DM], AF.Identity,
                                            bias=cols[:, 9:10], scale=cols[:, 8:9]), s_ap)
        wait("gpsimd", s_ap, s_ap.v)
        op("gpsimd", lambda e: e.tensor_mul(gn_t[:, 0:DM], xn2a_t[:, 0:DM], gammab[:]), s_wg)
        wait("vector", s_wg, s_wg.v)
        wait("vector", s_tp, s_tp.v)
        if s_xc.v >= 1:
            wait("vector", s_xc, s_xc.v - 1)
        op("vector", lambda e, d=dst_slice: e.tensor_tensor(d, gn_t[:, 0:DM], betab[:],
                                                            ALU.add), s_ln)

    def carry_update(mask_col, re_col, im_col):
        wait("vector", s_xc, s_xc.v)
        op("vector", lambda e, ic=im_col: e.tensor_scalar(ctmp[0:NST, 0:DM], carry_im[:],
                                                          ic, None, ALU.mult))
        op("vector", lambda e, rc=re_col: e.scalar_tensor_tensor(
            ctmp[0:NST, DM:2 * DM], carry_re[:], rc, ctmp[0:NST, 0:DM],
            ALU.mult, ALU.subtract))
        op("vector", lambda e, ic=im_col: e.tensor_scalar(ctmp[0:NST, 0:DM], carry_re[:],
                                                          ic, None, ALU.mult))
        op("vector", lambda e, rc=re_col: e.scalar_tensor_tensor(
            carry_im[:], carry_im[:], rc, ctmp[0:NST, 0:DM], ALU.mult, ALU.add))
        op("vector", lambda e, m=mask_col: e.scalar_tensor_tensor(
            carry_re[:], psA[0:NST, 0:DM], m, ctmp[0:NST, DM:2 * DM], ALU.mult, ALU.add))
        op("vector", lambda e, m=mask_col: e.scalar_tensor_tensor(
            carry_im[:], psB[0:NST, 0:DM], m, carry_im[:], ALU.mult, ALU.add), s_cu)

    wait("tensor", s_cu, s_cu.v)

    ap_ln1_pair = {}
    xc_after_pair = {}

    def ln_pair(base, pidx):
        src2 = x_t[:, base:base + 2 * DM]
        src2v = x_t[:, base:base + 2 * DM].rearrange("p (b d) -> p b d", d=DM)
        sq2 = sq_t[:, 0:2 * DM]
        sq2v = sq_t[:, 0:2 * DM].rearrange("p (b d) -> p b d", d=DM)
        wait("scalar", s_x, s_x.v)
        wait("vector", s_x, s_x.v)

        def stats2(sfull, sview):
            op("scalar", lambda e: e.activation(sq2, sfull, AF.Square), s_rs)
            op("vector", lambda e: e.tensor_reduce(stat[:, 0:2], sview, axis=AXX,
                                                   op=ALU.add, negate=True))
            wait("vector", s_rs, s_rs.v)
            op("vector", lambda e: e.tensor_reduce(stat[:, 2:4], sq2v, axis=AXX,
                                                   op=ALU.add))
            op("vector", lambda e: e.tensor_scalar(stat[:, 4:6], stat[:, 0:2],
                                                   1.0 / DM, None, ALU.mult))
            op("vector", lambda e: e.tensor_scalar(stat[:, 2:4], stat[:, 2:4],
                                                   1.0 / DM, EPS, ALU.mult, ALU.add))
            op("vector", lambda e: e.tensor_mul(stat[:, 6:8], stat[:, 4:6],
                                                stat[:, 4:6]))
            op("vector", lambda e: e.tensor_sub(stat[:, 8:10], stat[:, 2:4],
                                                stat[:, 6:8]), s_st)
            wait("scalar", s_st, s_st.v)
            op("scalar", lambda e: e.activation(stat[:, 10:12], stat[:, 8:10],
                                                AF.Sqrt), s_rs)
            wait("vector", s_rs, s_rs.v)
            op("vector", lambda e: e.reciprocal(stat[:, 12:14], stat[:, 10:12]))
            op("vector", lambda e: e.tensor_mul(stat[:, 14:16], stat[:, 4:6],
                                                stat[:, 12:14]), s_rc)
            wait("scalar", s_rc, s_rc.v)

        stats2(src2, src2v)
        for j in range(2):
            op("scalar", lambda e, j=j: e.activation(
                xn1_t[:, j * DM:(j + 1) * DM],
                x_t[:, base + j * DM:base + (j + 1) * DM],
                AF.Identity, bias=stat[:, 14 + j:15 + j],
                scale=stat[:, 12 + j:13 + j]), s_ap if j == 1 else None)
        ap_ln1_pair[pidx] = s_ap.v
        xn12 = xn1_t[:, 0:2 * DM]
        xn12v = xn1_t[:, 0:2 * DM].rearrange("p (b d) -> p b d", d=DM)
        wait("vector", s_ap, s_ap.v)
        stats2(xn12, xn12v)
        wait("scalar", s_ln, s_ln.v)
        for j in range(2):
            op("scalar", lambda e, j=j: e.activation(
                xn2a_t[:, j * DM:(j + 1) * DM], xn1_t[:, j * DM:(j + 1) * DM],
                AF.Identity, bias=stat[:, 14 + j:15 + j],
                scale=stat[:, 12 + j:13 + j]), s_ap if j == 1 else None)
        wait("gpsimd", s_ap, s_ap.v)
        for j in range(2):
            op("gpsimd", lambda e, j=j: e.tensor_mul(
                gn_t[:, j * DM:(j + 1) * DM], xn2a_t[:, j * DM:(j + 1) * DM],
                gammab[:]), s_wg if j == 1 else None)
        wait("vector", s_wg, s_wg.v)
        wait("vector", s_tp, s_tp.v)
        if pidx >= 2:
            wait("vector", s_xc, xc_after_pair[pidx - 2])
        for j in range(2):
            op("vector", lambda e, j=j: e.tensor_tensor(
                xn2b_t[:, base + j * DM:base + (j + 1) * DM],
                gn_t[:, j * DM:(j + 1) * DM], betab[:], ALU.add), s_ln)

    for g in range(PREG):
        for half in range(2):
            pidx = g * 2 + half
            base = (pidx % 2) * 2 * DM
            if pidx == 0:
                wait("sync", s_tp, s_tp.v)
            if pidx >= 2:
                wait("sync", s_ap, ap_ln1_pair[pidx - 2])
            for j in range(2):
                op("sync", lambda e, ch=pidx * 2 + j, base=base, j=j: e.dma_start(
                    out=x_t[:, base + j * DM:base + (j + 1) * DM],
                    in_=xp_r[ch]), s_x, 16)
            ln_pair(base, pidx)
            wait("tensor", s_ln, s_ln.v)
            if half == 0:
                wait("tensor", s_cu, s_cu.v)
            for j in range(2):
                k = half * 2 + j
                for h in range(2):
                    op("tensor", lambda e, k=k, h=h, base=base, j=j: e.matmul(
                        psA[0:NST, h * 512:(h + 1) * 512],
                        asre[:, (1 + k) * NST:(2 + k) * NST],
                        xn2b_t[:, base + j * DM + h * 512:
                               base + j * DM + (h + 1) * 512],
                        start=(k == 0), stop=(k == GSUB - 1)))
                    op("tensor", lambda e, k=k, h=h, base=base, j=j: e.matmul(
                        psB[0:NST, h * 512:(h + 1) * 512],
                        asim[:, (1 + k) * NST:(2 + k) * NST],
                        xn2b_t[:, base + j * DM + h * 512:
                               base + j * DM + (h + 1) * 512],
                        start=(k == 0), stop=(k == GSUB - 1)),
                       s_xc if h == 1 else None)
            xc_after_pair[pidx] = s_xc.v
        carry_update(pmask_sb[:, g:g + 1], aq_cols[:, 2:3], aq_cols[:, 3:4])

    l3_after = {}
    for i in range(NCH):
        buf = i % 2
        slot = i % 4
        if i % 2 == 0:
            p = i // 2
            pbase = (p % 2) * 2 * DM
            if p >= 2:
                wait("sync", s_l3, l3_after[i - 3])
            else:
                wait("sync", s_ap, ap_ln1_pair[22 + p])
            for j in range(2):
                op("sync", lambda e, ii=i + j, pbase=pbase, j=j: e.dma_start(
                    out=x_t[:, pbase + j * DM:pbase + (j + 1) * DM],
                    in_=xo_r[ii]), s_x, 16)
            ln_pair(pbase, 24 + p)
        if DEBUG == 1:
            op("gpsimd", lambda e, i=i, slot=slot: e.dma_start(
                out=dbg_r[i], in_=xn2b_t[:, slot * DM:(slot + 1) * DM]), s_wdma, 16)
        if DEBUG == 12 and i == 1:
            wait("sync", s_st, s_st.v)
            op("sync", lambda e: e.dma_start(out=dbg_r[0], in_=sq_t[:, 0:DM]), s_res, 16)
            op("sync", lambda e: e.dma_start(out=dbg_r[1][0:P, 0:512], in_=tr_t[:]), s_res, 16)
        if DEBUG == 8 and i < 8:
            wait("sync", s_ap, s_ap.v)
            op("sync", lambda e, i=i: e.dma_start(out=dbg_r[2 * i], in_=xn1_t[:, 0:DM]), s_res, 16)
            op("sync", lambda e, i=i: e.dma_start(out=dbg_r[2 * i + 1][0:P, 0:16],
                                                  in_=cols[:]), s_res, 16)
        if DEBUG == 9 and i < 8:
            wait("sync", s_x, s_x.v)
            op("sync", lambda e, i=i, slot=slot: e.dma_start(
                out=dbg_r[i], in_=x_t[:, slot * DM:(slot + 1) * DM]), s_res, 16)
        wait("gpsimd", s_cu, s_cu.v)
        op("gpsimd", lambda e: e.tensor_mul(wtmp[0:NST, 0:DM], cs_re[:], carry_re[:]))
        op("gpsimd", lambda e: e.tensor_mul(wtmp[0:NST, DM:2 * DM], cs_im[:], carry_im[:]))
        op("gpsimd", lambda e: e.tensor_sub(w_re[:], wtmp[0:NST, 0:DM], wtmp[0:NST, DM:2 * DM]))
        op("gpsimd", lambda e: e.tensor_mul(wtmp[0:NST, 0:DM], cs_re[:], carry_im[:]))
        op("gpsimd", lambda e: e.tensor_mul(wtmp[0:NST, DM:2 * DM], cs_im[:], carry_re[:]))
        op("gpsimd", lambda e: e.tensor_add(w_im[:], wtmp[0:NST, 0:DM],
                                            wtmp[0:NST, DM:2 * DM]), s_wg)
        wait("scalar", s_wg, s_wg.v)
        op("scalar", lambda e: e.copy(wreb[:], w_re[:]))
        op("scalar", lambda e: e.copy(wimb[:], w_im[:]), s_wl)
        wait("tensor", s_ln, s_ln.v)
        wait("tensor", s_cu, s_cu.v)
        if i > 0:
            wait("tensor", s_pw, s_pw.v)
        for h in range(2):
            op("tensor", lambda e, h=h, slot=slot: e.matmul(
                psA[:, h * 512:(h + 1) * 512], frb[:],
                xn2b_t[:, slot * DM + h * 512:slot * DM + (h + 1) * 512],
                start=True, stop=True))
            op("tensor", lambda e, h=h, slot=slot: e.matmul(
                psB[:, h * 512:(h + 1) * 512], fib[:],
                xn2b_t[:, slot * DM + h * 512:slot * DM + (h + 1) * 512],
                start=True, stop=True),
               s_fft if h == 1 else None)
        wait("vector", s_fft, s_fft.v)
        op("vector", lambda e: e.tensor_mul(m_t[:, 0:DM], psA[:], kfr[:]))
        op("vector", lambda e: e.tensor_mul(m_t[:, DM:2 * DM], psB[:], kfi[:]))
        op("vector", lambda e: e.tensor_mul(m_t[:, 2 * DM:3 * DM], psA[:], kfi[:]))
        op("vector", lambda e: e.tensor_mul(m_t[:, 3 * DM:4 * DM], psB[:], kfr2[:]), s_pw)
        wait("gpsimd", s_pw, s_pw.v)
        op("gpsimd", lambda e: e.tensor_sub(m_t[:, 0:DM], m_t[:, 0:DM], m_t[:, DM:2 * DM]))
        op("gpsimd", lambda e: e.tensor_add(m_t[:, 2 * DM:3 * DM], m_t[:, 2 * DM:3 * DM],
                                            m_t[:, 3 * DM:4 * DM]), s_yy)
        wait("scalar", s_yy, s_yy.v)
        op("scalar", lambda e: e.copy(yr_t[:], m_t[:, 0:DM]))
        op("scalar", lambda e: e.copy(yi_t[:], m_t[:, 2 * DM:3 * DM]), s_ys)
        wait("tensor", s_ys, s_ys.v)
        wait("tensor", s_wl, s_wl.v)
        if i > 0:
            wait("tensor", s_l3, s_l3.v)
        for h in range(2):
            op("tensor", lambda e, h=h: e.matmul(psD[:, h * 512:(h + 1) * 512], icrb[:],
                                                 yr_t[:, h * 512:(h + 1) * 512],
                                                 start=True, stop=False))
            op("tensor", lambda e, h=h: e.matmul(psD[:, h * 512:(h + 1) * 512], icib[:],
                                                 yi_t[:, h * 512:(h + 1) * 512],
                                                 start=False, stop=False))
            op("tensor", lambda e, h=h: e.matmul(psD[:, h * 512:(h + 1) * 512], ppre[:],
                                                 wreb[:, h * 512:(h + 1) * 512],
                                                 start=False, stop=False))
            op("tensor", lambda e, h=h: e.matmul(psD[:, h * 512:(h + 1) * 512], ppim[:],
                                                 wimb[:, h * 512:(h + 1) * 512],
                                                 start=False, stop=True),
               s_cv if h == 1 else None)
        wait("tensor", s_pw, s_pw.v)
        for h in range(2):
            op("tensor", lambda e, h=h, slot=slot: e.matmul(
                psA[0:NST, h * 512:(h + 1) * 512], asre[:, 0:NST],
                xn2b_t[:, slot * DM + h * 512:slot * DM + (h + 1) * 512],
                start=True, stop=True))
            op("tensor", lambda e, h=h, slot=slot: e.matmul(
                psB[0:NST, h * 512:(h + 1) * 512], asim[:, 0:NST],
                xn2b_t[:, slot * DM + h * 512:slot * DM + (h + 1) * 512],
                start=True, stop=True),
               s_xc if h == 1 else None)
        if i % 2 == 1:
            xc_after_pair[24 + i // 2] = s_xc.v
        carry_update(ones_mask[:, 0:1], aq_cols[:, 0:1], aq_cols[:, 1:2])
        if DEBUG == 2:
            wait("vector", s_cv, s_cv.v)
            op("vector", lambda e: e.tensor_copy(gn_t[:, 0:DM], psD[:]), s_rv)
            wait("sync", s_rv, s_rv.v)
            op("sync", lambda e, i=i: e.dma_start(out=dbg_r[i], in_=gn_t[:, 0:DM]), s_res, 16)
            wait("vector", s_res, s_res.v)
        wait("vector", s_cv, s_cv.v)
        op("vector", lambda e: e.tensor_copy(gn_t[:, 0:DM], psD[:]), s_rv)
        wait("scalar", s_rv, s_rv.v)
        ln_stats(lambda: gn_t[:, 0:DM], cols[:, 10:11], cols[:, 11:12], None,
                 sq_t[:, 0:DM], gp_tree=False)
        wait("scalar", s_st, s_st.v)
        op("scalar", lambda e: e.activation(cols[:, 12:13], cols[:, 11:12], AF.Sqrt), s_rs)
        wait("vector", s_rs, s_rs.v)
        if i >= 2:
            wait("vector", s_res, s_res.v - 16)
        op("vector", lambda e: e.reciprocal(cols[:, 13:14], cols[:, 12:13]))
        op("vector", lambda e: e.tensor_scalar(xe_t[:], gn_t[:, 0:DM], cols[:, 10:11],
                                               cols[:, 13:14], ALU.subtract, ALU.mult))
        op("vector", lambda e, buf=buf, slot=slot: e.tensor_add(
            xnew_t[:, buf * DM:(buf + 1) * DM], xe_t[:],
            x_t[:, slot * DM:(slot + 1) * DM]), s_l3)
        l3_after[i] = s_l3.v
        wait("sync", s_l3, s_l3.v)
        op("sync", lambda e, i=i, buf=buf: e.dma_start(
            out=xres_r[i], in_=xnew_t[:, buf * DM:(buf + 1) * DM]), s_res, 16)
        wait("scalar", s_l3, s_l3.v)
        ln_stats(lambda buf=buf: xnew_t[:, buf * DM:(buf + 1) * DM],
                 cols[:, 10:11], cols[:, 11:12], None, sq_t[:, 0:DM], gp_tree=False)
        wait("scalar", s_st, s_st.v)
        op("scalar", lambda e: e.activation(cols[:, 12:13], cols[:, 11:12], AF.Sqrt), s_rs)
        wait("vector", s_rs, s_rs.v)
        op("vector", lambda e: e.reciprocal(cols[:, 13:14], cols[:, 12:13]))
        op("vector", lambda e: e.tensor_scalar(cols[:, 6:7], cols[:, 10:11],
                                               cols[:, 13:14], -1.0, ALU.mult, ALU.mult),
           s_rc)
        wait("scalar", s_rc, s_rc.v)
        if i > 0:
            wait("scalar", s_tp, s_tp.v)
        op("scalar", lambda e, buf=buf: e.activation(
            xn4_t[:], xnew_t[:, buf * DM:(buf + 1) * DM], AF.Identity,
            bias=cols[:, 6:7], scale=cols[:, 13:14]), s_l4)
        if DEBUG == 10:
            wait("gpsimd", s_l4, s_l4.v)
            op("gpsimd", lambda e, i=i: e.dma_start(out=dbg_r[i], in_=xn4_t[:]), s_wdma, 16)
        wait("tensor", s_l4, s_l4.v)
        if i > 0:
            wait("tensor", s_t4, s_t4.v)
        for dc in range(8):
            op("tensor", lambda e, dc=dc: e.transpose(
                psC.bitcast(BF16)[:, dc * P:(dc + 1) * P],
                xn4_t[:, dc * P:(dc + 1) * P], identb[:]),
               s_tp if dc == 7 else None)
        wait("scalar", s_tp, s_tp.v)
        for dc in range(8):
            op("scalar", lambda e, dc=dc, i=i: e.copy(
                xn4T[:, dc * TOK + i * P:dc * TOK + (i + 1) * P],
                psC.bitcast(BF16)[:, dc * P:(dc + 1) * P]),
               s_t4 if dc == 7 else None)

    if STOP == "A":
        wait("sync", s_res, s_res.v)
        op("sync", lambda e: e.dma_start(out=out, in_=xres), s_out, 16)
        wait("sync", s_out, s_out.v)
        with nc.Block() as blk:
            blk.sync(lambda e: [f(e) for f in prog["sync"]])
            blk.tensor(lambda e: [f(e) for f in prog["tensor"]])
            blk.vector(lambda e: [f(e) for f in prog["vector"]])
            blk.scalar(lambda e: [f(e) for f in prog["scalar"]])
            blk.gpsimd(lambda e: [f(e) for f in prog["gpsimd"]])
        return nc

    wr_r = wrout.rearrange("(dc p) e -> p dc e", p=P)
    op("sync", lambda e: e.dma_start(
        out=m_t[0:P, 0:NE * 8].rearrange("p (dc e) -> p dc e", e=NE), in_=wr_r),
       s_x, 16)
    wait("scalar", s_x, s_x.v)
    op("scalar", lambda e: e.copy(wr_sb[:], m_t[0:P, 0:NE * 8]), s_ap)
    op("vector", lambda e: e.memset(ctile[:], 0.0))
    wait("tensor", s_ap, s_ap.v)
    wait("tensor", s_l3, s_l3.v)
    wait("tensor", s_t4, s_t4.v)
    for ti in range(NCH):
        if ti > 0:
            wait("tensor", s_rc2, s_rc2.v)
        for dc in range(8):
            op("tensor", lambda e, ti=ti, dc=dc: e.matmul(
                psD[0:P, 0:NE],
                xn4T[:, dc * TOK + ti * P:dc * TOK + (ti + 1) * P],
                wr_sb[:, dc * NE:(dc + 1) * NE],
                start=(dc == 0), stop=(dc == 7)),
               s_rr if dc == 7 else None)
        wait("vector", s_rr, s_rr.v)
        op("vector", lambda e: e.tensor_copy(lgits[:], psD[0:P, 0:NE]))
        op("vector", lambda e: e.tensor_tensor(rt8[:, 0:4], lgits[:, 0:4],
                                               lgits[:, 4:8], ALU.max))
        op("vector", lambda e: e.tensor_tensor(rt8[:, 0:2], rt8[:, 0:2],
                                               rt8[:, 2:4], ALU.max))
        op("vector", lambda e: e.tensor_tensor(rt8[:, 0:1], rt8[:, 0:1],
                                               rt8[:, 1:2], ALU.max))
        op("vector", lambda e: e.tensor_scalar(cols[:, 1:2], rt8[:, 0:1], -1.0, None,
                                               ALU.mult), s_rv)
        wait("scalar", s_rv, s_rv.v)
        op("scalar", lambda e: e.activation(e8[:], lgits[:], AF.Exp, bias=cols[:, 1:2]),
           s_ap)
        wait("vector", s_ap, s_ap.v)
        op("vector", lambda e: e.tensor_tensor(rt8[:, 4:8], e8[:, 0:4], e8[:, 4:8],
                                               ALU.add))
        op("vector", lambda e: e.tensor_tensor(rt8[:, 4:6], rt8[:, 4:6], rt8[:, 6:8],
                                               ALU.add))
        op("vector", lambda e: e.tensor_tensor(rt8[:, 4:5], rt8[:, 4:5], rt8[:, 5:6],
                                               ALU.add))
        op("vector", lambda e: e.reciprocal(cols[:, 3:4], rt8[:, 4:5]))
        op("vector", lambda e: e.tensor_scalar(cols[:, 4:5], cols[:, 3:4], ALPHA, None,
                                               ALU.mult))
        op("vector", lambda e: e.memset(cols[:, 7:8], 0.0))
        for ex in range(7):
            op("vector", lambda e, ex=ex: e.tensor_scalar(
                cols[:, 6:7], lgits[:, ex:ex + 1], rt8[:, 0:1], None, ALU.is_ge))
            op("vector", lambda e: e.tensor_scalar(cols[:, 8:9], cols[:, 7:8], -1.0, 1.0,
                                                   ALU.mult, ALU.add))
            op("vector", lambda e: e.tensor_mul(cols[:, 6:7], cols[:, 6:7], cols[:, 8:9]))
            op("vector", lambda e, ex=ex: e.tensor_mul(ctile[:, ex:ex + 1],
                                                       cols[:, 6:7], cols[:, 4:5]))
            op("vector", lambda e: e.tensor_tensor(cols[:, 7:8], cols[:, 7:8],
                                                   cols[:, 6:7], ALU.max))
        op("vector", lambda e: e.memset(ctile[:, 7:8], 1.0))
        op("vector", lambda e: e.tensor_copy(ctileb[:], ctile[:]), s_rv)
        wait("tensor", s_rv, s_rv.v)
        op("tensor", lambda e: e.transpose(psD.bitcast(BF16)[0:NE, 0:P], ctileb[:],
                                           identb[:]), s_rt)
        wait("scalar", s_rt, s_rt.v)
        op("scalar", lambda e, ti=ti: e.copy(coef_rows[:, ti * P:(ti + 1) * P],
                                             psD.bitcast(BF16)[0:NE, 0:P]), s_rc2)

    if DEBUG == 11:
        wait("gpsimd", s_t4, s_t4.v)
        for j in range(16):
            op("gpsimd", lambda e, j=j: e.dma_start(
                out=dbg_r[j], in_=xn4T[:, j * DM:(j + 1) * DM]), s_wdma, 16)
    if DEBUG == 3:
        wait("sync", s_rc2, s_rc2.v)
        op("sync", lambda e: e.dma_start(out=dbg_r[0][0:NE, 0:DM],
                                         in_=coef_rows[:, 0:DM]), s_res, 16)
        op("sync", lambda e: e.dma_start(out=dbg_r[1][0:NE, 0:DM],
                                         in_=coef_rows[:, DM:2 * DM]), s_res, 16)
    wait("sync", s_rc2, s_rc2.v)
    op("sync", lambda e: e.dma_start(out=coefd, in_=coef_rows[:]), s_xr, 16)

    if STOP == "R":
        wait("sync", s_res, s_res.v)
        wait("sync", s_rc2, s_rc2.v)
        op("sync", lambda e: e.dma_start(out=out, in_=xres), s_out, 16)
        wait("sync", s_out, s_out.v)
        with nc.Block() as blk:
            blk.sync(lambda e: [f(e) for f in prog["sync"]])
            blk.tensor(lambda e: [f(e) for f in prog["tensor"]])
            blk.vector(lambda e: [f(e) for f in prog["vector"]])
            blk.scalar(lambda e: [f(e) for f in prog["scalar"]])
            blk.gpsimd(lambda e: [f(e) for f in prog["gpsimd"]])
        return nc

    for _g in reversed(_a_guards):
        _g.__exit__(None, None, None)
    wu_buf = S("wu_buf", [P, 3 * 2048], BF16)
    wg_buf = S("wg_buf", [P, 3 * 2048], BF16)
    wd_buf = S("wd_buf", [P, 3 * 2048], BF16)
    xs = S("xs", [P, 8 * TH], BF16)
    h_half = S("h_half", [P, 16 * TH], BF16)
    accT = S("accT", [P, 8 * TH], F32)
    coefb = S("coefb", [P, TH], BF16)
    sg_t = S("sg_t", [P, 2 * TC], F32)
    sq5_t = S("sq5_t", [P, DM], F32)
    pc_t = S("pc_t", [P, DM], F32)
    out_t = S("out_t", [P, 2 * DM], F32)
    xr_t = S("xr_t", [P, 2 * DM], F32)

    wup_r = wup.rearrange("e (dc p) h -> e p dc h", p=P)
    wgate_r = wgate.rearrange("e (dc p) h -> e p dc h", p=P)
    wdown_r = wdown.rearrange("e (hc p) d -> e p hc d", p=P)
    wu_v = wu_buf.rearrange("p (b dc h) -> p b dc h", b=3, h=256)
    wg_v = wg_buf.rearrange("p (b dc h) -> p b dc h", b=3, h=256)
    wd_v = wd_buf.rearrange("p (b hc dl) -> p b hc dl", b=3, dl=P)

    wait("gpsimd", s_rc2, s_rc2.v)
    last_ug_of_buf = {0: 0, 1: 0, 2: 0}
    last_dn_of_buf = {0: 0, 1: 0, 2: 0}

    def ffn_expert(th, e, first):
        wait("gpsimd", s_xs, s_xs.v)
        if first:
            wait("gpsimd", s_xr, s_xr.v)
        op("gpsimd", lambda e_, e=e, th=th: e_.dma_start(
            out=coefb[:],
            in_=coefd[e:e + 1, th * TH:(th + 1) * TH].broadcast_to([P, TH])),
           s_cb, 16)
        wait("vector", s_cb, s_cb.v)
        if not first:
            wait("vector", s_ug, s_ug.v)
        for dc in range(8):
            op("vector", lambda e_, dc=dc, th=th: e_.tensor_mul(
                xs[:, dc * TH:(dc + 1) * TH],
                xn4T[:, dc * TOK + th * TH:dc * TOK + (th + 1) * TH], coefb[:]),
               s_xs if dc == 7 else None)
        for hh in range(2):
            for hg in range(8):
                gidx = hh * 8 + hg
                b = gidx % 3
                if last_ug_of_buf[b]:
                    wait("gpsimd", s_ug, last_ug_of_buf[b])
                h0 = (hh * 16 + hg * 2) * P
                op("gpsimd", lambda e_, e=e, h0=h0, b=b: e_.dma_start(
                    out=wu_v[:, b], in_=wup_r[e][:, :, h0:h0 + 256]), s_wdma, 16)
                op("gpsimd", lambda e_, e=e, h0=h0, b=b: e_.dma_start(
                    out=wg_v[:, b], in_=wgate_r[e][:, :, h0:h0 + 256]), s_wdma, 16)
                wait("tensor", s_wdma, s_wdma.v)
                if hg == 0:
                    wait("tensor", s_ac, s_ac.v)
                    if hh == 0:
                        wait("tensor", s_xs, s_xs.v)
                        wait("tensor", s_e5, s_e5.v)
                for hl in range(2):
                    hc = hh * 16 + hg * 2 + hl
                    hcl = hg * 2 + hl
                    pu, pg = (psA, psB) if hc % 2 == 0 else (psC, psD)
                    if s_hm.v >= 2:
                        wait("tensor", s_hm, s_hm.v - 1)
                    if s_sg.v >= 2:
                        wait("tensor", s_sg, s_sg.v - 1)
                    for tcol in range(2):
                        for dc in range(8):
                            op("tensor", lambda e_, b=b, hl=hl, dc=dc, tcol=tcol,
                               pu=pu: e_.matmul(
                                pu[:, tcol * 512:(tcol + 1) * 512],
                                wu_v[:, b, dc, hl * P:(hl + 1) * P],
                                xs[:, dc * TH + tcol * 512:dc * TH + (tcol + 1) * 512],
                                start=(dc == 0), stop=(dc == 7)))
                        for dc in range(8):
                            op("tensor", lambda e_, b=b, hl=hl, dc=dc, tcol=tcol,
                               pg=pg, th=th: e_.matmul(
                                pg[:, tcol * 512:(tcol + 1) * 512],
                                wg_v[:, b, dc, hl * P:(hl + 1) * P],
                                xn4T[:, dc * TOK + th * TH + tcol * 512:
                                     dc * TOK + th * TH + (tcol + 1) * 512],
                                start=(dc == 0), stop=(dc == 7)),
                               s_ug if (dc == 7 and tcol == 1) else None)
                    wait("scalar", s_ug, s_ug.v)
                    for tcol in range(2):
                        op("scalar", lambda e_, pg=pg, tcol=tcol: e_.activation(
                            sg_t[:, tcol * TC:(tcol + 1) * TC],
                            pg[:, tcol * 512:(tcol + 1) * 512], AF.Silu),
                           s_sg if tcol == 1 else None)
                    wait("vector", s_sg, s_sg.v)
                    for tcol in range(2):
                        op("vector", lambda e_, pu=pu, hcl=hcl, tcol=tcol: e_.tensor_mul(
                            h_half[:, hcl * TH + tcol * 512:hcl * TH + (tcol + 1) * 512],
                            pu[:, tcol * 512:(tcol + 1) * 512],
                            sg_t[:, tcol * TC:(tcol + 1) * TC]),
                           s_hm if tcol == 1 else None)
                last_ug_of_buf[b] = s_ug.v
            for dc in range(8):
                b = dc % 3
                if last_dn_of_buf[b]:
                    wait("gpsimd", s_dn, last_dn_of_buf[b])
                op("gpsimd", lambda e_, e=e, hh=hh, dc=dc, b=b: e_.dma_start(
                    out=wd_v[:, b],
                    in_=wdown_r[e][:, hh * 16:(hh + 1) * 16, dc * P:(dc + 1) * P]),
                   s_wdma, 16)
                wait("tensor", s_wdma, s_wdma.v)
                pd = psA if dc % 2 == 0 else psB
                if dc == 0:
                    wait("tensor", s_hm, s_hm.v)
                    wait("tensor", s_sg, s_sg.v)
                elif s_ac.v >= 2:
                    wait("tensor", s_ac, s_ac.v - 1)
                for tcol in range(2):
                    for hcl in range(16):
                        op("tensor", lambda e_, b=b, hcl=hcl, tcol=tcol, pd=pd: e_.matmul(
                            pd[:, tcol * 512:(tcol + 1) * 512],
                            wd_v[:, b, hcl, 0:P],
                            h_half[:, hcl * TH + tcol * 512:hcl * TH + (tcol + 1) * 512],
                            start=(hcl == 0), stop=(hcl == 15)),
                           s_dn if (hcl == 15 and tcol == 1) else None)
                last_dn_of_buf[b] = s_dn.v
                wait("vector", s_dn, s_dn.v)
                if first and hh == 0 and dc == 0:
                    wait("vector", s_am, s_am.v)
                for tcol in range(2):
                    op("vector", lambda e_, dc=dc, tcol=tcol, pd=pd: e_.tensor_tensor(
                        accT[:, dc * TH + tcol * 512:dc * TH + (tcol + 1) * 512],
                        accT[:, dc * TH + tcol * 512:dc * TH + (tcol + 1) * 512],
                        pd[:, tcol * 512:(tcol + 1) * 512], ALU.add),
                       s_ac if tcol == 1 else None)

    def phase_c(th):
        for tt in range(TH // P):
            ti = th * (TH // P) + tt
            buf = tt % 2
            wait("sync", s_l3, s_l3.v)
            op("sync", lambda e_, ti=ti, buf=buf: e_.dma_start(
                out=xr_t[:, buf * DM:(buf + 1) * DM], in_=xres_r[ti]), s_xr, 16)
            pt = psC if tt % 2 == 0 else psD
            if tt == 0:
                wait("tensor", s_ac, s_ac.v)
                wait("tensor", s_sg, s_sg.v)
                wait("tensor", s_hm, s_hm.v)
            elif s_e5.v >= 2:
                wait("tensor", s_e5, s_e5.v - 1)
            for dc in range(8):
                op("tensor", lambda e_, dc=dc, tt=tt, pt=pt: e_.transpose(
                    pt[:, dc * P:(dc + 1) * P],
                    accT[:, dc * TH + tt * P:dc * TH + (tt + 1) * P], identf[:]),
                   s_c5 if dc == 7 else None)
            wait("vector", s_c5, s_c5.v)
            op("vector", lambda e_, pt=pt: e_.tensor_copy(pc_t[:], pt[:]), s_rv)
            wait("scalar", s_rv, s_rv.v)
            ln_stats(lambda: pc_t[:], cols[:, 10:11], cols[:, 11:12], None, sq5_t,
                     gp_tree=False)
            wait("scalar", s_st, s_st.v)
            op("scalar", lambda e_: e_.activation(cols[:, 12:13], cols[:, 11:12],
                                                  AF.Sqrt), s_rs)
            wait("vector", s_rs, s_rs.v)
            wait("vector", s_xr, s_xr.v)
            if s_out.v >= 32:
                wait("vector", s_out, s_out.v - 16)
            op("vector", lambda e_: e_.reciprocal(cols[:, 13:14], cols[:, 12:13]))
            op("vector", lambda e_: e_.tensor_scalar(
                xe_t[:], pc_t[:], cols[:, 10:11], cols[:, 13:14],
                ALU.subtract, ALU.mult), s_e5)
            op("vector", lambda e_, buf=buf: e_.tensor_add(
                out_t[:, buf * DM:(buf + 1) * DM], xe_t[:],
                xr_t[:, buf * DM:(buf + 1) * DM]), s_l3)
            wait("sync", s_l3, s_l3.v)
            op("sync", lambda e_, ti=ti, buf=buf: e_.dma_start(
                out=out_r[ti], in_=out_t[:, buf * DM:(buf + 1) * DM]), s_out, 16)

    if STOP == "B":
        wait("vector", s_c5, s_c5.v)
        op("vector", lambda e_: e_.memset(accT[:], 0.0), s_am)
        ffn_expert(0, 0, first=True)
        phase_c(0)
        wait("sync", s_res, s_res.v)
        op("sync", lambda e: e.dma_start(out=out.rearrange("(h t) d -> h t d", h=2)[1],
                                         in_=xres.rearrange("(h t) d -> h t d", h=2)[1]),
           s_out, 16)
    else:
        for th in range(NTH):
            wait("vector", s_c5, s_c5.v)
            op("vector", lambda e_: e_.memset(accT[:], 0.0), s_am)
            for ei in range(NE):
                ffn_expert(th, ei, first=(th == 0 and ei == 0))
            phase_c(th)

    wait("sync", s_out, s_out.v)
    wait("sync", s_res, s_res.v)

    with nc.Block() as blk:
        blk.sync(lambda e: [f(e) for f in prog["sync"]])
        blk.tensor(lambda e: [f(e) for f in prog["tensor"]])
        blk.vector(lambda e: [f(e) for f in prog["vector"]])
        blk.scalar(lambda e: [f(e) for f in prog["scalar"]])
        blk.gpsimd(lambda e: [f(e) for f in prog["gpsimd"]])
    return nc


def kernel(L0_attn_x, L0_attn_gamma, L0_attn_beta, L0_attn_LR, L0_attn_LI,
           L0_attn_CR, L0_attn_CI, L0_attn_D, L0_ffn_w_router, L0_ffn_w_up,
           L0_ffn_w_gate, L0_ffn_w_down, L0_ffn_ws_up, L0_ffn_ws_gate,
           L0_ffn_ws_down):
    x = np.ascontiguousarray(np.asarray(L0_attn_x, dtype=np.float32)).reshape(
        NB * LSEQ, DM)
    wupc = np.ascontiguousarray(np.concatenate(
        [np.asarray(L0_ffn_w_up, np.float32),
         np.asarray(L0_ffn_ws_up, np.float32)[None]], axis=0))
    wgatec = np.ascontiguousarray(np.concatenate(
        [np.asarray(L0_ffn_w_gate, np.float32),
         np.asarray(L0_ffn_ws_gate, np.float32)[None]], axis=0))
    wdownc = np.ascontiguousarray(np.concatenate(
        [np.asarray(L0_ffn_w_down, np.float32),
         np.asarray(L0_ffn_ws_down, np.float32)[None]], axis=0))

    nc = bass.Bass("TRN2", target_bir_lowering=False, detect_race_conditions=False)
    build(nc)

    in_maps = []
    for c in range(8):
        q = c % 4
        bstart = (c // 4) * LSEQ
        xov = x[bstart + q * TOK: bstart + (q + 1) * TOK]
        xpv = np.zeros((PRETOK, DM), np.float32)
        if q > 0:
            xpv[PRETOK - q * TOK:] = x[bstart: bstart + q * TOK]
        pm = np.zeros((NST, PREG), np.float32)
        ng = q * TOK // 512
        if ng > 0:
            pm[:, PREG - ng:] = 1.0
        in_maps.append({
            "xo": np.ascontiguousarray(xov),
            "xp": xpv,
            "pmask": pm,
            "lr": np.asarray(L0_attn_LR, np.float32).reshape(NST, 1),
            "li": np.asarray(L0_attn_LI, np.float32).reshape(NST, 1),
            "cr": np.ascontiguousarray(np.asarray(L0_attn_CR, np.float32)),
            "ci": np.ascontiguousarray(np.asarray(L0_attn_CI, np.float32)),
            "dvec": np.asarray(L0_attn_D, np.float32).reshape(1, DM),
            "gamma": np.asarray(L0_attn_gamma, np.float32).reshape(1, DM),
            "beta": np.asarray(L0_attn_beta, np.float32).reshape(1, DM),
            "wrout": np.ascontiguousarray(np.asarray(L0_ffn_w_router, np.float32)),
            "wup": wupc, "wgate": wgatec, "wdown": wdownc,
        })

    res = run_bass_kernel_spmd(nc, in_maps, core_ids=list(range(8)))
    full = np.concatenate([res.results[c]["out"] for c in range(8)], axis=0)
    return full.reshape(NB, LSEQ, DM)


# version 14
# speedup vs baseline: 1.1388x; 1.0027x over previous
import math
import numpy as np
import concourse.bass as bass
import concourse.mybir as mybir
from concourse.bass_utils import run_bass_kernel_spmd

F32 = mybir.dt.float32
BF16 = mybir.dt.bfloat16
U32 = mybir.dt.uint32
AF = mybir.ActivationFunctionType
ALU = mybir.AluOpType
AXX = mybir.AxisListType.X
AX = mybir.AxisListType

P = 128
DM = 1024
HID = 4096
NE = 8
NST = 64
LSEQ = 8192
NB = 2
TOK = 2048
NCH = 16
PRETOK = 6144
PREG = 12
GSUB = 4
TH = 1024
NTH = 2
TC = 512
EPS = 1e-5
ALPHA = 2.0


def _host_consts():
    c = {}
    tl = np.arange(P, dtype=np.float64)
    f = np.arange(P, dtype=np.float64)
    ang = 2.0 * np.pi * np.outer(tl, f) / 256.0
    sgn = ((-1.0) ** f)[None, :]
    Fr = sgn * np.cos(ang)
    Fi = -sgn * np.sin(ang)
    Fi[:, 0] = (-1.0) ** tl
    c["Fr"], c["Fi"] = Fr, Fi
    KFr = np.cos(ang)
    KFi = -np.sin(ang)
    KFi[:, 0] = (-1.0) ** tl
    c["KFr"], c["KFi"] = KFr, KFi
    w = np.full(P, 2.0 / 256.0)
    w[0] = 1.0 / 256.0
    ICr = (w * ((-1.0) ** f))[:, None] * np.cos(ang.T)
    ICi = -(w * ((-1.0) ** f))[:, None] * np.sin(ang.T)
    ICi[0, :] = (1.0 / 256.0) * ((-1.0) ** tl)
    c["ICr"], c["ICi"] = ICr, ICi
    c["ident"] = np.eye(P)
    c["rev"] = np.eye(P)[::-1].copy()
    c["ones_row"] = np.ones((1, P))
    return {k: np.ascontiguousarray(v, dtype=np.float32) for k, v in c.items()}


class _Sem:
    def __init__(self, h):
        self.h = h
        self.v = 0

    def inc(self, n=1):
        self.v += n
        return self.v


_DTSZ = {"dt.float32": 4, "dt.int32": 4, "dt.uint32": 4, "dt.bfloat16": 2,
         "dt.float16": 2, "dt.uint8": 1, "dt.int8": 1}


def _ap_rect(x):
    if type(x).__name__ != "AP":
        return None
    try:
        sp = str(x.space)
        if ("SBUF" not in sp) and ("PSUM" not in sp):
            return None
        h = x.tensor
        hshape = list(h.shape)
        wb = int(hshape[-1]) * _DTSZ.get(str(h.dtype), 4)
        asz = _DTSZ.get(str(x.dtype), 4)
        wc = max(1, wb // asz)
        off = int(x.offset)
        p0 = off // wc
        f0 = off % wc
        shp = list(x.shape)
        if len(shp) == 2:
            pn, fn_ = shp
        elif len(shp) == 1:
            pn, fn_ = 1, shp[0]
        else:
            pn = shp[0]
            fn_ = 1
            for s_ in shp[1:]:
                fn_ *= s_
        return (x.name, p0, p0 + pn, f0 * asz, (f0 + fn_) * asz)
    except Exception:
        try:
            return (x.name, 0, 1 << 20, 0, 1 << 40)
        except Exception:
            return None


def _olap(a, b):
    return (a[0] == b[0] and a[1] < b[2] and b[1] < a[2]
            and a[3] < b[4] and b[3] < a[4])


class _RecDummy:
    def __getattr__(self, n):
        return lambda *a, **k: self


class _Rec:
    def __init__(self):
        self.calls = 0
        self.reads = []
        self.writes = []
        self.names = []
        self._dummy = _RecDummy()

    def __getattr__(self, name):
        def f(*a, **k):
            self.calls += 1
            self.names.append(name)
            vals = list(a) + [v for kk, v in k.items() if kk != "out"]
            first = k["out"] if "out" in k else None
            for v in vals:
                r = _ap_rect(v)
                if r is None:
                    continue
                if first is None:
                    first = v
                    self.writes.append(r)
                else:
                    self.reads.append(r)
            if "out" in k:
                r = _ap_rect(k["out"])
                if r is not None:
                    self.writes.append(r)
            return self._dummy
        return f


DEBUG = 0
STOP = ""


def build(nc):
    con = _host_consts()

    xo = nc.dram_tensor("xo", [TOK, DM], F32, kind="ExternalInput").ap()
    xp = nc.dram_tensor("xp", [PRETOK, DM], F32, kind="ExternalInput").ap()
    pmask = nc.dram_tensor("pmask", [NST, PREG], F32, kind="ExternalInput").ap()
    lr = nc.dram_tensor("lr", [NST, 1], F32, kind="ExternalInput").ap()
    li = nc.dram_tensor("li", [NST, 1], F32, kind="ExternalInput").ap()
    cr = nc.dram_tensor("cr", [DM, NST], F32, kind="ExternalInput").ap()
    ci = nc.dram_tensor("ci", [DM, NST], F32, kind="ExternalInput").ap()
    dvec = nc.dram_tensor("dvec", [1, DM], F32, kind="ExternalInput").ap()
    gamma = nc.dram_tensor("gamma", [1, DM], F32, kind="ExternalInput").ap()
    beta = nc.dram_tensor("beta", [1, DM], F32, kind="ExternalInput").ap()
    wrout = nc.dram_tensor("wrout", [DM, NE], F32, kind="ExternalInput").ap()
    wup = nc.dram_tensor("wup", [NE, DM, HID], F32, kind="ExternalInput").ap()
    wgate = nc.dram_tensor("wgate", [NE, DM, HID], F32, kind="ExternalInput").ap()
    wdown = nc.dram_tensor("wdown", [NE, HID, DM], F32, kind="ExternalInput").ap()
    out = nc.dram_tensor("out", [TOK, DM], F32, kind="ExternalOutput").ap()
    xres = nc.dram_tensor("xres", [TOK, DM], F32,
                          kind="ExternalOutput" if DEBUG else "Internal").ap()
    coefd = nc.dram_tensor("coefd", [NE, TOK], F32, kind="Internal").ap()
    dbg = (nc.dram_tensor("dbg", [TOK, DM], F32, kind="ExternalOutput").ap()
           if DEBUG else None)
    dbg_r = dbg.rearrange("(t p) d -> t p d", p=P) if DEBUG else None

    cst = {k: nc.inline_tensor(v, name="c_" + k).ap() for k, v in con.items()}

    def S(name, shape, dtype):
        return nc.alloc_sbuf_tensor(name, shape, dtype)

    _a_guards = []

    def SA(name, shape, dtype):
        g = nc.sbuf_tensor(name, shape, dtype)
        h = g.__enter__()
        _a_guards.append(g)
        return h

    xn4T = S("xn4T", [P, 8 * TOK], BF16)
    coef_rows = S("coef_rows", [NE, TOK], F32)
    identb = S("identb", [P, P], BF16)
    identf = S("identf", [P, P], F32)
    ones_colf = S("ones_colf", [1, P], F32)
    stat = S("stat", [P, 16], F32)
    cols = S("cols", [P, 16], F32)
    wr_sb = S("wr_sb", [P, NE * 8], BF16)
    xe_t = S("xe_t", [P, DM], F32)
    lgits = S("lgits", [P, NE], F32)
    e8 = S("e8", [P, NE], F32)
    i8u = S("i8u", [P, NE], U32)
    v8 = S("v8", [P, NE], F32)
    ctile = S("ctile", [P, NE], F32)
    ctileb = S("ctileb", [P, NE], BF16)
    rt8 = S("rt8", [P, NE], F32)
    junkc = S("junkc", [P, 4], F32)
    carry_re = SA("carry_re", [NST, DM], F32)
    carry_im = SA("carry_im", [NST, DM], F32)
    cs_re = SA("cs_re", [NST, DM], F32)
    cs_im = SA("cs_im", [NST, DM], F32)
    csreb = SA("csreb", [NST, DM], BF16)
    csimb = SA("csimb", [NST, DM], BF16)
    kfr = SA("kfr", [P, DM], F32)
    kfi = SA("kfi", [P, DM], F32)
    kfr2 = SA("kfr2", [P, DM], F32)
    gammab = SA("gammab", [P, DM], F32)
    betab = SA("betab", [P, DM], F32)
    frb = SA("frb", [P, P], BF16)
    fib = SA("fib", [P, P], BF16)
    icrb = SA("icrb", [P, P], BF16)
    icib = SA("icib", [P, P], BF16)
    kfrb_c = SA("kfrb_c", [P, P], BF16)
    kfib_c = SA("kfib_c", [P, P], BF16)
    revb = SA("revb", [P, P], BF16)
    asre = SA("asre", [P, NST * 5], BF16)
    asim = SA("asim", [P, NST * 5], BF16)
    ppre = SA("ppre", [NST, P], BF16)
    ppim = SA("ppim", [NST, P], BF16)
    pkre = SA("pkre", [NST, P], BF16)
    pkim = SA("pkim", [NST, P], BF16)
    kb = SA("kb", [P, DM], BF16)
    aq_cols = SA("aq_cols", [NST, 4], F32)
    pmask_sb = SA("pmask_sb", [NST, PREG], F32)
    ones_mask = SA("ones_mask", [NST, 1], F32)
    pw_re = SA("pw_re", [NST, P], F32)
    pw_im = SA("pw_im", [NST, P], F32)
    sc64 = SA("sc64", [NST, 16], F32)
    sci = SA("sci", [NST, 1], mybir.dt.int32)
    powtab = SA("powtab", [NST, 8], F32)
    powrow = SA("powrow", [8, NST], F32)
    x_t = SA("x_t", [P, 4 * DM], F32)
    xn1_t = SA("xn1_t", [P, 2 * DM], F32)
    xn2a_t = SA("xn2a_t", [P, 2 * DM], F32)
    gn_t = SA("gn_t", [P, 2 * DM], F32)
    sq_t = SA("sq_t", [P, 2 * DM], F32)
    xn2b_t = SA("xn2b_t", [P, 4 * DM], BF16)
    m_t = SA("m_t", [P, 4 * DM], F32)
    yr_t = SA("yr_t", [P, DM], BF16)
    yi_t = SA("yi_t", [P, DM], BF16)
    w_re = SA("w_re", [NST, DM], F32)
    w_im = SA("w_im", [NST, DM], F32)
    wreb = SA("wreb", [NST, DM], BF16)
    wimb = SA("wimb", [NST, DM], BF16)
    wtmp = SA("wtmp", [NST, 2 * DM], F32)
    ctmp = SA("ctmp", [NST, 2 * DM], F32)
    xnew_t = SA("xnew_t", [P, 2 * DM], F32)
    xn4_t = SA("xn4_t", [P, DM], BF16)

    psA = nc.alloc_psum_tensor("psA", [P, DM], F32)
    psB = nc.alloc_psum_tensor("psB", [P, DM], F32)
    psC = nc.alloc_psum_tensor("psC", [P, DM], F32)
    psD = nc.alloc_psum_tensor("psD", [P, DM], F32)

    sems = {}

    def mksem(name):
        s = _Sem(nc.alloc_semaphore(name))
        sems[name] = s
        return s

    s_x = mksem("s_x")
    s_st = mksem("s_st")
    s_rs = mksem("s_rs")
    s_rc = mksem("s_rc")
    s_ap = mksem("s_ap")
    s_ln = mksem("s_ln")
    s_fft = mksem("s_fft")
    s_pw = mksem("s_pw")
    s_yy = mksem("s_yy")
    s_wl = mksem("s_wl")
    s_wg = mksem("s_wg")
    s_ys = mksem("s_ys")
    s_cv = mksem("s_cv")
    s_xc = mksem("s_xc")
    s_cu = mksem("s_cu")
    s_l3 = mksem("s_l3")
    s_l4 = mksem("s_l4")
    s_tp = mksem("s_tp")
    s_t4 = mksem("s_t4")
    s_rr = mksem("s_rr")
    s_rv = mksem("s_rv")
    s_rt = mksem("s_rt")
    s_rc2 = mksem("s_rc2")
    s_cfd = mksem("s_cfd")
    s_wdma = mksem("s_wdma")
    s_cb = mksem("s_cb")
    s_xs = mksem("s_xs")
    s_ug = mksem("s_ug")
    s_sg = mksem("s_sg")
    s_hm = mksem("s_hm")
    s_dn = mksem("s_dn")
    s_ac = mksem("s_ac")
    s_am = mksem("s_am")
    s_c5 = mksem("s_c5")
    s_e5 = mksem("s_e5")
    s_xr = mksem("s_xr")
    s_res = mksem("s_res")
    s_out = mksem("s_out")

    prog = {k: [] for k in ("sync", "tensor", "vector", "scalar", "gpsimd")}

    def em(eng, fn):
        prog[eng].append(fn)

    def wait(eng, sem, v):
        em(eng, lambda e, s=sem, v=v: e.wait_ge(s.h, v))

    _vhist = []
    _haz_sites = {}
    _n_spacers = [0]

    def _v_hazard(reads):
        between = 0
        for writes, real in reversed(_vhist):
            if between >= 1:
                break
            if any(_olap(r, w) for r in reads for w in writes):
                return True
            if real:
                between += 1
        return False

    def _v_push(writes, real):
        _vhist.append((writes, real))
        if len(_vhist) > 6:
            _vhist.pop(0)

    def op(eng, fn, sem=None, n=1):
        if eng == "vector":
            rec = _Rec()
            try:
                fn(rec)
            except Exception as ex:
                print("RAW-checker: recorder failed:", ex)
                rec.reads = rec.writes = [(nm, 0, 1 << 20, 0, 1 << 40)
                                          for nm in ("x_t", "cols", "ctmp")]
            if _v_hazard(rec.reads):
                import traceback
                ln = traceback.extract_stack()[-2].lineno
                _haz_sites[ln] = _haz_sites.get(ln, 0) + 1
                _n_spacers[0] += 1
                em("vector", lambda e: e.tensor_copy(junkc[:, 0:1], junkc[:, 2:3]))
                _v_push([("junkc", 0, P, 0, 4)], True)
            _v_push(rec.writes,
                    bool(rec.names) and any(nm != "memset" for nm in rec.names))
        if sem is None:
            em(eng, lambda e, f=fn: f(e))
        else:
            sem.inc(n)
            em(eng, lambda e, f=fn, s=sem, n=n: f(e).then_inc(s.h, n))

    build.haz_sites = _haz_sites
    build.n_spacers = _n_spacers

    op("vector", lambda e: e.memset(junkc[:], 0.0))

    for dst, src in [
        (sc64[:, 0:1], lr), (sc64[:, 1:2], li), (pmask_sb[:], pmask),
        (gammab[0:1, 0:DM], gamma), (betab[0:1, 0:DM], beta),
        (gn_t[0:1, DM:2 * DM], dvec), (ones_colf[0:1, 0:P], cst["ones_row"]),
        (identf[:], cst["ident"]),
        (m_t[0:P, 4 * P:5 * P], cst["Fr"]), (m_t[0:P, 5 * P:6 * P], cst["Fi"]),
        (m_t[0:P, 6 * P:7 * P], cst["ICr"]), (m_t[0:P, 7 * P:8 * P], cst["ICi"]),
        (m_t[0:P, 0:P], cst["KFr"]), (m_t[0:P, P:2 * P], cst["KFi"]),
        (m_t[0:P, 2 * P:3 * P], cst["rev"]),
    ]:
        op("sync", lambda e, d=dst, s=src: e.dma_start(out=d, in_=s), s_x, 16)
    v_ld0 = s_x.v

    wait("scalar", s_x, v_ld0)
    op("scalar", lambda e: e.copy(frb[:], m_t[0:P, 4 * P:5 * P]))
    op("scalar", lambda e: e.copy(fib[:], m_t[0:P, 5 * P:6 * P]))
    op("scalar", lambda e: e.copy(icrb[:], m_t[0:P, 6 * P:7 * P]))
    op("scalar", lambda e: e.copy(icib[:], m_t[0:P, 7 * P:8 * P]))
    op("scalar", lambda e: e.copy(kfrb_c[:], m_t[0:P, 0:P]))
    op("scalar", lambda e: e.copy(kfib_c[:], m_t[0:P, P:2 * P]))
    op("scalar", lambda e: e.copy(revb[:], m_t[0:P, 2 * P:3 * P]))
    op("scalar", lambda e: e.copy(identb[:], identf[:]), s_ap)

    wait("tensor", s_x, v_ld0)
    for ps, row, off in [(psA, gammab, 0), (psB, betab, 0), (psC, gn_t, DM)]:
        for h in range(2):
            op("tensor", lambda e, ps=ps, row=row, h=h, off=off: e.matmul(
                ps[:, h * 512:(h + 1) * 512], ones_colf[0:1, :],
                row[0:1, off + h * 512:off + (h + 1) * 512], start=True, stop=True),
               s_rr if (ps is psC and h == 1) else None)
    wait("scalar", s_rr, s_rr.v)
    op("scalar", lambda e: e.copy(gammab[:], psA[:]))
    op("scalar", lambda e: e.copy(betab[:], psB[:]))
    op("scalar", lambda e: e.copy(gn_t[:, DM:2 * DM], psC[:]), s_ap)

    wait("scalar", s_x, v_ld0)
    op("scalar", lambda e: e.activation(sc64[:, 2:3], sc64[:, 0:1], AF.Exp))
    op("scalar", lambda e: e.activation(sc64[:, 3:4], sc64[:, 1:2], AF.Exp), s_ap)
    wait("vector", s_ap, s_ap.v)
    op("vector", lambda e: e.tensor_scalar(sc64[:, 14:15], sc64[:, 2:3], -1.0, None, ALU.mult), s_rv)
    wait("scalar", s_rv, s_rv.v)
    op("scalar", lambda e: e.activation(sc64[:, 6:7], sc64[:, 14:15], AF.Exp), s_ap)
    twopi = 2.0 * math.pi
    wait("vector", s_ap, s_ap.v)
    op("vector", lambda e: e.tensor_scalar(sc64[:, 7:8], sc64[:, 3:4], 1.0 / twopi,
                                           None, ALU.mult))
    op("vector", lambda e: e.tensor_copy(sci[:], sc64[:, 7:8]))
    op("vector", lambda e: e.tensor_copy(sc64[:, 10:11], sci[:]))
    op("vector", lambda e: e.scalar_tensor_tensor(sc64[:, 7:8], sc64[:, 10:11], -twopi,
                                                  sc64[:, 3:4], ALU.mult, ALU.add))
    op("vector", lambda e: e.tensor_single_scalar(sc64[:, 10:11], sc64[:, 7:8], math.pi,
                                                  ALU.is_gt))
    op("vector", lambda e: e.scalar_tensor_tensor(sc64[:, 7:8], sc64[:, 10:11], -twopi,
                                                  sc64[:, 7:8], ALU.mult, ALU.add))
    op("vector", lambda e: e.tensor_single_scalar(sc64[:, 10:11], sc64[:, 7:8], -math.pi,
                                                  ALU.is_lt))
    op("vector", lambda e: e.scalar_tensor_tensor(sc64[:, 7:8], sc64[:, 10:11], twopi,
                                                  sc64[:, 7:8], ALU.mult, ALU.add))
    op("vector", lambda e: e.tensor_scalar(sc64[:, 11:12], sc64[:, 7:8], -1.0, None,
                                           ALU.mult))
    op("vector", lambda e: e.tensor_tensor(sc64[:, 11:12], sc64[:, 11:12], sc64[:, 7:8],
                                           ALU.max))
    op("vector", lambda e: e.tensor_scalar(sc64[:, 11:12], sc64[:, 11:12], -1.0,
                                           math.pi / 2.0, ALU.mult, ALU.add), s_rv)
    wait("scalar", s_rv, s_rv.v)
    op("scalar", lambda e: e.activation(sc64[:, 9:10], sc64[:, 7:8], AF.Sin))
    op("scalar", lambda e: e.activation(sc64[:, 8:9], sc64[:, 11:12], AF.Sin), s_ap)
    wait("vector", s_ap, s_ap.v)
    op("vector", lambda e: e.tensor_mul(sc64[:, 4:5], sc64[:, 6:7], sc64[:, 8:9]))
    op("vector", lambda e: e.tensor_mul(sc64[:, 5:6], sc64[:, 6:7], sc64[:, 9:10]))

    op("vector", lambda e: e.memset(pw_re[:, 0:1], 1.0))
    op("vector", lambda e: e.memset(pw_im[:, 0:1], 0.0))
    op("vector", lambda e: e.tensor_copy(sc64[:, 12:13], sc64[:, 4:5]))
    op("vector", lambda e: e.tensor_copy(sc64[:, 13:14], sc64[:, 5:6]))

    def csq():
        op("vector", lambda e: e.tensor_mul(sc64[:, 14:15], sc64[:, 12:13], sc64[:, 12:13]))
        op("vector", lambda e: e.tensor_mul(sc64[:, 15:16], sc64[:, 13:14], sc64[:, 13:14]))
        op("vector", lambda e: e.tensor_mul(sc64[:, 13:14], sc64[:, 12:13], sc64[:, 13:14]))
        op("vector", lambda e: e.tensor_scalar(sc64[:, 13:14], sc64[:, 13:14], 2.0, None, ALU.mult))
        op("vector", lambda e: e.tensor_sub(sc64[:, 12:13], sc64[:, 14:15], sc64[:, 15:16]))

    b = 1
    while b < P:
        op("vector", lambda e, b=b: e.tensor_scalar(ctmp[0:NST, 0:b], pw_re[:, 0:b],
                                                    sc64[:, 12:13], None, ALU.mult))
        op("vector", lambda e, b=b: e.tensor_scalar(ctmp[0:NST, DM:DM + b], pw_im[:, 0:b],
                                                    sc64[:, 13:14], None, ALU.mult))
        op("vector", lambda e, b=b: e.tensor_sub(pw_re[:, b:2 * b], ctmp[0:NST, 0:b],
                                                 ctmp[0:NST, DM:DM + b]))
        op("vector", lambda e, b=b: e.tensor_scalar(ctmp[0:NST, 0:b], pw_re[:, 0:b],
                                                    sc64[:, 13:14], None, ALU.mult))
        op("vector", lambda e, b=b: e.tensor_scalar(ctmp[0:NST, DM:DM + b], pw_im[:, 0:b],
                                                    sc64[:, 12:13], None, ALU.mult))
        op("vector", lambda e, b=b: e.tensor_add(pw_im[:, b:2 * b], ctmp[0:NST, 0:b],
                                                 ctmp[0:NST, DM:DM + b]))
        csq()
        b *= 2
    op("vector", lambda e: e.tensor_copy(aq_cols[:, 0:1], sc64[:, 12:13]))
    op("vector", lambda e: e.tensor_copy(aq_cols[:, 1:2], sc64[:, 13:14]))
    csq()
    csq()
    op("vector", lambda e: e.tensor_copy(aq_cols[:, 2:3], sc64[:, 12:13]))
    op("vector", lambda e: e.tensor_copy(aq_cols[:, 3:4], sc64[:, 13:14]))
    op("vector", lambda e: e.memset(ones_mask[:], 1.0))

    op("vector", lambda e: e.memset(powtab[:, 0:1], 1.0))
    op("vector", lambda e: e.memset(powtab[:, 1:2], 0.0))
    op("vector", lambda e: e.tensor_copy(powtab[:, 2:3], aq_cols[:, 0:1]))
    op("vector", lambda e: e.tensor_copy(powtab[:, 3:4], aq_cols[:, 1:2]))
    op("vector", lambda e: e.tensor_mul(sc64[:, 14:15], aq_cols[:, 0:1], aq_cols[:, 0:1]))
    op("vector", lambda e: e.tensor_mul(sc64[:, 15:16], aq_cols[:, 1:2], aq_cols[:, 1:2]))
    op("vector", lambda e: e.tensor_sub(powtab[:, 4:5], sc64[:, 14:15], sc64[:, 15:16]))
    op("vector", lambda e: e.tensor_mul(sc64[:, 14:15], aq_cols[:, 0:1], aq_cols[:, 1:2]))
    op("vector", lambda e: e.tensor_scalar(powtab[:, 5:6], sc64[:, 14:15], 2.0, None, ALU.mult))
    op("vector", lambda e: e.tensor_mul(sc64[:, 14:15], powtab[:, 4:5], aq_cols[:, 0:1]))
    op("vector", lambda e: e.tensor_mul(sc64[:, 15:16], powtab[:, 5:6], aq_cols[:, 1:2]))
    op("vector", lambda e: e.tensor_sub(powtab[:, 6:7], sc64[:, 14:15], sc64[:, 15:16]))
    op("vector", lambda e: e.tensor_mul(sc64[:, 14:15], powtab[:, 4:5], aq_cols[:, 1:2]))
    op("vector", lambda e: e.tensor_mul(sc64[:, 15:16], powtab[:, 5:6], aq_cols[:, 0:1]))
    op("vector", lambda e: e.tensor_add(powtab[:, 7:8], sc64[:, 14:15], sc64[:, 15:16]))

    op("vector", lambda e: e.tensor_scalar(sc64[:, 14:15], sc64[:, 4:5], -1.0, None, ALU.add))
    op("vector", lambda e: e.tensor_mul(sc64[:, 15:16], sc64[:, 2:3], sc64[:, 2:3]))
    op("vector", lambda e: e.tensor_mul(sc64[:, 10:11], sc64[:, 3:4], sc64[:, 3:4]))
    op("vector", lambda e: e.tensor_add(sc64[:, 15:16], sc64[:, 15:16], sc64[:, 10:11]))
    op("vector", lambda e: e.reciprocal(sc64[:, 15:16], sc64[:, 15:16]))
    op("vector", lambda e: e.tensor_mul(sc64[:, 10:11], sc64[:, 14:15], sc64[:, 2:3]))
    op("vector", lambda e: e.tensor_mul(sc64[:, 11:12], sc64[:, 5:6], sc64[:, 3:4]))
    op("vector", lambda e: e.tensor_sub(sc64[:, 11:12], sc64[:, 11:12], sc64[:, 10:11]))
    op("vector", lambda e: e.tensor_mul(sc64[:, 6:7], sc64[:, 11:12], sc64[:, 15:16]))
    op("vector", lambda e: e.tensor_mul(sc64[:, 10:11], sc64[:, 5:6], sc64[:, 2:3]))
    op("vector", lambda e: e.tensor_mul(sc64[:, 11:12], sc64[:, 14:15], sc64[:, 3:4]))
    op("vector", lambda e: e.tensor_add(sc64[:, 10:11], sc64[:, 10:11], sc64[:, 11:12]))
    op("vector", lambda e: e.tensor_mul(sc64[:, 10:11], sc64[:, 10:11], sc64[:, 15:16]))
    op("vector", lambda e: e.tensor_scalar(sc64[:, 7:8], sc64[:, 10:11], -1.0, None, ALU.mult),
       s_rv)

    cr_r = cr.rearrange("(t p) n -> t p n", p=P)
    ci_r = ci.rearrange("(t p) n -> t p n", p=P)
    for half, src_r in [(0, cr_r), (1, ci_r)]:
        for t8 in range(8):
            buf = t8 % 2
            if half or t8 >= 2:
                wait("sync", s_tp, s_tp.v)
            op("sync", lambda e, src_r=src_r, t8=t8, buf=buf: e.dma_start(
                out=x_t[0:P, buf * DM:buf * DM + NST], in_=src_r[t8]), s_x, 16)
            wait("tensor", s_x, s_x.v)
            if half or t8:
                wait("tensor", s_rv, s_rv.v)
            op("tensor", lambda e, buf=buf: e.transpose(
                psD[0:NST, 0:P], x_t[0:P, buf * DM:buf * DM + NST], identf[:]), s_tp)
            wait("vector", s_tp, s_tp.v)
            op("vector", lambda e, half=half, t8=t8: e.tensor_copy(
                wtmp[0:NST, half * DM + t8 * P:half * DM + (t8 + 1) * P],
                psD[0:NST, 0:P]), s_rv)
    op("vector", lambda e: e.tensor_scalar(ctmp[0:NST, 0:DM], wtmp[0:NST, 0:DM],
                                           sc64[:, 6:7], None, ALU.mult))
    op("vector", lambda e: e.tensor_scalar(ctmp[0:NST, DM:2 * DM], wtmp[0:NST, DM:2 * DM],
                                           sc64[:, 7:8], None, ALU.mult))
    op("vector", lambda e: e.tensor_sub(cs_re[:], ctmp[0:NST, 0:DM], ctmp[0:NST, DM:2 * DM]))
    op("vector", lambda e: e.tensor_scalar(ctmp[0:NST, 0:DM], wtmp[0:NST, 0:DM],
                                           sc64[:, 7:8], None, ALU.mult))
    op("vector", lambda e: e.tensor_scalar(ctmp[0:NST, DM:2 * DM], wtmp[0:NST, DM:2 * DM],
                                           sc64[:, 6:7], None, ALU.mult))
    op("vector", lambda e: e.tensor_add(cs_im[:], ctmp[0:NST, 0:DM], ctmp[0:NST, DM:2 * DM]),
       s_rv)
    wait("scalar", s_rv, s_rv.v)
    op("scalar", lambda e: e.copy(csreb[:], cs_re[:]))
    op("scalar", lambda e: e.copy(csimb[:], cs_im[:]))

    wait("scalar", s_rv, s_rv.v)
    op("scalar", lambda e: e.copy(pkre[:], pw_re[:]), s_ap)
    op("vector", lambda e: e.tensor_scalar(pkim[:], pw_im[:], -1.0, None, ALU.mult), s_rv)
    wait("scalar", s_rv, s_rv.v)
    wait("tensor", s_rv, s_rv.v)

    for m in range(4):
        if m > 0:
            wait("vector", s_ap, s_ap.v)
        op("vector", lambda e, m=m: e.tensor_scalar(ctmp[0:NST, 0:P], pw_re[:],
                                                    powtab[:, 2 * m:2 * m + 1], None, ALU.mult))
        op("vector", lambda e, m=m: e.tensor_scalar(ctmp[0:NST, P:2 * P], pw_im[:],
                                                    powtab[:, 2 * m + 1:2 * m + 2], None, ALU.mult))
        op("vector", lambda e: e.tensor_sub(ctmp[0:NST, 2 * P:3 * P], ctmp[0:NST, 0:P],
                                            ctmp[0:NST, P:2 * P]))
        op("vector", lambda e, m=m: e.tensor_scalar(ctmp[0:NST, 0:P], pw_re[:],
                                                    powtab[:, 2 * m + 1:2 * m + 2], None, ALU.mult))
        op("vector", lambda e, m=m: e.tensor_scalar(ctmp[0:NST, P:2 * P], pw_im[:],
                                                    powtab[:, 2 * m:2 * m + 1], None, ALU.mult))
        op("vector", lambda e: e.tensor_add(ctmp[0:NST, 3 * P:4 * P], ctmp[0:NST, 0:P],
                                            ctmp[0:NST, P:2 * P]), s_rv)
        wait("scalar", s_rv, s_rv.v)
        op("scalar", lambda e: e.copy(xn2b_t[0:NST, 0:P], ctmp[0:NST, 2 * P:3 * P]))
        op("scalar", lambda e: e.copy(xn2b_t[0:NST, P:2 * P], ctmp[0:NST, 3 * P:4 * P]), s_ap)
        wait("tensor", s_ap, s_ap.v)
        if m > 0:
            wait("tensor", s_t4, s_t4.v)
        op("tensor", lambda e: e.transpose(psA.bitcast(BF16)[0:P, 0:NST],
                                           xn2b_t[0:NST, 0:P], identb[0:NST, 0:NST]))
        op("tensor", lambda e: e.transpose(psB.bitcast(BF16)[0:P, 0:NST],
                                           xn2b_t[0:NST, P:2 * P], identb[0:NST, 0:NST]),
           s_tp)
        wait("scalar", s_tp, s_tp.v)
        op("scalar", lambda e: e.copy(xn4_t[0:P, 0:NST], psA.bitcast(BF16)[0:P, 0:NST]))
        op("scalar", lambda e: e.copy(xn4_t[0:P, NST:2 * NST],
                                      psB.bitcast(BF16)[0:P, 0:NST]), s_ap)
        wait("tensor", s_ap, s_ap.v)
        op("tensor", lambda e: e.matmul(psA[0:P, 0:NST], revb[:], xn4_t[0:P, 0:NST],
                                        start=True, stop=True))
        op("tensor", lambda e: e.matmul(psB[0:P, 0:NST], revb[:], xn4_t[0:P, NST:2 * NST],
                                        start=True, stop=True), s_rr)
        wait("scalar", s_rr, s_rr.v)
        slots = [0, 4] if m == 0 else [4 - m]
        for sl in slots:
            op("scalar", lambda e, sl=sl: e.copy(asre[:, sl * NST:(sl + 1) * NST],
                                                 psA[0:P, 0:NST]))
            op("scalar", lambda e, sl=sl: e.copy(asim[:, sl * NST:(sl + 1) * NST],
                                                 psB[0:P, 0:NST]),
               s_t4 if sl == slots[-1] else None)
    wait("vector", s_t4, s_t4.v)

    if DEBUG == 7:
        wait("gpsimd", s_t4, s_t4.v)
        op("gpsimd", lambda e: e.dma_start(out=dbg_r[0][0:P, 0:320], in_=asre[:]), s_wdma, 16)
        op("gpsimd", lambda e: e.dma_start(out=dbg_r[1][0:P, 0:320], in_=asim[:]), s_wdma, 16)
        wait("gpsimd", s_rv, s_rv.v)
        op("gpsimd", lambda e: e.dma_start(out=dbg_r[2][0:NST, 0:P], in_=ctmp[0:NST, 2 * P:3 * P]), s_wdma, 16)
        op("gpsimd", lambda e: e.dma_start(out=dbg_r[2][64:P, 0:P], in_=ctmp[0:NST, 3 * P:4 * P]), s_wdma, 16)
        op("gpsimd", lambda e: e.dma_start(out=dbg_r[3][0:NST, 0:8], in_=powtab[:]), s_wdma, 16)
    wait("vector", s_ap, s_ap.v)
    op("vector", lambda e: e.tensor_scalar(ctmp[0:NST, 0:P], pw_re[:], sc64[:, 4:5], None, ALU.mult))
    op("vector", lambda e: e.tensor_scalar(ctmp[0:NST, P:2 * P], pw_im[:], sc64[:, 5:6], None, ALU.mult))
    op("vector", lambda e: e.tensor_sub(ctmp[0:NST, 2 * P:3 * P], ctmp[0:NST, 0:P],
                                        ctmp[0:NST, P:2 * P]))
    op("vector", lambda e: e.tensor_scalar(ctmp[0:NST, 0:P], pw_re[:], sc64[:, 5:6], None, ALU.mult))
    op("vector", lambda e: e.tensor_scalar(ctmp[0:NST, P:2 * P], pw_im[:], sc64[:, 4:5], None, ALU.mult))
    op("vector", lambda e: e.tensor_add(ctmp[0:NST, 3 * P:4 * P], ctmp[0:NST, 0:P],
                                        ctmp[0:NST, P:2 * P]), s_rv)
    wait("scalar", s_rv, s_rv.v)
    op("scalar", lambda e: e.copy(ppre[:], ctmp[0:NST, 2 * P:3 * P]), s_ap)
    op("vector", lambda e: e.tensor_scalar(ppim[:], ctmp[0:NST, 3 * P:4 * P], -1.0,
                                           None, ALU.mult), s_rv)

    wait("tensor", s_ap, s_ap.v)
    for h in range(2):
        op("tensor", lambda e, h=h: e.matmul(psA[:, h * 512:(h + 1) * 512], pkre[:],
                                             csreb[:, h * 512:(h + 1) * 512],
                                             start=True, stop=False))
        op("tensor", lambda e, h=h: e.matmul(psA[:, h * 512:(h + 1) * 512], pkim[:],
                                             csimb[:, h * 512:(h + 1) * 512],
                                             start=False, stop=True),
           s_rr if h == 1 else None)
    wait("scalar", s_rr, s_rr.v)
    op("scalar", lambda e: e.copy(kb[:], psA[:]), s_ap)
    wait("tensor", s_ap, s_ap.v)
    for h in range(2):
        op("tensor", lambda e, h=h: e.matmul(psA[:, h * 512:(h + 1) * 512], kfrb_c[:],
                                             kb[:, h * 512:(h + 1) * 512],
                                             start=True, stop=True))
        op("tensor", lambda e, h=h: e.matmul(psB[:, h * 512:(h + 1) * 512], kfib_c[:],
                                             kb[:, h * 512:(h + 1) * 512],
                                             start=True, stop=True),
           s_rr if h == 1 else None)
    wait("vector", s_rr, s_rr.v)
    op("vector", lambda e: e.tensor_add(kfr[:], psA[:], gn_t[:, DM:2 * DM]))
    op("vector", lambda e: e.tensor_copy(kfi[:], psB[:]))
    op("vector", lambda e: e.memset(kfi[0:1, 0:DM], 0.0))
    op("vector", lambda e: e.tensor_copy(kfr2[:], kfr[:]))
    op("vector", lambda e: e.tensor_add(kfr2[0:1, 0:DM], psB[0:1, 0:DM], gn_t[0:1, DM:2 * DM]))
    op("vector", lambda e: e.memset(carry_re[:], 0.0))
    op("vector", lambda e: e.memset(carry_im[:], 0.0), s_cu)
    if DEBUG == 5:
        wait("sync", s_cu, s_cu.v)
        op("sync", lambda e: e.dma_start(out=dbg_r[0][0:NST, 0:P], in_=pw_re[:]), s_res, 16)
        op("sync", lambda e: e.dma_start(out=dbg_r[0][64:128, 0:P], in_=pw_im[:]), s_res, 16)
        op("sync", lambda e: e.dma_start(out=dbg_r[1][0:NST, 0:DM], in_=cs_re[:]), s_res, 16)
        op("sync", lambda e: e.dma_start(out=dbg_r[2][0:NST, 0:DM], in_=cs_im[:]), s_res, 16)
        op("sync", lambda e: e.dma_start(out=dbg_r[3][0:NST, 0:16], in_=sc64[:]), s_res, 16)
        op("sync", lambda e: e.dma_start(out=dbg_r[3][64:128, 0:4], in_=aq_cols[:]), s_res, 16)
    if DEBUG == 4:
        wait("sync", s_cu, s_cu.v)
        op("sync", lambda e: e.dma_start(out=dbg_r[0], in_=kfr[:]), s_res, 16)
        op("sync", lambda e: e.dma_start(out=dbg_r[1], in_=kfi[:]), s_res, 16)
        op("sync", lambda e: e.dma_start(out=dbg_r[2], in_=kfr2[:]), s_res, 16)
        wait("gpsimd", s_cu, s_cu.v)
        op("gpsimd", lambda e: e.dma_start(out=dbg_r[3], in_=kb[:]), s_wdma, 16)

    xo_r = xo.rearrange("(t p) d -> t p d", p=P)
    xp_r = xp.rearrange("(t p) d -> t p d", p=P)
    xres_r = xres.rearrange("(t p) d -> t p d", p=P)
    out_r = out.rearrange("(t p) d -> t p d", p=P)


    def tree_steps(eng, buf, w0, sem=None):
        w = w0
        while w > 1:
            h = w // 2
            op(eng, lambda e, h=h, w=w, buf=buf: e.tensor_tensor(
                buf[:, 0:h], buf[:, 0:h], buf[:, h:w], ALU.add),
               sem if h == 1 else None)
            w = h

    _lncnt = [0]

    def ln_stats(src_fn, mu_col, var_col, tree_buf, sq_buf, gp_tree=False):
        _lncnt[0] += 1
        """Emit mean/var(+eps) of src [128 x 1024] into cols columns.
        ACT squares; DVE does two tensor_reduce sums + the stats tail.
        Ends with var+eps in var_col (inc s_st)."""
        op("scalar", lambda e: e.activation(sq_buf[:], src_fn(), AF.Square), s_rs)
        op("vector", lambda e: e.tensor_reduce(stat[:, 0:1], src_fn(),
                                               axis=AXX, op=ALU.add))
        wait("vector", s_rs, s_rs.v)
        op("vector", lambda e: e.tensor_reduce(cols[:, 14:15], sq_buf[:],
                                               axis=AXX, op=ALU.add))
        op("vector", lambda e: e.tensor_scalar(mu_col, stat[:, 0:1], 1.0 / DM,
                                               None, ALU.mult))
        op("vector", lambda e: e.tensor_scalar(cols[:, 14:15], cols[:, 14:15],
                                               1.0 / DM, EPS, ALU.mult, ALU.add))
        op("vector", lambda e: e.tensor_mul(cols[:, 15:16], mu_col, mu_col))
        op("vector", lambda e: e.tensor_sub(var_col, cols[:, 14:15], cols[:, 15:16]),
           s_st)

    def ln_pipeline(src_slice, dst_slice):
        wait("vector", s_x, s_x.v)
        wait("scalar", s_x, s_x.v)
        ln_stats(lambda: src_slice, cols[:, 0:1], cols[:, 1:2], None,
                 sq_t[:, 0:DM], gp_tree=False)
        wait("scalar", s_st, s_st.v)
        op("scalar", lambda e: e.activation(cols[:, 2:3], cols[:, 1:2], AF.Sqrt), s_rs)
        wait("vector", s_rs, s_rs.v)
        op("vector", lambda e: e.reciprocal(cols[:, 3:4], cols[:, 2:3]))
        op("vector", lambda e: e.tensor_scalar(cols[:, 4:5], cols[:, 0:1], cols[:, 3:4], -1.0,
                                               ALU.mult, ALU.mult), s_rc)
        wait("scalar", s_rc, s_rc.v)
        op("scalar", lambda e: e.activation(xn1_t[:, 0:DM], src_slice, AF.Identity,
                                            bias=cols[:, 4:5], scale=cols[:, 3:4]), s_ap)
        wait("vector", s_ap, s_ap.v)
        ln_stats(lambda: xn1_t[:, 0:DM], cols[:, 5:6], cols[:, 6:7], None,
                 sq_t[:, 0:DM], gp_tree=False)
        wait("scalar", s_st, s_st.v)
        op("scalar", lambda e: e.activation(cols[:, 7:8], cols[:, 6:7], AF.Sqrt), s_rs)
        wait("vector", s_rs, s_rs.v)
        op("vector", lambda e: e.reciprocal(cols[:, 8:9], cols[:, 7:8]))
        op("vector", lambda e: e.tensor_scalar(cols[:, 9:10], cols[:, 5:6], cols[:, 8:9], -1.0,
                                               ALU.mult, ALU.mult), s_rc)
        wait("scalar", s_rc, s_rc.v)
        wait("scalar", s_ln, s_ln.v)
        op("scalar", lambda e: e.activation(xn2a_t[:, 0:DM], xn1_t[:, 0:DM], AF.Identity,
                                            bias=cols[:, 9:10], scale=cols[:, 8:9]), s_ap)
        wait("gpsimd", s_ap, s_ap.v)
        op("gpsimd", lambda e: e.tensor_mul(gn_t[:, 0:DM], xn2a_t[:, 0:DM], gammab[:]), s_wg)
        wait("vector", s_wg, s_wg.v)
        wait("vector", s_tp, s_tp.v)
        if s_xc.v >= 1:
            wait("vector", s_xc, s_xc.v - 1)
        op("vector", lambda e, d=dst_slice: e.tensor_tensor(d, gn_t[:, 0:DM], betab[:],
                                                            ALU.add), s_ln)

    def carry_update(mask_col, re_col, im_col):
        wait("vector", s_xc, s_xc.v)
        op("vector", lambda e, ic=im_col: e.tensor_scalar(ctmp[0:NST, 0:DM], carry_im[:],
                                                          ic, None, ALU.mult))
        op("vector", lambda e, rc=re_col: e.scalar_tensor_tensor(
            ctmp[0:NST, DM:2 * DM], carry_re[:], rc, ctmp[0:NST, 0:DM],
            ALU.mult, ALU.subtract))
        op("vector", lambda e, ic=im_col: e.tensor_scalar(ctmp[0:NST, 0:DM], carry_re[:],
                                                          ic, None, ALU.mult))
        op("vector", lambda e, rc=re_col: e.scalar_tensor_tensor(
            carry_im[:], carry_im[:], rc, ctmp[0:NST, 0:DM], ALU.mult, ALU.add))
        op("vector", lambda e, m=mask_col: e.scalar_tensor_tensor(
            carry_re[:], psA[0:NST, 0:DM], m, ctmp[0:NST, DM:2 * DM], ALU.mult, ALU.add))
        op("vector", lambda e, m=mask_col: e.scalar_tensor_tensor(
            carry_im[:], psB[0:NST, 0:DM], m, carry_im[:], ALU.mult, ALU.add), s_cu)

    wait("tensor", s_cu, s_cu.v)

    ap_ln1_pair = {}
    xc_after_pair = {}

    def ln_pair(base, pidx):
        src2 = x_t[:, base:base + 2 * DM]
        src2v = x_t[:, base:base + 2 * DM].rearrange("p (b d) -> p b d", d=DM)
        sq2 = sq_t[:, 0:2 * DM]
        sq2v = sq_t[:, 0:2 * DM].rearrange("p (b d) -> p b d", d=DM)
        wait("scalar", s_x, s_x.v)
        wait("vector", s_x, s_x.v)

        def stats2(sfull, sview):
            op("scalar", lambda e: e.activation(sq2, sfull, AF.Square), s_rs)
            op("vector", lambda e: e.tensor_reduce(stat[:, 0:2], sview, axis=AXX,
                                                   op=ALU.add, negate=True))
            wait("vector", s_rs, s_rs.v)
            op("vector", lambda e: e.tensor_reduce(stat[:, 2:4], sq2v, axis=AXX,
                                                   op=ALU.add))
            op("vector", lambda e: e.tensor_scalar(stat[:, 4:6], stat[:, 0:2],
                                                   1.0 / DM, None, ALU.mult))
            op("vector", lambda e: e.tensor_scalar(stat[:, 2:4], stat[:, 2:4],
                                                   1.0 / DM, EPS, ALU.mult, ALU.add))
            op("vector", lambda e: e.tensor_mul(stat[:, 6:8], stat[:, 4:6],
                                                stat[:, 4:6]))
            op("vector", lambda e: e.tensor_sub(stat[:, 8:10], stat[:, 2:4],
                                                stat[:, 6:8]), s_st)
            wait("scalar", s_st, s_st.v)
            op("scalar", lambda e: e.activation(stat[:, 10:12], stat[:, 8:10],
                                                AF.Sqrt), s_rs)
            wait("vector", s_rs, s_rs.v)
            op("vector", lambda e: e.reciprocal(stat[:, 12:14], stat[:, 10:12]))
            op("vector", lambda e: e.tensor_mul(stat[:, 14:16], stat[:, 4:6],
                                                stat[:, 12:14]), s_rc)
            wait("scalar", s_rc, s_rc.v)

        stats2(src2, src2v)
        for j in range(2):
            op("scalar", lambda e, j=j: e.activation(
                xn1_t[:, j * DM:(j + 1) * DM],
                x_t[:, base + j * DM:base + (j + 1) * DM],
                AF.Identity, bias=stat[:, 14 + j:15 + j],
                scale=stat[:, 12 + j:13 + j]), s_ap if j == 1 else None)
        ap_ln1_pair[pidx] = s_ap.v
        xn12 = xn1_t[:, 0:2 * DM]
        xn12v = xn1_t[:, 0:2 * DM].rearrange("p (b d) -> p b d", d=DM)
        wait("vector", s_ap, s_ap.v)
        stats2(xn12, xn12v)
        wait("scalar", s_ln, s_ln.v)
        for j in range(2):
            op("scalar", lambda e, j=j: e.activation(
                xn2a_t[:, j * DM:(j + 1) * DM], xn1_t[:, j * DM:(j + 1) * DM],
                AF.Identity, bias=stat[:, 14 + j:15 + j],
                scale=stat[:, 12 + j:13 + j]), s_ap if j == 1 else None)
        wait("gpsimd", s_ap, s_ap.v)
        for j in range(2):
            op("gpsimd", lambda e, j=j: e.tensor_mul(
                gn_t[:, j * DM:(j + 1) * DM], xn2a_t[:, j * DM:(j + 1) * DM],
                gammab[:]), s_wg if j == 1 else None)
        wait("vector", s_wg, s_wg.v)
        wait("vector", s_tp, s_tp.v)
        if pidx >= 2:
            wait("vector", s_xc, xc_after_pair[pidx - 2])
        for j in range(2):
            op("vector", lambda e, j=j: e.tensor_tensor(
                xn2b_t[:, base + j * DM:base + (j + 1) * DM],
                gn_t[:, j * DM:(j + 1) * DM], betab[:], ALU.add), s_ln)

    for g in range(PREG):
        for half in range(2):
            pidx = g * 2 + half
            base = (pidx % 2) * 2 * DM
            if pidx == 0:
                wait("sync", s_tp, s_tp.v)
            if pidx >= 2:
                wait("sync", s_ap, ap_ln1_pair[pidx - 2])
            for j in range(2):
                op("sync", lambda e, ch=pidx * 2 + j, base=base, j=j: e.dma_start(
                    out=x_t[:, base + j * DM:base + (j + 1) * DM],
                    in_=xp_r[ch]), s_x, 16)
            ln_pair(base, pidx)
            wait("tensor", s_ln, s_ln.v)
            if half == 0:
                wait("tensor", s_cu, s_cu.v)
            for j in range(2):
                k = half * 2 + j
                for h in range(2):
                    op("tensor", lambda e, k=k, h=h, base=base, j=j: e.matmul(
                        psA[0:NST, h * 512:(h + 1) * 512],
                        asre[:, (1 + k) * NST:(2 + k) * NST],
                        xn2b_t[:, base + j * DM + h * 512:
                               base + j * DM + (h + 1) * 512],
                        start=(k == 0), stop=(k == GSUB - 1)))
                    op("tensor", lambda e, k=k, h=h, base=base, j=j: e.matmul(
                        psB[0:NST, h * 512:(h + 1) * 512],
                        asim[:, (1 + k) * NST:(2 + k) * NST],
                        xn2b_t[:, base + j * DM + h * 512:
                               base + j * DM + (h + 1) * 512],
                        start=(k == 0), stop=(k == GSUB - 1)),
                       s_xc if h == 1 else None)
            xc_after_pair[pidx] = s_xc.v
        carry_update(pmask_sb[:, g:g + 1], aq_cols[:, 2:3], aq_cols[:, 3:4])

    l3_after = {}
    for i in range(NCH):
        buf = i % 2
        slot = i % 4
        if i % 2 == 0:
            p = i // 2
            pbase = (p % 2) * 2 * DM
            if p >= 2:
                wait("sync", s_l3, l3_after[i - 3])
            else:
                wait("sync", s_ap, ap_ln1_pair[22 + p])
            for j in range(2):
                op("sync", lambda e, ii=i + j, pbase=pbase, j=j: e.dma_start(
                    out=x_t[:, pbase + j * DM:pbase + (j + 1) * DM],
                    in_=xo_r[ii]), s_x, 16)
            ln_pair(pbase, 24 + p)
        if DEBUG == 1:
            op("gpsimd", lambda e, i=i, slot=slot: e.dma_start(
                out=dbg_r[i], in_=xn2b_t[:, slot * DM:(slot + 1) * DM]), s_wdma, 16)
        if DEBUG == 12 and i == 1:
            wait("sync", s_st, s_st.v)
            op("sync", lambda e: e.dma_start(out=dbg_r[0], in_=sq_t[:, 0:DM]), s_res, 16)
            op("sync", lambda e: e.dma_start(out=dbg_r[1][0:P, 0:512], in_=tr_t[:]), s_res, 16)
        if DEBUG == 8 and i < 8:
            wait("sync", s_ap, s_ap.v)
            op("sync", lambda e, i=i: e.dma_start(out=dbg_r[2 * i], in_=xn1_t[:, 0:DM]), s_res, 16)
            op("sync", lambda e, i=i: e.dma_start(out=dbg_r[2 * i + 1][0:P, 0:16],
                                                  in_=cols[:]), s_res, 16)
        if DEBUG == 9 and i < 8:
            wait("sync", s_x, s_x.v)
            op("sync", lambda e, i=i, slot=slot: e.dma_start(
                out=dbg_r[i], in_=x_t[:, slot * DM:(slot + 1) * DM]), s_res, 16)
        wait("gpsimd", s_cu, s_cu.v)
        op("gpsimd", lambda e: e.tensor_mul(wtmp[0:NST, 0:DM], cs_re[:], carry_re[:]))
        op("gpsimd", lambda e: e.tensor_mul(wtmp[0:NST, DM:2 * DM], cs_im[:], carry_im[:]))
        op("gpsimd", lambda e: e.tensor_sub(w_re[:], wtmp[0:NST, 0:DM],
                                            wtmp[0:NST, DM:2 * DM]), s_wg)
        op("vector", lambda e: e.tensor_mul(ctmp[0:NST, 0:DM], cs_re[:], carry_im[:]))
        op("vector", lambda e: e.tensor_mul(ctmp[0:NST, DM:2 * DM], cs_im[:], carry_re[:]))
        op("vector", lambda e: e.tensor_add(w_im[:], ctmp[0:NST, 0:DM],
                                            ctmp[0:NST, DM:2 * DM]), s_rv)
        wait("scalar", s_wg, s_wg.v)
        op("scalar", lambda e: e.copy(wreb[:], w_re[:]))
        wait("scalar", s_rv, s_rv.v)
        op("scalar", lambda e: e.copy(wimb[:], w_im[:]), s_wl)
        wait("tensor", s_ln, s_ln.v)
        wait("tensor", s_cu, s_cu.v)
        if i > 0:
            wait("tensor", s_pw, s_pw.v)
        for h in range(2):
            op("tensor", lambda e, h=h, slot=slot: e.matmul(
                psA[:, h * 512:(h + 1) * 512], frb[:],
                xn2b_t[:, slot * DM + h * 512:slot * DM + (h + 1) * 512],
                start=True, stop=True))
            op("tensor", lambda e, h=h, slot=slot: e.matmul(
                psB[:, h * 512:(h + 1) * 512], fib[:],
                xn2b_t[:, slot * DM + h * 512:slot * DM + (h + 1) * 512],
                start=True, stop=True),
               s_fft if h == 1 else None)
        wait("vector", s_fft, s_fft.v)
        op("vector", lambda e: e.tensor_mul(m_t[:, 0:DM], psA[:], kfr[:]))
        op("vector", lambda e: e.tensor_mul(m_t[:, DM:2 * DM], psB[:], kfi[:]))
        op("vector", lambda e: e.tensor_mul(m_t[:, 2 * DM:3 * DM], psA[:], kfi[:]))
        op("vector", lambda e: e.tensor_mul(m_t[:, 3 * DM:4 * DM], psB[:], kfr2[:]), s_pw)
        wait("gpsimd", s_pw, s_pw.v)
        op("gpsimd", lambda e: e.tensor_sub(m_t[:, 0:DM], m_t[:, 0:DM], m_t[:, DM:2 * DM]))
        op("gpsimd", lambda e: e.tensor_add(m_t[:, 2 * DM:3 * DM], m_t[:, 2 * DM:3 * DM],
                                            m_t[:, 3 * DM:4 * DM]), s_yy)
        wait("scalar", s_yy, s_yy.v)
        op("scalar", lambda e: e.copy(yr_t[:], m_t[:, 0:DM]))
        op("scalar", lambda e: e.copy(yi_t[:], m_t[:, 2 * DM:3 * DM]), s_ys)
        wait("tensor", s_ys, s_ys.v)
        wait("tensor", s_wl, s_wl.v)
        if i > 0:
            wait("tensor", s_l3, s_l3.v)
        for h in range(2):
            op("tensor", lambda e, h=h: e.matmul(psD[:, h * 512:(h + 1) * 512], icrb[:],
                                                 yr_t[:, h * 512:(h + 1) * 512],
                                                 start=True, stop=False))
            op("tensor", lambda e, h=h: e.matmul(psD[:, h * 512:(h + 1) * 512], icib[:],
                                                 yi_t[:, h * 512:(h + 1) * 512],
                                                 start=False, stop=False))
            op("tensor", lambda e, h=h: e.matmul(psD[:, h * 512:(h + 1) * 512], ppre[:],
                                                 wreb[:, h * 512:(h + 1) * 512],
                                                 start=False, stop=False))
            op("tensor", lambda e, h=h: e.matmul(psD[:, h * 512:(h + 1) * 512], ppim[:],
                                                 wimb[:, h * 512:(h + 1) * 512],
                                                 start=False, stop=True),
               s_cv if h == 1 else None)
        wait("tensor", s_pw, s_pw.v)
        for h in range(2):
            op("tensor", lambda e, h=h, slot=slot: e.matmul(
                psA[0:NST, h * 512:(h + 1) * 512], asre[:, 0:NST],
                xn2b_t[:, slot * DM + h * 512:slot * DM + (h + 1) * 512],
                start=True, stop=True))
            op("tensor", lambda e, h=h, slot=slot: e.matmul(
                psB[0:NST, h * 512:(h + 1) * 512], asim[:, 0:NST],
                xn2b_t[:, slot * DM + h * 512:slot * DM + (h + 1) * 512],
                start=True, stop=True),
               s_xc if h == 1 else None)
        if i % 2 == 1:
            xc_after_pair[24 + i // 2] = s_xc.v
        carry_update(ones_mask[:, 0:1], aq_cols[:, 0:1], aq_cols[:, 1:2])
        if DEBUG == 2:
            wait("vector", s_cv, s_cv.v)
            op("vector", lambda e: e.tensor_copy(gn_t[:, 0:DM], psD[:]), s_rv)
            wait("sync", s_rv, s_rv.v)
            op("sync", lambda e, i=i: e.dma_start(out=dbg_r[i], in_=gn_t[:, 0:DM]), s_res, 16)
            wait("vector", s_res, s_res.v)
        wait("vector", s_cv, s_cv.v)
        op("vector", lambda e: e.tensor_copy(gn_t[:, 0:DM], psD[:]), s_rv)
        wait("scalar", s_rv, s_rv.v)
        ln_stats(lambda: gn_t[:, 0:DM], cols[:, 10:11], cols[:, 11:12], None,
                 sq_t[:, 0:DM], gp_tree=False)
        wait("scalar", s_st, s_st.v)
        op("scalar", lambda e: e.activation(cols[:, 12:13], cols[:, 11:12], AF.Sqrt), s_rs)
        wait("vector", s_rs, s_rs.v)
        if i >= 2:
            wait("vector", s_res, s_res.v - 16)
        op("vector", lambda e: e.reciprocal(cols[:, 13:14], cols[:, 12:13]))
        op("vector", lambda e: e.tensor_scalar(xe_t[:], gn_t[:, 0:DM], cols[:, 10:11],
                                               cols[:, 13:14], ALU.subtract, ALU.mult))
        op("vector", lambda e, buf=buf, slot=slot: e.tensor_add(
            xnew_t[:, buf * DM:(buf + 1) * DM], xe_t[:],
            x_t[:, slot * DM:(slot + 1) * DM]), s_l3)
        l3_after[i] = s_l3.v
        wait("sync", s_l3, s_l3.v)
        op("sync", lambda e, i=i, buf=buf: e.dma_start(
            out=xres_r[i], in_=xnew_t[:, buf * DM:(buf + 1) * DM]), s_res, 16)
        wait("scalar", s_l3, s_l3.v)
        ln_stats(lambda buf=buf: xnew_t[:, buf * DM:(buf + 1) * DM],
                 cols[:, 10:11], cols[:, 11:12], None, sq_t[:, 0:DM], gp_tree=False)
        wait("scalar", s_st, s_st.v)
        op("scalar", lambda e: e.activation(cols[:, 12:13], cols[:, 11:12], AF.Sqrt), s_rs)
        wait("vector", s_rs, s_rs.v)
        op("vector", lambda e: e.reciprocal(cols[:, 13:14], cols[:, 12:13]))
        op("vector", lambda e: e.tensor_scalar(cols[:, 6:7], cols[:, 10:11],
                                               cols[:, 13:14], -1.0, ALU.mult, ALU.mult),
           s_rc)
        wait("scalar", s_rc, s_rc.v)
        if i > 0:
            wait("scalar", s_tp, s_tp.v)
        op("scalar", lambda e, buf=buf: e.activation(
            xn4_t[:], xnew_t[:, buf * DM:(buf + 1) * DM], AF.Identity,
            bias=cols[:, 6:7], scale=cols[:, 13:14]), s_l4)
        if DEBUG == 10:
            wait("gpsimd", s_l4, s_l4.v)
            op("gpsimd", lambda e, i=i: e.dma_start(out=dbg_r[i], in_=xn4_t[:]), s_wdma, 16)
        wait("tensor", s_l4, s_l4.v)
        if i > 0:
            wait("tensor", s_t4, s_t4.v)
        for dc in range(8):
            op("tensor", lambda e, dc=dc: e.transpose(
                psC.bitcast(BF16)[:, dc * P:(dc + 1) * P],
                xn4_t[:, dc * P:(dc + 1) * P], identb[:]),
               s_tp if dc == 7 else None)
        wait("scalar", s_tp, s_tp.v)
        for dc in range(8):
            op("scalar", lambda e, dc=dc, i=i: e.copy(
                xn4T[:, dc * TOK + i * P:dc * TOK + (i + 1) * P],
                psC.bitcast(BF16)[:, dc * P:(dc + 1) * P]),
               s_t4 if dc == 7 else None)

    if STOP == "A":
        wait("sync", s_res, s_res.v)
        op("sync", lambda e: e.dma_start(out=out, in_=xres), s_out, 16)
        wait("sync", s_out, s_out.v)
        with nc.Block() as blk:
            blk.sync(lambda e: [f(e) for f in prog["sync"]])
            blk.tensor(lambda e: [f(e) for f in prog["tensor"]])
            blk.vector(lambda e: [f(e) for f in prog["vector"]])
            blk.scalar(lambda e: [f(e) for f in prog["scalar"]])
            blk.gpsimd(lambda e: [f(e) for f in prog["gpsimd"]])
        return nc

    wr_r = wrout.rearrange("(dc p) e -> p dc e", p=P)
    op("sync", lambda e: e.dma_start(
        out=m_t[0:P, 0:NE * 8].rearrange("p (dc e) -> p dc e", e=NE), in_=wr_r),
       s_x, 16)
    wait("scalar", s_x, s_x.v)
    op("scalar", lambda e: e.copy(wr_sb[:], m_t[0:P, 0:NE * 8]), s_ap)
    op("vector", lambda e: e.memset(ctile[:], 0.0))
    wait("tensor", s_ap, s_ap.v)
    wait("tensor", s_l3, s_l3.v)
    wait("tensor", s_t4, s_t4.v)
    for ti in range(NCH):
        if ti > 0:
            wait("tensor", s_rc2, s_rc2.v)
        for dc in range(8):
            op("tensor", lambda e, ti=ti, dc=dc: e.matmul(
                psD[0:P, 0:NE],
                xn4T[:, dc * TOK + ti * P:dc * TOK + (ti + 1) * P],
                wr_sb[:, dc * NE:(dc + 1) * NE],
                start=(dc == 0), stop=(dc == 7)),
               s_rr if dc == 7 else None)
        wait("vector", s_rr, s_rr.v)
        op("vector", lambda e: e.tensor_copy(lgits[:], psD[0:P, 0:NE]))
        op("vector", lambda e: e.tensor_tensor(rt8[:, 0:4], lgits[:, 0:4],
                                               lgits[:, 4:8], ALU.max))
        op("vector", lambda e: e.tensor_tensor(rt8[:, 0:2], rt8[:, 0:2],
                                               rt8[:, 2:4], ALU.max))
        op("vector", lambda e: e.tensor_tensor(rt8[:, 0:1], rt8[:, 0:1],
                                               rt8[:, 1:2], ALU.max))
        op("vector", lambda e: e.tensor_scalar(cols[:, 1:2], rt8[:, 0:1], -1.0, None,
                                               ALU.mult), s_rv)
        wait("scalar", s_rv, s_rv.v)
        op("scalar", lambda e: e.activation(e8[:], lgits[:], AF.Exp, bias=cols[:, 1:2]),
           s_ap)
        wait("vector", s_ap, s_ap.v)
        op("vector", lambda e: e.tensor_tensor(rt8[:, 4:8], e8[:, 0:4], e8[:, 4:8],
                                               ALU.add))
        op("vector", lambda e: e.tensor_tensor(rt8[:, 4:6], rt8[:, 4:6], rt8[:, 6:8],
                                               ALU.add))
        op("vector", lambda e: e.tensor_tensor(rt8[:, 4:5], rt8[:, 4:5], rt8[:, 5:6],
                                               ALU.add))
        op("vector", lambda e: e.reciprocal(cols[:, 3:4], rt8[:, 4:5]))
        op("vector", lambda e: e.tensor_scalar(cols[:, 4:5], cols[:, 3:4], ALPHA, None,
                                               ALU.mult))
        op("vector", lambda e: e.memset(cols[:, 7:8], 0.0))
        for ex in range(7):
            op("vector", lambda e, ex=ex: e.tensor_scalar(
                cols[:, 6:7], lgits[:, ex:ex + 1], rt8[:, 0:1], None, ALU.is_ge))
            op("vector", lambda e: e.tensor_scalar(cols[:, 8:9], cols[:, 7:8], -1.0, 1.0,
                                                   ALU.mult, ALU.add))
            op("vector", lambda e: e.tensor_mul(cols[:, 6:7], cols[:, 6:7], cols[:, 8:9]))
            op("vector", lambda e, ex=ex: e.tensor_mul(ctile[:, ex:ex + 1],
                                                       cols[:, 6:7], cols[:, 4:5]))
            op("vector", lambda e: e.tensor_tensor(cols[:, 7:8], cols[:, 7:8],
                                                   cols[:, 6:7], ALU.max))
        op("vector", lambda e: e.memset(ctile[:, 7:8], 1.0))
        op("vector", lambda e: e.tensor_copy(ctileb[:], ctile[:]), s_rv)
        wait("tensor", s_rv, s_rv.v)
        op("tensor", lambda e: e.transpose(psD.bitcast(BF16)[0:NE, 0:P], ctileb[:],
                                           identb[:]), s_rt)
        wait("scalar", s_rt, s_rt.v)
        op("scalar", lambda e, ti=ti: e.copy(coef_rows[:, ti * P:(ti + 1) * P],
                                             psD.bitcast(BF16)[0:NE, 0:P]), s_rc2)

    if DEBUG == 11:
        wait("gpsimd", s_t4, s_t4.v)
        for j in range(16):
            op("gpsimd", lambda e, j=j: e.dma_start(
                out=dbg_r[j], in_=xn4T[:, j * DM:(j + 1) * DM]), s_wdma, 16)
    if DEBUG == 3:
        wait("sync", s_rc2, s_rc2.v)
        op("sync", lambda e: e.dma_start(out=dbg_r[0][0:NE, 0:DM],
                                         in_=coef_rows[:, 0:DM]), s_res, 16)
        op("sync", lambda e: e.dma_start(out=dbg_r[1][0:NE, 0:DM],
                                         in_=coef_rows[:, DM:2 * DM]), s_res, 16)
    wait("sync", s_rc2, s_rc2.v)
    op("sync", lambda e: e.dma_start(out=coefd, in_=coef_rows[:]), s_xr, 16)

    if STOP == "R":
        wait("sync", s_res, s_res.v)
        wait("sync", s_rc2, s_rc2.v)
        op("sync", lambda e: e.dma_start(out=out, in_=xres), s_out, 16)
        wait("sync", s_out, s_out.v)
        with nc.Block() as blk:
            blk.sync(lambda e: [f(e) for f in prog["sync"]])
            blk.tensor(lambda e: [f(e) for f in prog["tensor"]])
            blk.vector(lambda e: [f(e) for f in prog["vector"]])
            blk.scalar(lambda e: [f(e) for f in prog["scalar"]])
            blk.gpsimd(lambda e: [f(e) for f in prog["gpsimd"]])
        return nc

    for _g in reversed(_a_guards):
        _g.__exit__(None, None, None)
    wu_buf = S("wu_buf", [P, 3 * 2048], BF16)
    wg_buf = S("wg_buf", [P, 3 * 2048], BF16)
    wd_buf = S("wd_buf", [P, 3 * 2048], BF16)
    xs = S("xs", [P, 8 * TH], BF16)
    h_half = S("h_half", [P, 16 * TH], BF16)
    accT = S("accT", [P, 8 * TH], F32)
    coefb = S("coefb", [P, TH], BF16)
    sg_t = S("sg_t", [P, 2 * TC], F32)
    sq5_t = S("sq5_t", [P, DM], F32)
    pc_t = S("pc_t", [P, DM], F32)
    out_t = S("out_t", [P, 2 * DM], F32)
    xr_t = S("xr_t", [P, 2 * DM], F32)

    wup_r = wup.rearrange("e (dc p) h -> e p dc h", p=P)
    wgate_r = wgate.rearrange("e (dc p) h -> e p dc h", p=P)
    wdown_r = wdown.rearrange("e (hc p) d -> e p hc d", p=P)
    wu_v = wu_buf.rearrange("p (b dc h) -> p b dc h", b=3, h=256)
    wg_v = wg_buf.rearrange("p (b dc h) -> p b dc h", b=3, h=256)
    wd_v = wd_buf.rearrange("p (b hc dl) -> p b hc dl", b=3, dl=P)

    wait("gpsimd", s_rc2, s_rc2.v)
    last_ug_of_buf = {0: 0, 1: 0, 2: 0}
    last_dn_of_buf = {0: 0, 1: 0, 2: 0}

    def ffn_expert(th, e, first):
        wait("gpsimd", s_xs, s_xs.v)
        if first:
            wait("gpsimd", s_xr, s_xr.v)
        op("gpsimd", lambda e_, e=e, th=th: e_.dma_start(
            out=coefb[:],
            in_=coefd[e:e + 1, th * TH:(th + 1) * TH].broadcast_to([P, TH])),
           s_cb, 16)
        wait("vector", s_cb, s_cb.v)
        if not first:
            wait("vector", s_ug, s_ug.v)
        for dc in range(8):
            op("vector", lambda e_, dc=dc, th=th: e_.tensor_mul(
                xs[:, dc * TH:(dc + 1) * TH],
                xn4T[:, dc * TOK + th * TH:dc * TOK + (th + 1) * TH], coefb[:]),
               s_xs if dc == 7 else None)
        for hh in range(2):
            for hg in range(8):
                gidx = hh * 8 + hg
                b = gidx % 3
                if last_ug_of_buf[b]:
                    wait("gpsimd", s_ug, last_ug_of_buf[b])
                h0 = (hh * 16 + hg * 2) * P
                op("gpsimd", lambda e_, e=e, h0=h0, b=b: e_.dma_start(
                    out=wu_v[:, b], in_=wup_r[e][:, :, h0:h0 + 256]), s_wdma, 16)
                op("gpsimd", lambda e_, e=e, h0=h0, b=b: e_.dma_start(
                    out=wg_v[:, b], in_=wgate_r[e][:, :, h0:h0 + 256]), s_wdma, 16)
                wait("tensor", s_wdma, s_wdma.v)
                if hg == 0:
                    wait("tensor", s_ac, s_ac.v)
                    if hh == 0:
                        wait("tensor", s_xs, s_xs.v)
                        wait("tensor", s_e5, s_e5.v)
                for hl in range(2):
                    hc = hh * 16 + hg * 2 + hl
                    hcl = hg * 2 + hl
                    pu, pg = (psA, psB) if hc % 2 == 0 else (psC, psD)
                    if s_hm.v >= 2:
                        wait("tensor", s_hm, s_hm.v - 1)
                    if s_sg.v >= 2:
                        wait("tensor", s_sg, s_sg.v - 1)
                    for tcol in range(2):
                        for dc in range(8):
                            op("tensor", lambda e_, b=b, hl=hl, dc=dc, tcol=tcol,
                               pu=pu: e_.matmul(
                                pu[:, tcol * 512:(tcol + 1) * 512],
                                wu_v[:, b, dc, hl * P:(hl + 1) * P],
                                xs[:, dc * TH + tcol * 512:dc * TH + (tcol + 1) * 512],
                                start=(dc == 0), stop=(dc == 7)))
                        for dc in range(8):
                            op("tensor", lambda e_, b=b, hl=hl, dc=dc, tcol=tcol,
                               pg=pg, th=th: e_.matmul(
                                pg[:, tcol * 512:(tcol + 1) * 512],
                                wg_v[:, b, dc, hl * P:(hl + 1) * P],
                                xn4T[:, dc * TOK + th * TH + tcol * 512:
                                     dc * TOK + th * TH + (tcol + 1) * 512],
                                start=(dc == 0), stop=(dc == 7)),
                               s_ug if (dc == 7 and tcol == 1) else None)
                    wait("scalar", s_ug, s_ug.v)
                    for tcol in range(2):
                        op("scalar", lambda e_, pg=pg, tcol=tcol: e_.activation(
                            sg_t[:, tcol * TC:(tcol + 1) * TC],
                            pg[:, tcol * 512:(tcol + 1) * 512], AF.Silu),
                           s_sg if tcol == 1 else None)
                    wait("vector", s_sg, s_sg.v)
                    for tcol in range(2):
                        op("vector", lambda e_, pu=pu, hcl=hcl, tcol=tcol: e_.tensor_mul(
                            h_half[:, hcl * TH + tcol * 512:hcl * TH + (tcol + 1) * 512],
                            pu[:, tcol * 512:(tcol + 1) * 512],
                            sg_t[:, tcol * TC:(tcol + 1) * TC]),
                           s_hm if tcol == 1 else None)
                last_ug_of_buf[b] = s_ug.v
            for dc in range(8):
                b = dc % 3
                if last_dn_of_buf[b]:
                    wait("gpsimd", s_dn, last_dn_of_buf[b])
                op("gpsimd", lambda e_, e=e, hh=hh, dc=dc, b=b: e_.dma_start(
                    out=wd_v[:, b],
                    in_=wdown_r[e][:, hh * 16:(hh + 1) * 16, dc * P:(dc + 1) * P]),
                   s_wdma, 16)
                wait("tensor", s_wdma, s_wdma.v)
                pd = psA if dc % 2 == 0 else psB
                if dc == 0:
                    wait("tensor", s_hm, s_hm.v)
                    wait("tensor", s_sg, s_sg.v)
                elif s_ac.v >= 2:
                    wait("tensor", s_ac, s_ac.v - 1)
                for tcol in range(2):
                    for hcl in range(16):
                        op("tensor", lambda e_, b=b, hcl=hcl, tcol=tcol, pd=pd: e_.matmul(
                            pd[:, tcol * 512:(tcol + 1) * 512],
                            wd_v[:, b, hcl, 0:P],
                            h_half[:, hcl * TH + tcol * 512:hcl * TH + (tcol + 1) * 512],
                            start=(hcl == 0), stop=(hcl == 15)),
                           s_dn if (hcl == 15 and tcol == 1) else None)
                last_dn_of_buf[b] = s_dn.v
                wait("vector", s_dn, s_dn.v)
                if first and hh == 0 and dc == 0:
                    wait("vector", s_am, s_am.v)
                for tcol in range(2):
                    op("vector", lambda e_, dc=dc, tcol=tcol, pd=pd: e_.tensor_tensor(
                        accT[:, dc * TH + tcol * 512:dc * TH + (tcol + 1) * 512],
                        accT[:, dc * TH + tcol * 512:dc * TH + (tcol + 1) * 512],
                        pd[:, tcol * 512:(tcol + 1) * 512], ALU.add),
                       s_ac if tcol == 1 else None)

    def phase_c(th):
        for tt in range(TH // P):
            ti = th * (TH // P) + tt
            buf = tt % 2
            wait("sync", s_l3, s_l3.v)
            op("sync", lambda e_, ti=ti, buf=buf: e_.dma_start(
                out=xr_t[:, buf * DM:(buf + 1) * DM], in_=xres_r[ti]), s_xr, 16)
            pt = psC if tt % 2 == 0 else psD
            if tt == 0:
                wait("tensor", s_ac, s_ac.v)
                wait("tensor", s_sg, s_sg.v)
                wait("tensor", s_hm, s_hm.v)
            elif s_e5.v >= 2:
                wait("tensor", s_e5, s_e5.v - 1)
            for dc in range(8):
                op("tensor", lambda e_, dc=dc, tt=tt, pt=pt: e_.transpose(
                    pt[:, dc * P:(dc + 1) * P],
                    accT[:, dc * TH + tt * P:dc * TH + (tt + 1) * P], identf[:]),
                   s_c5 if dc == 7 else None)
            wait("vector", s_c5, s_c5.v)
            op("vector", lambda e_, pt=pt: e_.tensor_copy(pc_t[:], pt[:]), s_rv)
            wait("scalar", s_rv, s_rv.v)
            ln_stats(lambda: pc_t[:], cols[:, 10:11], cols[:, 11:12], None, sq5_t,
                     gp_tree=False)
            wait("scalar", s_st, s_st.v)
            op("scalar", lambda e_: e_.activation(cols[:, 12:13], cols[:, 11:12],
                                                  AF.Sqrt), s_rs)
            wait("vector", s_rs, s_rs.v)
            wait("vector", s_xr, s_xr.v)
            if s_out.v >= 32:
                wait("vector", s_out, s_out.v - 16)
            op("vector", lambda e_: e_.reciprocal(cols[:, 13:14], cols[:, 12:13]))
            op("vector", lambda e_: e_.tensor_scalar(
                xe_t[:], pc_t[:], cols[:, 10:11], cols[:, 13:14],
                ALU.subtract, ALU.mult), s_e5)
            op("vector", lambda e_, buf=buf: e_.tensor_add(
                out_t[:, buf * DM:(buf + 1) * DM], xe_t[:],
                xr_t[:, buf * DM:(buf + 1) * DM]), s_l3)
            wait("sync", s_l3, s_l3.v)
            op("sync", lambda e_, ti=ti, buf=buf: e_.dma_start(
                out=out_r[ti], in_=out_t[:, buf * DM:(buf + 1) * DM]), s_out, 16)

    if STOP == "B":
        wait("vector", s_c5, s_c5.v)
        op("vector", lambda e_: e_.memset(accT[:], 0.0), s_am)
        ffn_expert(0, 0, first=True)
        phase_c(0)
        wait("sync", s_res, s_res.v)
        op("sync", lambda e: e.dma_start(out=out.rearrange("(h t) d -> h t d", h=2)[1],
                                         in_=xres.rearrange("(h t) d -> h t d", h=2)[1]),
           s_out, 16)
    else:
        for th in range(NTH):
            wait("vector", s_c5, s_c5.v)
            op("vector", lambda e_: e_.memset(accT[:], 0.0), s_am)
            for ei in range(NE):
                ffn_expert(th, ei, first=(th == 0 and ei == 0))
            phase_c(th)

    wait("sync", s_out, s_out.v)
    wait("sync", s_res, s_res.v)

    with nc.Block() as blk:
        blk.sync(lambda e: [f(e) for f in prog["sync"]])
        blk.tensor(lambda e: [f(e) for f in prog["tensor"]])
        blk.vector(lambda e: [f(e) for f in prog["vector"]])
        blk.scalar(lambda e: [f(e) for f in prog["scalar"]])
        blk.gpsimd(lambda e: [f(e) for f in prog["gpsimd"]])
    return nc


def kernel(L0_attn_x, L0_attn_gamma, L0_attn_beta, L0_attn_LR, L0_attn_LI,
           L0_attn_CR, L0_attn_CI, L0_attn_D, L0_ffn_w_router, L0_ffn_w_up,
           L0_ffn_w_gate, L0_ffn_w_down, L0_ffn_ws_up, L0_ffn_ws_gate,
           L0_ffn_ws_down):
    x = np.ascontiguousarray(np.asarray(L0_attn_x, dtype=np.float32)).reshape(
        NB * LSEQ, DM)
    wupc = np.ascontiguousarray(np.concatenate(
        [np.asarray(L0_ffn_w_up, np.float32),
         np.asarray(L0_ffn_ws_up, np.float32)[None]], axis=0))
    wgatec = np.ascontiguousarray(np.concatenate(
        [np.asarray(L0_ffn_w_gate, np.float32),
         np.asarray(L0_ffn_ws_gate, np.float32)[None]], axis=0))
    wdownc = np.ascontiguousarray(np.concatenate(
        [np.asarray(L0_ffn_w_down, np.float32),
         np.asarray(L0_ffn_ws_down, np.float32)[None]], axis=0))

    nc = bass.Bass("TRN2", target_bir_lowering=False, detect_race_conditions=False)
    build(nc)

    in_maps = []
    for c in range(8):
        q = c % 4
        bstart = (c // 4) * LSEQ
        xov = x[bstart + q * TOK: bstart + (q + 1) * TOK]
        xpv = np.zeros((PRETOK, DM), np.float32)
        if q > 0:
            xpv[PRETOK - q * TOK:] = x[bstart: bstart + q * TOK]
        pm = np.zeros((NST, PREG), np.float32)
        ng = q * TOK // 512
        if ng > 0:
            pm[:, PREG - ng:] = 1.0
        in_maps.append({
            "xo": np.ascontiguousarray(xov),
            "xp": xpv,
            "pmask": pm,
            "lr": np.asarray(L0_attn_LR, np.float32).reshape(NST, 1),
            "li": np.asarray(L0_attn_LI, np.float32).reshape(NST, 1),
            "cr": np.ascontiguousarray(np.asarray(L0_attn_CR, np.float32)),
            "ci": np.ascontiguousarray(np.asarray(L0_attn_CI, np.float32)),
            "dvec": np.asarray(L0_attn_D, np.float32).reshape(1, DM),
            "gamma": np.asarray(L0_attn_gamma, np.float32).reshape(1, DM),
            "beta": np.asarray(L0_attn_beta, np.float32).reshape(1, DM),
            "wrout": np.ascontiguousarray(np.asarray(L0_ffn_w_router, np.float32)),
            "wup": wupc, "wgate": wgatec, "wdown": wdownc,
        })

    res = run_bass_kernel_spmd(nc, in_maps, core_ids=list(range(8)))
    full = np.concatenate([res.results[c]["out"] for c in range(8)], axis=0)
    return full.reshape(NB, LSEQ, DM)
